# version 12
# baseline (speedup 1.0000x reference)
import math
import contextlib
import numpy as np
import ml_dtypes
import concourse.bass as bass
import concourse.mybir as mybir
from concourse.bass_utils import run_bass_kernel_spmd

F32 = mybir.dt.float32
BF16 = mybir.dt.bfloat16
AF = mybir.ActivationFunctionType
ALU = mybir.AluOpType
AX = mybir.AxisListType
BF = ml_dtypes.bfloat16

B, L, D, H, HD, NE = 4, 4096, 2048, 16, 128, 4
G = 512
NG = L // G
NT = L // 128
OWN = L // 2
ONT = OWN // 128
SCALE = 1.0 / math.sqrt(HD)
M1, M2 = 24.0, 14.0
NEG = -1.0e9
EPS = 1e-5

EXT = [4 * (s + 2) if s % 2 == 0 else 4 * (s + 1) for s in range(NG)]
OWN_SLOTS = [1, 3, 5, 7]
OT2G = [(ot // 4) * 8 + 4 + (ot % 4) for ot in range(ONT)]

ENGS = ("te", "sc", "ve", "gp", "sy")
DBG_T16 = None
DBG_PROJ = None
DBG_OC = None
DBG_H1 = None
DBG_H2 = None
DBG_OT = None
DBG_SL = None
DBG_E = None


class Sem:
    def __init__(self, h):
        self.h = h
        self.n = 0


def build_program(taps=(), max_phase=99):
    nc = bass.Bass("TRN2", target_bir_lowering=False)
    es = contextlib.ExitStack()

    def din(name, shape, dt=F32):
        return nc.dram_tensor(name, shape, dt, kind="ExternalInput").ap()

    x0p = din("x0p", [L, D])
    dmask = din("dmask", [4, 128, 512])
    pmask = din("pmask", [NG, 128, 512])
    cost = din("cost", [128, L])
    sint = din("sint", [128, L])
    rot_t = din("rot_t", [128, 128], BF16)
    wq_t = din("wq_t", [16, 128, 16, 128])
    wk_t = din("wk_t", [16, 128, 16, 128])
    wv_t = din("wv_t", [16, 128, 16, 128])
    wo_t = din("wo_t", [D, D])
    bq_t = din("bq_t", [128, 16])
    bk_t = din("bk_t", [128, 16])
    bv_t = din("bv_t", [128, 16])
    bo_b = din("bo_b", [1, D])
    ln1w = din("ln1w", [1, D])
    ln1b = din("ln1b", [1, D])
    ln2w = din("ln2w", [1, D])
    ln2b = din("ln2b", [1, D])
    wg_t = din("wg_t", [128, 64])
    be1_t = din("be1_t", [128, 64])
    be2_t = din("be2_t", [NE, D])
    we1_t = din("we1_t", [NE, 16, 128, 16, 128])
    we2_t = din("we2_t", [NE, D, D])
    ones_r = din("ones_r", [1, 128])
    fconst = din("fconst", [128, 4])
    pmul_in = din("pmul_in", [128, NG])
    ident_in = din("ident_in", [128, 128], BF16)

    out = nc.dram_tensor("out", [OWN, D], F32, kind="ExternalOutput").ap()

    def dsc(name, shape, dt):
        return nc.dram_tensor(name, shape, dt, kind="Internal").ap()

    n1_d = dsc("n1_d", [L, D], BF16)
    a1_d = dsc("a1_d", [D, L], BF16)
    x1_d = dsc("x1_d", [OWN, D], F32)
    hsbo_d = dsc("hsbo_d", [L, D], BF16)
    ln1_d = dsc("ln1_d", [L, D], BF16)
    k2_d = dsc("k2_d", [D, L], BF16)
    v2_d = dsc("v2_d", [D, L], BF16)
    q2_d = dsc("q2_d", [D, OWN], BF16)
    h_d = dsc("h_d", [OWN, D], F32)
    ln2h_d = dsc("ln2h_d", [OWN, D], BF16)
    ln2lo_d = dsc("ln2lo_d", [OWN, D], BF16)
    m_d = dsc("m_d", [OWN, D], F32)

    tap_outs = {}
    _tap_src = {"n1": (n1_d, BF16), "a1": (a1_d, BF16), "x1": (x1_d, F32),
                "hsbo": (hsbo_d, BF16), "ln1": (ln1_d, BF16),
                "k2": (k2_d, BF16), "v2": (v2_d, BF16), "q2": (q2_d, BF16),
                "h": (h_d, F32), "ln2h": (ln2h_d, BF16), "m": (m_d, F32)}
    for t in taps:
        src, dt = _tap_src[t]
        tap_outs[t] = (nc.dram_tensor("tap_" + t, list(src.shape), dt,
                                      kind="ExternalOutput").ap(), src)

    segs = []

    def seg(name, dt, elems, p0, p1=None):
        ph = frozenset(p0) if isinstance(p0, (list, tuple, set)) else \
            frozenset(range(p0, (p1 if p1 is not None else p0) + 1))
        segs.append([name, dt, elems, ph, None])

    seg("N1T", BF16, 4 * D, 1, 1)
    seg("X0Q", F32, 4 * D, 1, 1)
    seg("K1F", BF16, 2 * L, 2, 2)
    seg("V1S", BF16, 2 * L, 2, 2)
    seg("PT", BF16, 2 * 512, (2, 6))
    seg("AVN", BF16, 2 * 512, (2, 6))
    seg("AO", BF16, 2 * 512, 2, 2)
    seg("A1T", BF16, 2 * D, (3, 4))
    seg("HSB", BF16, 2 * D, (3, 4))
    seg("LN1T", BF16, 2 * D, (3, 4))
    seg("COS", BF16, L, 0, 5)
    seg("SIN", BF16, L, 0, 5)
    seg("BOC", BF16, D, 0, 4)
    seg("LW1", BF16, D, 0, 4)
    seg("LB1", BF16, D, 0, 4)
    seg("LW2", BF16, D, 7, 7)
    seg("LB2", BF16, D, 7, 7)
    seg("L1F", BF16, 16 * OWN, (4, 5))
    seg("WB", BF16, 3 * D, (4, 5))
    seg("KRO", BF16, 2 * 512, (4, 5))
    seg("KB", BF16, 2 * 512, (4, 5))
    seg("ROT", BF16, 128, 0, 5)
    seg("K2F", BF16, 2 * L, 6, 6)
    seg("V2S", BF16, 2 * L, 6, 6)
    seg("Q2F", BF16, 2 * OWN, 6, 6)
    seg("V2F", BF16, 2 * L, 6, 6)
    seg("A2P", BF16, 16 * OWN, 6, 7)
    seg("WO", BF16, 16 * D, 7, 7)
    seg("HSBT", BF16, 2 * D, 7, 7)
    seg("LN2T", BF16, 2 * D, 7, 7)
    seg("L2F", BF16, 2 * 16 * 512, 8, 9)
    seg("WG", BF16, 64, 0, 8)
    seg("WGLO", BF16, 64, 0, 8)
    seg("L2FLO", BF16, 16 * 512, 8, 8)
    seg("LN2LO", BF16, 2 * D, 7, 7)
    seg("CMB", BF16, 4, 8, 8)
    seg("CROW", BF16, NE * OWN, 8, 9)
    seg("WE2", BF16, 16 * D, 9, 9)
    seg("HMW", BF16, 16 * 512, 9, 9)
    seg("W1B", BF16, 2 * D, 9, 9)
    seg("GT", BF16, 512, 9, 9)
    seg("CB", BF16, 512, 9, 9)
    seg("BE2", BF16, D, 9, 9)
    seg("ONEB", BF16, 128, 0, 10)
    seg("IDN", BF16, 128, 0, 10)
    seg("OCB", BF16, 1, 0, 10)

    seg("DM1", F32, 4 * 512, 0, 2)
    seg("PM1", F32, NG * 512, 0, 2)
    seg("DM2", F32, 4 * 512, 6, 6)
    seg("PM2", F32, NG * 512, 6, 6)
    seg("PMUL", F32, NG, 0, 6)
    seg("X0T", F32, 2 * D, (1, 3, 4))
    seg("ST", F32, 64, 0, 10)
    seg("RZ", F32, 2048, (2, 6))
    seg("ZB", F32, 2 * 512, (2, 6))
    seg("X1T", F32, 2 * D, (3, 4))
    seg("HST", F32, D, (3, 4))
    seg("TMP1", F32, D, (3, 4, 7))
    seg("TMP2", F32, D, (3, 4, 7))
    seg("RT1", F32, 512, (4, 5))
    seg("RT2", F32, 512, (4, 5))
    seg("HT", F32, 2 * D, 7, 7)
    seg("GE4", F32, 8, 8, 8)
    seg("GRW", F32, 16, 8, 8)
    seg("MST", F32, 2 * 512, 9, 9)
    seg("FIN", F32, 3 * 2 * D, 10, 10)
    seg("OUTT", F32, 2 * D, 10, 10)
    seg("ONER", F32, 128, 0, 10)
    seg("FCONST", F32, 4, 0, 10)
    seg("BQ", F32, 16, 0, 5)
    seg("BK", F32, 16, 0, 5)
    seg("BV", F32, 16, 0, 5)
    seg("BE1", F32, 64, 0, 9)
    seg("WGF", F32, 64, 0, 8)

    placed = []
    hi = [0]
    for s in sorted(segs, key=lambda q: -(q[2] * (2 if q[1] is BF16 else 4))):
        esize = 2 if s[1] is BF16 else 4
        size = (s[2] * esize + 3) & ~3
        cand = 0
        moved = True
        while moved:
            moved = False
            for q in placed:
                if s[3] & q[3]:
                    if not (cand + size <= q[4] or q[4] + q[5] <= cand):
                        cand = (q[4] + q[5] + 3) & ~3
                        moved = True
        s[4] = cand
        placed.append([s[0], s[1], s[2], s[3], cand, size])
        hi[0] = max(hi[0], cand + size)
    total = hi[0]
    assert total <= 206000, f"SBUF over budget: {total} bytes/partition"

    ab = es.enter_context(nc.sbuf_tensor("AB", [128, (total + 1) // 2], BF16))
    SEG = {}
    for name, dt, elems, ph, boff in segs:
        SEG[name] = (dt, boff, elems)

    def V(name, a=0, n=None, rows=(0, 128)):
        dt, boff, elems = SEG[name]
        if n is None:
            n = elems - a
        assert a + n <= elems, (name, a, n, elems)
        if dt is BF16:
            o = boff // 2 + a
            return ab[rows[0]:rows[1], o:o + n]
        o = boff // 2 + 2 * a
        return ab[rows[0]:rows[1], o:o + 2 * n].bitcast(F32)

    pbk = [es.enter_context(nc.psum_tensor(f"PB{i}", [128, 512], F32))
           for i in range(7)]
    pt_bf = es.enter_context(nc.psum_tensor("PTB", [128, 1024], BF16))

    sems = {}
    for name in ("phb", "mm", "ex", "vv",
                 "dG0", "dG1", "dS", "dK0", "dK1", "dX0", "dX1",
                 "dA0", "dA1", "dN0", "dN1", "dLn0", "dLn1", "dSt0", "dSt1",
                 "dAo0", "dAo1", "dKr0", "dKr1", "dM0", "dM1", "dWE2",
                 "dWO", "dO0", "dO1", "dV0", "dV1", "dG2", "gpv",
                 "dX2", "dX3"):
        sems[name] = Sem(es.enter_context(nc.semaphore(name)))

    plan = {k: [] for k in ENGS}

    def A(ek, f):
        plan[ek].append(f)

    def W(ek, sem, val=None):
        v = sem.n if val is None else val
        if v is None or v <= 0:
            return
        plan[ek].append(lambda e, s=sem.h, v=v: e.wait_ge(s, v))

    _self_sem = {}

    def I(ek, sem, k, f, chain=True):
        if ek in ("sc", "ve") and chain:
            prev = _self_sem.get(ek)
            if prev is not None and prev[1] > 0:
                plan[ek].append(
                    lambda e, s=prev[0], v=prev[1]: e.wait_ge(s, v))
        sem.n += k
        plan[ek].append(lambda e, s=sem.h, k=k, f=f: f(e).then_inc(s, k))
        _self_sem[ek] = (sem.h, sem.n)

    phase_no = [0]
    cuts = {}

    SY_SEMS = ("dS", "dK0", "dK1", "dX0", "dX1", "dA0", "dA1", "dN0", "dN1",
               "dLn0", "dLn1", "dSt0", "dSt1", "dAo0", "dAo1", "dKr0", "dKr1",
               "dO0", "dO1")
    SC_SEMS = ("dV0", "dV1")
    GP_SEMS = ("dG0", "dG1", "dG2", "dM0", "dM1", "dWE2", "dWO")

    def barrier():
        for nm in SY_SEMS:
            s = sems[nm]
            if s.n > 0:
                plan["sy"].append(lambda e, h=s.h, v=s.n: e.wait_ge(h, v))
        for nm in GP_SEMS:
            s = sems[nm]
            if s.n > 0:
                plan["gp"].append(lambda e, h=s.h, v=s.n: e.wait_ge(h, v))
        for nm in SC_SEMS:
            s = sems[nm]
            if s.n > 0:
                plan["sc"].append(lambda e, h=s.h, v=s.n: e.wait_ge(h, v))
        for ek, s in (("te", mm), ("sc", ex), ("ve", vv)):
            if s.n > 0:
                plan[ek].append(lambda e, h=s.h, v=s.n: e.wait_ge(h, v))
        phase_no[0] += 1
        tgt = 5 * phase_no[0]
        for ek in ENGS:
            plan[ek].append(lambda e, s=sems["phb"].h: e.sem_inc(s, 1))
            plan[ek].append(lambda e, s=sems["phb"].h, t=tgt: e.wait_ge(s, t))
        cuts[phase_no[0]] = {ek: len(plan[ek]) for ek in ENGS}

    mm, ex, vv = sems["mm"], sems["ex"], sems["vv"]
    dS = sems["dS"]
    dLn = (sems["dLn0"], sems["dLn1"])
    dO = (sems["dO0"], sems["dO1"])
    dM = (sems["dM0"], sems["dM1"])
    dG0, dG1 = sems["dG0"], sems["dG1"]
    dX = (sems["dX0"], sems["dX1"])
    dA = (sems["dA0"], sems["dA1"])
    dN = (sems["dN0"], sems["dN1"])
    dK = (sems["dK0"], sems["dK1"])
    dSt = (sems["dSt0"], sems["dSt1"])
    dAo = (sems["dAo0"], sems["dAo1"])
    dKr = (sems["dKr0"], sems["dKr1"])
    dGp = (dG0, dG1, sems["dG2"])

    I("sy", dS, 16, lambda e: e.dma_start(
        out=V("DM1").rearrange("p (c q) -> p c q", c=4), in_=dmask.transpose([1, 0, 2])))
    I("sy", dS, 16, lambda e: e.dma_start(
        out=V("PM1").rearrange("p (c q) -> p c q", c=NG), in_=pmask.transpose([1, 0, 2])))
    I("sy", dS, 16, lambda e: e.dma_start(out=V("PMUL"), in_=pmul_in[:, :]))
    I("sy", dS, 16, lambda e: e.dma_start(out=V("ONER", rows=(0, 1)), in_=ones_r[:, :]))
    I("sy", dS, 16, lambda e: e.dma_start(out=V("BQ"), in_=bq_t[:, :]))
    I("sy", dS, 16, lambda e: e.dma_start(out=V("BK"), in_=bk_t[:, :]))
    I("sy", dS, 16, lambda e: e.dma_start(out=V("BV"), in_=bv_t[:, :]))
    I("sy", dS, 16, lambda e: e.dma_start(out=V("BE1"), in_=be1_t[:, :]))
    I("sy", dS, 16, lambda e: e.dma_start(out=V("IDN"), in_=ident_in[:, :]))
    I("sy", dS, 16, lambda e: e.dma_start(out=V("WGF"), in_=wg_t[:, :]))
    I("sy", dS, 16, lambda e: e.dma_start(out=V("FCONST"), in_=fconst[:, :]))
    consts_sy = dS.n
    nc.const_aps.register(F32, float(EPS), V("FCONST", a=0, n=1))
    nc.const_aps.register(F32, float(-M1 * SCALE), V("FCONST", a=1, n=1))
    nc.const_aps.register(F32, float(-M2 * SCALE), V("FCONST", a=2, n=1))
    I("gp", dG0, 16, lambda e: e.dma_start(out=V("COS"), in_=cost[:, :]))
    I("gp", dG0, 16, lambda e: e.dma_start(out=V("SIN"), in_=sint[:, :]))
    I("gp", dG0, 16, lambda e: e.dma_start(out=V("ROT"), in_=rot_t[:, :]))
    I("gp", dG0, 16, lambda e: e.dma_start(
        out=V("ONEB", rows=(0, 1), n=128), in_=ones_r[:, :]))
    I("gp", dG0, 16, lambda e: e.dma_start(
        out=V("OCB"), in_=ones_r.rearrange("o p -> p o")))
    I("gp", dG0, 16, lambda e: e.dma_start(out=V("BOC", rows=(0, 1)), in_=bo_b[:, :]))
    I("gp", dG0, 16, lambda e: e.dma_start(out=V("LW1", rows=(0, 1)), in_=ln1w[:, :]))
    I("gp", dG0, 16, lambda e: e.dma_start(out=V("LB1", rows=(0, 1)), in_=ln1b[:, :]))
    I("gp", dG0, 16, lambda e: e.dma_start(out=V("WG"), in_=wg_t[:, :]))
    consts_gp = dG0.n
    W("te", dS, consts_sy)
    W("te", dG0, consts_gp)
    W("ve", dS, consts_sy)
    W("ve", dG0, consts_gp)
    I("ve", vv, 1, lambda e: e.tensor_tensor(
        V("WGLO"), V("WGF"), V("WG"), ALU.subtract))
    for nm in ("BOC", "LW1", "LB1"):
        for h2 in range(4):
            I("te", mm, 1, lambda e, h2=h2, nm=nm: e.matmul(
                pbk[0][:, :], V("ONEB", rows=(0, 1), n=128),
                V(nm, a=h2 * 512, n=512, rows=(0, 1)), start=True, stop=True))
            W("sc", mm)
            I("sc", ex, 1, lambda e, h2=h2, nm=nm: e.copy(
                V(nm, a=h2 * 512, n=512), pbk[0][:, :]))
            W("te", ex)
    barrier()

    def stats_chain(src_fn, par=0):
        def st(a, n):
            return V("ST", a=32 * par + a, n=n)
        for i in range(4):
            I("ve", vv, 1, lambda e, i=i: e.bn_stats(
                st(6 * i, 6), src_fn()[:, 512 * i:512 * (i + 1)]))
        I("ve", vv, 1, lambda e: e.bn_aggr(st(24, 2), st(0, 24)))
        W("sc", vv)
        I("sc", ex, 1, lambda e: e.activation(
            st(26, 1), st(25, 1), AF.Sqrt, bias=EPS, scale=1.0))
        W("ve", ex)
        I("ve", vv, 1, lambda e: e.reciprocal(st(27, 1), st(26, 1)))
        I("ve", vv, 1, lambda e: e.scalar_tensor_tensor(
            st(28, 1), st(24, 1), -1.0, st(27, 1), ALU.mult, ALU.mult))
        return (lambda: st(27, 1)), (lambda: st(28, 1))

    def rope_emit(kb_ap, cos_a, out_ap):
        W("te", vv)
        W("te", ex)
        I("te", mm, 1, lambda e, kb=kb_ap: e.matmul(
            pbk[4][:, :], V("ROT"), kb, start=True, stop=True))
        rq_mm = mm.n
        cos = V("COS", a=cos_a, n=512)
        sin = V("SIN", a=cos_a, n=512)
        t1 = V("RT1")
        t2 = V("RT2")
        I("ve", vv, 1, lambda e, kb=kb_ap: e.tensor_tensor(t1, kb, cos, ALU.mult))
        W("ve", mm, rq_mm)
        I("ve", vv, 1, lambda e: e.tensor_tensor(t2, pbk[4][:, :], sin, ALU.mult))
        I("ve", vv, 1, lambda e, o=out_ap: e.tensor_tensor(o, t1, t2, ALU.add))

    attn_state = {"blk": 0, "stt": [None, None], "zbc": None, "rz": 0,
                  "zln": [None, None]}

    def emit_attn(s, ext, kf_ap, qf_ap, vs_ap, bias_imm, out_sb, dm="DM1",
                  pm="PM1"):
        par = attn_state["blk"] % 2
        attn_state["blk"] += 1
        AVB = pbk[2] if par == 0 else pbk[4]
        ZB_ = pbk[3] if par == 0 else pbk[6]
        M0 = ext - 8
        W("te", vv, attn_state["stt"][par])
        W("te", ex, attn_state["zln"][par])
        exp_base = ex.n
        pt_ready = {}

        def emit_av(c):
            sb = c % 2
            kind, cnt = pt_ready[c]
            W("te", ex if kind == "ex" else vv, cnt)
            I("te", mm, 1, lambda e, c=c, sb=sb, last=(c == ext - 1): e.matmul(
                AVB[:, :], vs_ap[:, c * 128:(c + 1) * 128],
                V("PT", a=sb * 512, n=512), start=(c == 0), stop=last))
            I("te", mm, 1, lambda e, c=c, sb=sb, last=(c == ext - 1): e.matmul(
                ZB_[0:1, :], V("OCB"), V("PT", a=sb * 512, n=512),
                start=(c == 0), stop=last))

        for c in range(ext + 2):
            if c >= 2:
                emit_av(c - 2)
            if c >= ext:
                continue
            sb = c % 2
            if c >= 2:
                W("te", ex, exp_base + (c - 1))
            I("te", mm, 1, lambda e, c=c, sb=sb: e.matmul(
                pbk[sb][:, :], kf_ap[:, c * 128:(c + 1) * 128], qf_ap,
                start=True, stop=True))
            mm_after_s = mm.n
            ci = c - M0
            diag = ci >= 0 and ((s % 2 == 0) == (ci < 4))
            partner = ci >= 0 and not diag
            if diag:
                di = ci if ci < 4 else ci - 4
                W("ve", mm, mm_after_s)
                I("ve", vv, 1, lambda e, sb=sb, di=di: e.tensor_tensor(
                    pbk[sb][:, :], pbk[sb][:, :], V(dm, a=di * 512, n=512),
                    ALU.add))
                W("sc", vv)
            else:
                W("sc", mm, mm_after_s)
            I("sc", ex, 1, lambda e, sb=sb: e.activation(
                V("PT", a=sb * 512, n=512), pbk[sb][:, :], AF.Exp,
                bias=bias_imm, scale=SCALE), chain=False)
            if partner:
                W("ve", ex)
                I("ve", vv, 1, lambda e, sb=sb, s=s: e.tensor_scalar_mul(
                    V("PT", a=sb * 512, n=512), V("PT", a=sb * 512, n=512),
                    V("PMUL", a=s, n=1)))
                pt_ready[c] = ("ve", vv.n)
            else:
                pt_ready[c] = ("ex", ex.n)
        z_done = mm.n

        rzpar = attn_state["rz"] % 2
        attn_state["rz"] += 1
        zt = V("RZ", a=rzpar * 1024, n=512, rows=(0, 1))
        rz = V("RZ", a=rzpar * 1024 + 512, n=512, rows=(0, 1))
        W("sc", mm, z_done)
        I("sc", ex, 1, lambda e, zt=zt, Z=ZB_: e.activation(
            zt, Z[0:1, :], AF.Ln, bias=0.0, scale=1.0), chain=False)
        I("sc", ex, 1, lambda e, zt=zt, rz=rz: e.activation(
            rz, zt, AF.Exp, bias=0.0, scale=-1.0))
        attn_state["zln"][par] = ex.n
        recip_done = ex.n
        avn = V("AVN", a=par * 512, n=512)
        W("ve", mm, z_done)
        I("ve", vv, 1, lambda e, avn=avn, A=AVB: e.tensor_copy(avn, A[:, :]),
          chain=False)
        attn_state["stt"][par] = vv.n

        def epilogue():
            W("te", ex, attn_state["zbc"])
            W("te", ex, recip_done)
            I("te", mm, 1, lambda e, rz=rz: e.matmul(
                pbk[5][:, :], V("ONER", rows=(0, 1), n=128), rz,
                start=True, stop=True))
            W("sc", mm)
            zb = V("ZB", a=(par % 2) * 512, n=512)
            I("sc", ex, 1, lambda e, zb=zb: e.copy(zb, pbk[5][:, :]))
            attn_state["zbc"] = ex.n
            W("ve", ex)
            I("ve", vv, 1, lambda e, zb=zb, o=out_sb, avn=avn:
                e.scalar_tensor_tensor(o, avn, 1.0, zb, ALU.mult, ALU.mult))
        return epilogue

    x0_loads = [0, 0]
    x0q_loads = [0, 0, 0, 0]
    n1q_free = [None, None, None, None]
    n1t_store = [None, None, None, None]
    dXq = (dX[0], dX[1], sems["dX2"], sems["dX3"])
    dNq = (dN[0], dN[1], sems["dAo0"], sems["dAo1"])
    for tt in range(NT):
        pp = tt % 4
        W("sy", ex, n1q_free[pp])
        I("sy", dXq[pp], 16, lambda e, tt=tt, pp=pp: e.dma_start(
            out=V("X0Q", a=pp * D, n=D), in_=x0p[tt * 128:(tt + 1) * 128, :]))
        x0q_loads[pp] += 1
        W("ve", dXq[pp], 16 * x0q_loads[pp])
        rstd, negmur = stats_chain(lambda pp=pp: V("X0Q", a=pp * D, n=D),
                                   par=pp % 2)
        W("sc", vv)
        W("sc", dNq[pp], n1t_store[pp])
        I("sc", ex, 1, lambda e, pp=pp, r=rstd, nm=negmur: e.activation(
            V("N1T", a=pp * D, n=D), V("X0Q", a=pp * D, n=D), AF.Identity,
            bias=nm(), scale=r()))
        n1q_free[pp] = ex.n
        W("sy", ex, ex.n)
        I("sy", dNq[pp], 16, lambda e, tt=tt, pp=pp: e.dma_start(
            out=n1_d[tt * 128:(tt + 1) * 128, :], in_=V("N1T", a=pp * D, n=D)))
        n1t_store[pp] = dNq[pp].n
    barrier()

    kv_loads = [0, 0]
    head_free = [None, None]
    ao_store = [None, None]
    pend_epi = [None]
    pend_store = [lambda: None]
    blk = 0
    for h in range(DBG_H1 if DBG_H1 else H):
        pp = h % 2
        W("sy", mm, head_free[pp])
        I("sy", dK[pp], 16, lambda e, h=h, pp=pp: e.dma_start(
            out=V("K1F", a=pp * L, n=L), in_=n1_d[:, h * 128:(h + 1) * 128],
            transpose=True))
        I("sy", dK[pp], 16, lambda e, h=h, pp=pp: e.dma_start(
            out=V("V1S", a=pp * L, n=L).rearrange("p (kc i) -> p kc i", kc=32),
            in_=n1_d[:, h * 128:(h + 1) * 128].rearrange("(kc p) i -> kc p i", p=128).transpose([1, 0, 2])))
        kv_loads[pp] += 2
        W("te", dK[pp], 16 * kv_loads[pp])
        for s in range(NG):
            ob = blk % 2
            W("ve", dAo[ob], ao_store[ob])
            epi = emit_attn(s, EXT[s], V("K1F", a=pp * L, n=L),
                            V("K1F", a=pp * L + s * 512, n=512),
                            V("V1S", a=pp * L, n=L), -M1 * SCALE,
                            V("AO", a=ob * 512, n=512))
            if pend_epi[0] is not None:
                pend_epi[0]()
                pend_store[0]()
            pend_epi[0] = epi

            def mk_store(h=h, s=s, ob=ob):
                def st():
                    W("sy", vv)
                    I("sy", dAo[ob], 16, lambda e: e.dma_start(
                        out=a1_d[h * 128:(h + 1) * 128, s * 512:(s + 1) * 512],
                        in_=V("AO", a=ob * 512, n=512)))
                    ao_store[ob] = dAo[ob].n
                return st
            pend_store[0] = mk_store()
            blk += 1
        head_free[pp] = mm.n
    pend_epi[0]()
    pend_store[0]()
    pend_epi[0] = None
    barrier()

    a1_loads = [0, 0]
    x0t_free = [None, None]
    x1t_free = [None, None]
    hsb_store = [None, None]
    ln1_store = [None, None]
    x1_stores = [None, None]
    kro_store = [None, None]
    kt_free = [None, None]
    wband_free = [None, None, None]
    wband_mm = [None, None, None]
    evac_ex = [None, None, None, None]

    def emit_tile(tt):
        pp = tt % 2
        W("sy", vv, x0t_free[pp])
        I("sy", dX[pp], 16, lambda e, tt=tt, pp=pp: e.dma_start(
            out=V("X0T", a=pp * D, n=D), in_=x0p[tt * 128:(tt + 1) * 128, :]))
        x0_loads[pp] += 1
        I("sy", dA[pp], 16, lambda e, tt=tt, pp=pp: e.dma_start(
            out=V("A1T", a=pp * D, n=D), in_=a1_d[:, tt * 128:(tt + 1) * 128],
            transpose=True))
        a1_loads[pp] += 1
        W("ve", dX[pp], dX[pp].n)
        W("ve", dA[pp], dA[pp].n)
        W("ve", vv, x1t_free[pp])
        W("ve", dSt[pp], x1_stores[pp])
        I("ve", vv, 1, lambda e, pp=pp: e.tensor_tensor(
            V("X1T", a=pp * D, n=D), V("X0T", a=pp * D, n=D),
            V("A1T", a=pp * D, n=D), ALU.add))
        x0t_free[pp] = vv.n
        own = (tt // 4) % 2 == 1
        if own:
            grp = tt // 4
            j = (grp - 1) // 2
            orow = j * 512 + (tt % 4) * 128
            W("sy", vv)
            I("sy", dSt[pp], 16, lambda e, orow=orow, pp=pp: e.dma_start(
                out=x1_d[orow:orow + 128, :], in_=V("X1T", a=pp * D, n=D)))
            x1_stores[pp] = dSt[pp].n
        rstd, negmur = stats_chain(lambda pp=pp: V("X1T", a=pp * D, n=D), par=pp)
        W("ve", sems["gpv"], sems["gpv"].n)
        I("ve", vv, 1, lambda e, pp=pp, r=rstd, nm=negmur: e.tensor_scalar(
            V("HST"), V("X1T", a=pp * D, n=D), r(), nm(), ALU.mult, ALU.add))
        x1t_free[pp] = vv.n
        W("gp", vv)
        W("gp", dN[pp], hsb_store[pp])
        I("gp", sems["gpv"], 1, lambda e, pp=pp: e.tensor_tensor(
            V("HSB", a=pp * D, n=D), V("HST"), V("BOC"), ALU.add))
        W("sy", sems["gpv"], sems["gpv"].n)
        I("sy", dN[pp], 16, lambda e, tt=tt, pp=pp: e.dma_start(
            out=hsbo_d[tt * 128:(tt + 1) * 128, :], in_=V("HSB", a=pp * D, n=D)))
        hsb_store[pp] = dN[pp].n
        sb0 = 32 * pp
        I("ve", vv, 1, lambda e, sb0=sb0: e.tensor_scalar(
            V("ST", a=sb0, n=1), V("ST", a=sb0 + 25, n=1), 1.0 + EPS, EPS * EPS,
            ALU.mult, ALU.add))
        W("sc", vv)
        I("sc", ex, 1, lambda e, sb0=sb0: e.activation(
            V("ST", a=sb0 + 1, n=1), V("ST", a=sb0, n=1), AF.Sqrt,
            bias=0.0, scale=1.0))
        W("ve", ex)
        I("ve", vv, 1, lambda e, sb0=sb0: e.reciprocal(
            V("ST", a=sb0 + 2, n=1), V("ST", a=sb0 + 1, n=1)))
        I("ve", vv, 1, lambda e, sb0=sb0: e.tensor_tensor(
            V("ST", a=sb0 + 3, n=1), V("ST", a=sb0 + 26, n=1),
            V("ST", a=sb0 + 2, n=1), ALU.mult))
        I("ve", vv, 1, lambda e, sb0=sb0: e.tensor_scalar(
            V("TMP1"), V("HST"), V("ST", a=sb0 + 3, n=1), None, ALU.mult))
        W("gp", vv)
        I("gp", sems["gpv"], 1, lambda e: e.tensor_tensor(
            V("TMP2"), V("TMP1"), V("LW1"), ALU.mult))
        W("ve", sems["gpv"], sems["gpv"].n)
        W("ve", dLn[pp], ln1_store[pp])
        I("ve", vv, 1, lambda e, pp=pp: e.tensor_tensor(
            V("LN1T", a=pp * D, n=D), V("TMP2"), V("LB1"), ALU.add))
        W("sy", vv)
        I("sy", dLn[pp], 16, lambda e, tt=tt, pp=pp: e.dma_start(
            out=ln1_d[tt * 128:(tt + 1) * 128, :], in_=V("LN1T", a=pp * D, n=D)))
        ln1_store[pp] = dLn[pp].n

    def emit_l1f_loads(half):
        for fc in range(16):
            I("sy", dS, 16, lambda e, fc=fc, half=half: e.dma_start(
                out=V("L1F", a=fc * OWN, n=OWN),
                in_=ln1_d[half * OWN:(half + 1) * OWN, fc * 128:(fc + 1) * 128],
                transpose=True))
        W("te", dS, dS.n)

    def emit_proj_oc(wt, bias_name, tgs, rope, dst, half, oc):
        pp2 = oc % 3
        W("gp", ex, wband_free[pp2])
        W("gp", mm, wband_mm[pp2])
        I("gp", dGp[pp2], 16, lambda e, oc=oc, pp2=pp2, wt=wt: e.dma_start(
            out=V("WB", a=pp2 * D, n=D),
            in_=wt[oc].rearrange("p e i -> p (e i)")))
        W("te", dGp[pp2], dGp[pp2].n)
        for ec in range(16):
            for bi, tg in enumerate(tgs):
                if ec == 0:
                    W("te", ex, evac_ex[bi])
                I("te", mm, 1, lambda e, ec=ec, bi=bi, tg=tg, pp2=pp2:
                    e.matmul(pbk[bi][:, :],
                             V("WB", a=pp2 * D + ec * 128, n=128),
                             V("L1F", a=ec * OWN + tg * 512, n=512),
                             start=(ec == 0), stop=(ec == 15)))
        mm_done = mm.n
        wband_mm[pp2] = mm.n
        for bi, tg in enumerate(tgs):
            kb = (oc * len(tgs) + bi) % 2
            W("sc", mm, mm_done)
            evac_mark = True
            if rope:
                W("sc", vv, kt_free[kb])
                I("sc", ex, 1, lambda e, bi=bi, kb=kb, oc=oc: e.activation(
                    V("KB", a=kb * 512, n=512), pbk[bi][:, :], AF.Identity,
                    bias=V(bias_name, a=oc, n=1), scale=1.0))
                evac_ex[bi] = ex.n
                W("ve", ex)
                W("ve", dKr[kb], kro_store[kb])
                rope_emit(V("KB", a=kb * 512, n=512),
                          half * OWN + tg * 512,
                          V("KRO", a=kb * 512, n=512))
                kt_free[kb] = vv.n
                W("sy", vv)
            else:
                W("sc", dKr[kb], kro_store[kb])
                I("sc", ex, 1, lambda e, bi=bi, kb=kb, oc=oc: e.activation(
                    V("KRO", a=kb * 512, n=512), pbk[bi][:, :], AF.Identity,
                    bias=V(bias_name, a=oc, n=1), scale=1.0))
                evac_ex[bi] = ex.n
                W("sy", ex)
            if dst is q2_d:
                col = (2 * half + (1 if tg == 3 else 0)) * 512
            else:
                col = half * OWN + tg * 512
            I("sy", dKr[kb], 16, lambda e, oc=oc, col=col, kb=kb, dst=dst:
                e.dma_start(
                    out=dst[oc * 128:(oc + 1) * 128, col:col + 512],
                    in_=V("KRO", a=kb * 512, n=512)))
            kro_store[kb] = dKr[kb].n
        wband_free[pp2] = ex.n

    def proj_units(half):
        sel = DBG_PROJ if DBG_PROJ is not None else "kvq"
        units = []
        noc = DBG_OC if DBG_OC else 16
        if "k" in sel:
            for oc in range(noc):
                units.append(lambda oc=oc, h=half: emit_proj_oc(
                    wk_t, "BK", [0, 1, 2, 3], True, k2_d, h, oc))
        if "v" in sel:
            for oc in range(noc):
                units.append(lambda oc=oc, h=half: emit_proj_oc(
                    wv_t, "BV", [0, 1, 2, 3], False, v2_d, h, oc))
        if "q" in sel:
            for oc in range(noc):
                units.append(lambda oc=oc, h=half: emit_proj_oc(
                    wq_t, "BQ", [1, 3], True, q2_d, h, oc))
        return units

    nt3 = DBG_T16 if DBG_T16 else 16
    for t16 in range(nt3):
        emit_tile(t16)
    barrier()
    emit_l1f_loads(0)
    units0 = proj_units(0)
    ti = 0
    for i, u in enumerate(units0):
        u()
        if i % 3 == 2 and ti < nt3:
            emit_tile(16 + ti)
            ti += 1
    while ti < nt3:
        emit_tile(16 + ti)
        ti += 1
    barrier()
    emit_l1f_loads(1)
    for u in proj_units(1):
        u()
    barrier()

    I("sy", dS, 16, lambda e: e.dma_start(
        out=V("DM2").rearrange("p (c q) -> p c q", c=4), in_=dmask.transpose([1, 0, 2])))
    I("sy", dS, 16, lambda e: e.dma_start(
        out=V("PM2").rearrange("p (c q) -> p c q", c=NG), in_=pmask.transpose([1, 0, 2])))
    dm2_ready = dS.n
    W("te", dS, dm2_ready)
    W("ve", dS, dm2_ready)
    kv2_loads = [0, 0]
    head_free2 = [None, None]
    for h in range(DBG_H2 if DBG_H2 else H):
        pp = h % 2
        W("sy", mm, head_free2[pp])
        I("sy", dK[pp], 16, lambda e, h=h, pp=pp: e.dma_start(
            out=V("K2F", a=pp * L, n=L), in_=k2_d[h * 128:(h + 1) * 128, :]))
        I("sy", dK[pp], 16, lambda e, h=h, pp=pp: e.dma_start(
            out=V("V2F", a=pp * L, n=L), in_=v2_d[h * 128:(h + 1) * 128, :]))
        I("sy", dK[pp], 16, lambda e, h=h, pp=pp: e.dma_start(
            out=V("Q2F", a=pp * OWN, n=OWN), in_=q2_d[h * 128:(h + 1) * 128, :]))
        kv2_loads[pp] += 3
        W("te", dK[pp], 16 * (kv_loads[pp] + kv2_loads[pp]))
        pb0_bf = pbk[0][:, 0:64].bitcast(BF16)
        cpv = {}
        for kc in range(32):
            if kc >= 2:
                W("te", vv, cpv[kc - 2])
            I("te", mm, 1, lambda e, pp=pp, kc=kc: e.transpose(
                pt_bf[:, 0:128] if kc % 2 == 0 else pb0_bf,
                V("V2F", a=pp * L + kc * 128, n=128), V("IDN")))
            W("ve", mm, mm.n)
            I("ve", vv, 1, lambda e, pp=pp, kc=kc: e.tensor_copy(
                V("V2S", a=pp * L + kc * 128, n=128),
                pt_bf[:, 0:128] if kc % 2 == 0 else pb0_bf),
              chain=False)
            cpv[kc] = vv.n
        v2s_ready = vv.n
        W("te", vv, v2s_ready)
        for j, s in enumerate(OWN_SLOTS):
            epi = emit_attn(s, EXT[s], V("K2F", a=pp * L, n=L),
                            V("Q2F", a=pp * OWN + j * 512, n=512),
                            V("V2S", a=pp * L, n=L), -M2 * SCALE,
                            V("A2P", a=h * OWN + j * 512, n=512),
                            dm="DM2", pm="PM2")
            if pend_epi[0] is not None:
                pend_epi[0]()
            pend_epi[0] = epi
        head_free2[pp] = mm.n
    pend_epi[0]()
    pend_epi[0] = None
    barrier()

    for dc in range(16):
        I("gp", sems["dWO"], 16, lambda e, dc=dc: e.dma_start(
            out=V("WO", a=dc * D, n=D), in_=wo_t[dc * 128:(dc + 1) * 128, :]))
    I("gp", dG0, 16, lambda e: e.dma_start(out=V("LW2", rows=(0, 1)), in_=ln2w[:, :]))
    I("gp", dG0, 16, lambda e: e.dma_start(out=V("LB2", rows=(0, 1)), in_=ln2b[:, :]))
    W("te", dG0, dG0.n)
    for nm in ("LW2", "LB2"):
        for h2 in range(4):
            I("te", mm, 1, lambda e, h2=h2, nm=nm: e.matmul(
                pbk[0][:, :], V("ONEB", rows=(0, 1), n=128),
                V(nm, a=h2 * 512, n=512, rows=(0, 1)), start=True, stop=True))
            W("sc", mm)
            I("sc", ex, 1, lambda e, h2=h2, nm=nm: e.copy(
                V(nm, a=h2 * 512, n=512), pbk[0][:, :]))
            W("te", ex)
    wo_ready = sems["dWO"].n
    W("te", sems["dWO"], wo_ready)
    hsb_loads = [0, 0]
    ln2_store = [None, None]
    hsbt_free = [None, None]
    ht_free = [None, None]
    h_stores = [None, None]
    for ot in range(DBG_OT if DBG_OT else ONT):
        pp = ot % 2
        gt = OT2G[ot]
        W("sy", vv, hsbt_free[pp])
        I("sy", dA[pp], 16, lambda e, gt=gt, pp=pp: e.dma_start(
            out=V("HSBT", a=pp * D, n=D), in_=hsbo_d[gt * 128:(gt + 1) * 128, :]))
        hsb_loads[pp] += 1
        for dc in range(16):
            for eg in range(4):
                I("te", mm, 1, lambda e, dc=dc, eg=eg, ot=ot: e.matmul(
                    pbk[eg][:, :], V("A2P", a=dc * D + ot * 128, n=128),
                    V("WO", a=dc * D + eg * 512, n=512),
                    start=(dc == 0), stop=(dc == 15)))
        mm_done = mm.n
        W("ve", mm, mm_done)
        W("ve", dA[pp], 16 * (a1_loads[pp] + hsb_loads[pp]))
        W("ve", ex, ht_free[pp])
        W("ve", dSt[pp], h_stores[pp])
        for eg in range(4):
            I("ve", vv, 1, lambda e, eg=eg, pp=pp: e.tensor_tensor(
                V("HT", a=pp * D + eg * 512, n=512), pbk[eg][:, :],
                V("HSBT", a=pp * D + eg * 512, n=512), ALU.add))
        hsbt_free[pp] = vv.n
        W("te", vv)
        W("sy", vv)
        I("sy", dSt[pp], 16, lambda e, ot=ot, pp=pp: e.dma_start(
            out=h_d[ot * 128:(ot + 1) * 128, :], in_=V("HT", a=pp * D, n=D)))
        h_stores[pp] = dSt[pp].n
        rstd3, negmur3 = stats_chain(lambda pp=pp: V("HT", a=pp * D, n=D), par=pp)
        W("sc", vv)
        W("sc", dN[pp], ln2_store[pp])
        I("sc", ex, 1, lambda e, pp=pp, r=rstd3, nm=negmur3: e.activation(
            V("TMP1"), V("HT", a=pp * D, n=D), AF.Identity, bias=nm(), scale=r()))
        ht_free[pp] = ex.n
        W("ve", ex)
        I("ve", vv, 1, lambda e: e.tensor_tensor(
            V("TMP2"), V("TMP1"), V("LW2"), ALU.mult))
        I("ve", vv, 1, lambda e: e.tensor_tensor(
            V("TMP1"), V("TMP2"), V("LB2"), ALU.add))
        I("ve", vv, 1, lambda e, pp=pp: e.tensor_copy(
            V("LN2T", a=pp * D, n=D), V("TMP1")))
        I("ve", vv, 1, lambda e, pp=pp: e.tensor_tensor(
            V("LN2LO", a=pp * D, n=D), V("TMP1"), V("LN2T", a=pp * D, n=D),
            ALU.subtract))
        W("sy", vv)
        I("sy", dN[pp], 16, lambda e, ot=ot, pp=pp: e.dma_start(
            out=ln2h_d[ot * 128:(ot + 1) * 128, :], in_=V("LN2T", a=pp * D, n=D)))
        I("sy", dN[pp], 16, lambda e, ot=ot, pp=pp: e.dma_start(
            out=ln2lo_d[ot * 128:(ot + 1) * 128, :], in_=V("LN2LO", a=pp * D, n=D)))
        ln2_store[pp] = dN[pp].n
    barrier()

    for sl in range(DBG_SL if DBG_SL else 4):
        pp = sl % 2
        if sl >= 1:
            W("sy", mm)
        for fc in range(16):
            I("sy", dK[pp], 16, lambda e, sl=sl, fc=fc, pp=pp: e.dma_start(
                out=V("L2F", a=pp * 16 * 512 + fc * 512, n=512),
                in_=ln2h_d[sl * 512:(sl + 1) * 512, fc * 128:(fc + 1) * 128],
                transpose=True))
            I("sy", dK[pp], 16, lambda e, sl=sl, fc=fc, pp=pp: e.dma_start(
                out=V("L2FLO", a=fc * 512, n=512),
                in_=ln2lo_d[sl * 512:(sl + 1) * 512, fc * 128:(fc + 1) * 128],
                transpose=True))
        kv2_loads[pp] += 32
        W("te", dK[pp], 16 * (kv_loads[pp] + kv2_loads[pp]))
        for twi in range(4):
            tt = sl * 4 + twi
            W("te", ex)
            for fc in range(16):
                I("te", mm, 1, lambda e, fc=fc, twi=twi, pp=pp: e.matmul(
                    pbk[4][:, 0:4],
                    V("L2F", a=pp * 16 * 512 + fc * 512 + twi * 128, n=128),
                    V("WG", a=fc * 4, n=4), start=(fc == 0), stop=False))
                I("te", mm, 1, lambda e, fc=fc, twi=twi, pp=pp: e.matmul(
                    pbk[4][:, 0:4],
                    V("L2F", a=pp * 16 * 512 + fc * 512 + twi * 128, n=128),
                    V("WGLO", a=fc * 4, n=4), start=False, stop=False))
                I("te", mm, 1, lambda e, fc=fc, twi=twi, pp=pp: e.matmul(
                    pbk[4][:, 0:4],
                    V("L2FLO", a=fc * 512 + twi * 128, n=128),
                    V("WG", a=fc * 4, n=4), start=False,
                    stop=(fc == 15)))
            gmm = mm.n
            W("ve", mm, gmm)
            I("ve", vv, 1, lambda e: e.reduce_max(
                V("ST", a=32, n=1), pbk[4][:, 0:4], axis=AX.X))
            I("ve", vv, 1, lambda e: e.tensor_scalar(
                V("ST", a=33, n=1), V("ST", a=32, n=1), -1.0, None, ALU.mult))
            W("sc", vv)
            I("sc", ex, 1, lambda e: e.activation(
                V("GE4", a=0, n=4), pbk[4][:, 0:4], AF.Exp,
                bias=V("ST", a=33, n=1), scale=1.0, accum_out=V("ST", a=34, n=1)))
            W("ve", ex)
            I("ve", vv, 1, lambda e: e.reciprocal(V("ST", a=35, n=1), V("ST", a=34, n=1)))
            I("ve", vv, 1, lambda e: e.tensor_scalar(
                V("GRW", a=0, n=4), V("GE4", a=0, n=4), V("ST", a=35, n=1),
                None, ALU.mult))
            I("ve", vv, 1, lambda e: e.reduce_max(
                V("ST", a=36, n=1), V("GRW", a=0, n=4), axis=AX.X))
            I("ve", vv, 1, lambda e: e.tensor_scalar(
                V("GRW", a=4, n=4), V("GRW", a=0, n=4), V("ST", a=36, n=1),
                None, ALU.is_lt))
            I("ve", vv, 1, lambda e: e.tensor_tensor(
                V("GRW", a=8, n=4), V("GRW", a=0, n=4), V("GRW", a=4, n=4), ALU.mult))
            I("ve", vv, 1, lambda e: e.reduce_max(
                V("ST", a=37, n=1), V("GRW", a=8, n=4), axis=AX.X))
            I("ve", vv, 1, lambda e: e.tensor_scalar(
                V("GRW", a=4, n=4), V("GRW", a=0, n=4), V("ST", a=37, n=1),
                None, ALU.is_ge))
            W("sc", vv)
            I("sc", ex, 1, lambda e: e.activation(
                V("ST", a=38, n=1), V("ST", a=36, n=1), AF.Exp, bias=0.0, scale=1.0))
            I("sc", ex, 1, lambda e: e.activation(
                V("ST", a=39, n=1), V("ST", a=37, n=1), AF.Exp, bias=0.0, scale=1.0))
            I("sc", ex, 1, lambda e: e.activation(
                V("GE4", a=4, n=4), V("GRW", a=0, n=4), AF.Exp, bias=0.0, scale=1.0))
            W("ve", ex)
            I("ve", vv, 1, lambda e: e.tensor_tensor(
                V("ST", a=40, n=1), V("ST", a=38, n=1), V("ST", a=39, n=1), ALU.add))
            I("ve", vv, 1, lambda e: e.reciprocal(V("ST", a=41, n=1), V("ST", a=40, n=1)))
            I("ve", vv, 1, lambda e: e.tensor_tensor(
                V("GRW", a=12, n=4), V("GE4", a=4, n=4), V("GRW", a=4, n=4), ALU.mult))
            I("ve", vv, 1, lambda e: e.tensor_scalar(
                V("CMB"), V("GRW", a=12, n=4), V("ST", a=41, n=1), None, ALU.mult))
            W("te", vv)
            for e4 in range(NE):
                I("te", mm, 1, lambda e, e4=e4: e.transpose(
                    pt_bf[0:1, e4 * 128:(e4 + 1) * 128], V("CMB", a=e4, n=1),
                    V("IDN")))
            W("ve", mm)
            for e4 in range(NE):
                I("ve", vv, 1, lambda e, e4=e4, tt=tt: e.tensor_copy(
                    V("CROW", a=e4 * OWN + tt * 128, n=128, rows=(0, 1)),
                    pt_bf[0:1, e4 * 128:(e4 + 1) * 128]))
            W("te", vv)
    barrier()

    w1b_free = [None, None]
    w1b_mm = [None, None]
    gelu_ex = [None, None]
    mst_free = [None, None]
    m_prev_expert = [0, 0]
    for e4 in range(DBG_E if DBG_E else NE):
        for ic in range(16):
            I("gp", sems["dWE2"], 16, lambda e, e4=e4, ic=ic: e.dma_start(
                out=V("WE2", a=ic * D, n=D),
                in_=we2_t[e4, ic * 128:(ic + 1) * 128, :]))
        I("gp", sems["dWE2"], 16, lambda e, e4=e4: e.dma_start(
            out=V("BE2", rows=(0, 1)), in_=be2_t[e4:e4 + 1, :]))
        we2_ready = sems["dWE2"].n
        if e4 > 0:
            W("gp", dM[0], m_prev_expert[0])
            W("gp", dM[1], m_prev_expert[1])
        for sl in range(DBG_SL if DBG_SL else 4):
            pp = sl % 2
            if e4 > 0 or sl >= 2:
                W("sy", mm)
            for fc in range(16):
                I("sy", dK[pp], 16, lambda e, sl=sl, fc=fc, pp=pp: e.dma_start(
                    out=V("L2F", a=pp * 16 * 512 + fc * 512, n=512),
                    in_=ln2h_d[sl * 512:(sl + 1) * 512, fc * 128:(fc + 1) * 128],
                    transpose=True))
            kv2_loads[pp] += 16
            l2f_ready = 16 * (kv_loads[pp] + kv2_loads[pp])
            W("te", sems["dWE2"], we2_ready)
            W("te", vv)
            I("te", mm, 1, lambda e, e4=e4, sl=sl: e.matmul(
                pbk[5][:, :], V("ONEB", rows=(0, 1), n=128),
                V("CROW", a=e4 * OWN + sl * 512, n=512, rows=(0, 1)),
                start=True, stop=True))
            W("sc", mm)
            I("sc", ex, 1, lambda e: e.copy(V("CB"), pbk[5][:, :]))
            cb_ready = ex.n
            for ic in range(16):
                pp2 = ic % 2
                bi = ic % 2
                W("gp", ex, w1b_free[pp2])
                W("gp", mm, w1b_mm[pp2])
                I("gp", dGp[pp2], 16, lambda e, e4=e4, ic=ic, pp2=pp2, sl=sl: e.dma_start(
                    out=V("W1B", a=pp2 * D, n=D),
                    in_=we1_t[e4, ic].rearrange("p f i -> p (f i)")).annotate(
                        f"w1b e{e4} sl{sl} ic{ic} pp{pp2}"))
                W("te", dGp[pp2], dGp[pp2].n)
                W("te", dK[pp], l2f_ready)
                W("te", ex, gelu_ex[bi])
                for fc in range(16):
                    I("te", mm, 1, lambda e, fc=fc, bi=bi, pp=pp, pp2=pp2,
                      e4=e4, sl=sl, ic=ic: e.matmul(
                        pbk[bi][:, :], V("W1B", a=pp2 * D + fc * 128, n=128),
                        V("L2F", a=pp * 16 * 512 + fc * 512, n=512),
                        start=(fc == 0), stop=(fc == 15)).annotate(
                            f"mm1 e{e4} sl{sl} ic{ic} fc{fc}"))
                w1b_mm[pp2] = mm.n
                W("sc", mm)
                W("sc", vv)
                I("sc", ex, 1, lambda e, bi=bi, ic=ic, e4=e4: e.activation(
                    V("GT"), pbk[bi][:, :], AF.Gelu,
                    bias=V("BE1", a=ic * 4 + e4, n=1), scale=1.0))
                gelu_ex[bi] = ex.n
                w1b_free[pp2] = ex.n
                W("ve", ex)
                W("ve", ex, cb_ready)
                I("ve", vv, 1, lambda e, ic=ic: e.tensor_tensor(
                    V("HMW", a=ic * 512, n=512), V("GT"), V("CB"), ALU.mult))
            hm_ready = vv.n
            W("te", vv, hm_ready)
            for twi in range(4):
                tt = sl * 4 + twi
                for ic in range(16):
                    for dg in range(4):
                        I("te", mm, 1, lambda e, ic=ic, dg=dg, twi=twi: e.matmul(
                            pbk[dg][:, :], V("HMW", a=ic * 512 + twi * 128, n=128),
                            V("WE2", a=ic * D + dg * 512, n=512),
                            start=(ic == 0), stop=False))
                for dg in range(4):
                    I("te", mm, 1, lambda e, dg=dg, e4=e4, tt=tt: e.matmul(
                        pbk[dg][:, :],
                        V("CROW", a=e4 * OWN + tt * 128, n=128, rows=(0, 1)),
                        V("BE2", a=dg * 512, n=512, rows=(0, 1)),
                        start=False, stop=True))
                m2done = mm.n
                for dg in range(4):
                    mb = (tt * 4 + dg) % 2
                    W("ve", mm, m2done)
                    W("ve", dM[mb], mst_free[mb])
                    I("ve", vv, 1, lambda e, dg=dg, mb=mb: e.tensor_copy(
                        V("MST", a=mb * 512, n=512), pbk[dg][:, :]))
                    W("gp", vv)
                    if e4 == 0:
                        I("gp", dM[mb], 16, lambda e, tt=tt, dg=dg, mb=mb: e.dma_start(
                            out=m_d[tt * 128:(tt + 1) * 128, dg * 512:(dg + 1) * 512],
                            in_=V("MST", a=mb * 512, n=512)))
                    else:
                        I("gp", dM[mb], 16, lambda e, tt=tt, dg=dg, mb=mb: e.dma_start(
                            out=m_d[tt * 128:(tt + 1) * 128, dg * 512:(dg + 1) * 512],
                            in_=V("MST", a=mb * 512, n=512), accum_op=ALU.add))
                    mst_free[mb] = dM[mb].n
                W("te", vv)
        m_prev_expert[0] = dM[0].n
        m_prev_expert[1] = dM[1].n
    barrier()

    fin_free = [None, None]
    out_store = [None, None]
    fin_loads = [0, 0]
    for ot in range(ONT):
        pp = ot % 2
        base = pp * 3 * D
        W("sy", vv, fin_free[pp])
        I("sy", dX[pp], 16, lambda e, ot=ot, base=base: e.dma_start(
            out=V("FIN", a=base, n=D), in_=x1_d[ot * 128:(ot + 1) * 128, :]))
        I("sy", dX[pp], 16, lambda e, ot=ot, base=base: e.dma_start(
            out=V("FIN", a=base + D, n=D), in_=h_d[ot * 128:(ot + 1) * 128, :]))
        I("sy", dX[pp], 16, lambda e, ot=ot, base=base: e.dma_start(
            out=V("FIN", a=base + 2 * D, n=D), in_=m_d[ot * 128:(ot + 1) * 128, :]))
        x0_loads[pp] += 3
        W("ve", dX[pp], dX[pp].n)
        W("ve", dO[pp], out_store[pp])
        I("ve", vv, 1, lambda e, base=base: e.tensor_tensor(
            V("FIN", a=base, n=D), V("FIN", a=base, n=D),
            V("FIN", a=base + D, n=D), ALU.add))
        I("ve", vv, 1, lambda e, base=base, pp=pp: e.tensor_tensor(
            V("OUTT", a=pp * D, n=D), V("FIN", a=base, n=D),
            V("FIN", a=base + 2 * D, n=D), ALU.add))
        fin_free[pp] = vv.n
        W("sy", vv)
        I("sy", dO[pp], 16, lambda e, ot=ot, pp=pp: e.dma_start(
            out=out[ot * 128:(ot + 1) * 128, :], in_=V("OUTT", a=pp * D, n=D)))
        out_store[pp] = dO[pp].n
    if tap_outs:
        barrier()
        for t, (dst, src) in tap_outs.items():
            I("sy", dO[0], 16, lambda e, dst=dst, src=src: e.dma_start(
                out=dst[:, :], in_=src[:, :]))
    W("sy", dO[0], dO[0].n)
    W("sy", dO[1], dO[1].n)

    if max_phase < 99:
        cut_ph = min(k for k in cuts if k > max_phase) if any(
            k > max_phase for k in cuts) else None
        if cut_ph is not None:
            for ek in ENGS:
                plan[ek] = plan[ek][:cuts[cut_ph][ek]]
    with nc.Block() as block:
        @block.tensor
        def _(e):
            for f in plan["te"]:
                f(e)

        @block.scalar
        def _(e):
            for f in plan["sc"]:
                f(e)

        @block.vector
        def _(e):
            for f in plan["ve"]:
                f(e)

        @block.gpsimd
        def _(e):
            for f in plan["gp"]:
                f(e)

        @block.sync
        def _(e):
            for f in plan["sy"]:
                f(e)

    es.close()
    return nc



def _host_inputs(core, args):
    b, p = core // 2, core % 2
    g = np.arange(NG)
    tg = (g ^ 1) if p == 0 else g
    tp = (tg[:, None] * G + np.arange(G)[None, :]).reshape(-1)

    x0 = np.ascontiguousarray(np.asarray(args["L0_attn_x"], np.float32)[b][tp])

    ki = np.arange(128)
    qj = np.arange(512)
    dmask = np.empty((4, 128, 512), np.float32)
    for k in range(4):
        dmask[k] = np.where((k * 128 + ki)[:, None] <= qj[None, :], 0.0, NEG)
    pmask = np.empty((NG, 128, 512), np.float32)
    for s in range(NG):
        pmask[s] = 0.0 if tg[s ^ 1] < tg[s] else NEG

    inv = 1.0 / (10000.0 ** (np.arange(0, HD, 2) / HD))
    ang = tp[None, :].astype(np.float64) * inv[:, None]
    cost = np.concatenate([np.cos(ang), np.cos(ang)], 0).astype(np.float32)
    sint = np.concatenate([np.sin(ang), np.sin(ang)], 0).astype(np.float32)
    rot = np.zeros((128, 128), np.float32)
    rot[64:, :64] = -np.eye(64)
    rot[:64, 64:] = np.eye(64)

    def wt(w):
        return np.ascontiguousarray(
            np.asarray(w, np.float32).reshape(16, 128, 16, 128).transpose(2, 1, 0, 3))

    def bt(v):
        return np.ascontiguousarray(np.asarray(v, np.float32).reshape(16, 128).T)

    we1 = np.asarray(args["L0_ffn_we1"], np.float32)
    we1_t = np.ascontiguousarray(
        we1.reshape(NE, 16, 128, 16, 128).transpose(0, 3, 2, 1, 4))
    wg = np.asarray(args["L0_ffn_w_gate"], np.float32)
    wg_t = np.ascontiguousarray(
        wg.reshape(16, 128, 4).transpose(1, 0, 2).reshape(128, 64))
    be1 = np.asarray(args["L0_ffn_be1"], np.float32)
    be1_t = np.ascontiguousarray(
        be1.reshape(NE, 16, 128).transpose(2, 1, 0).reshape(128, 64))

    return {
        "x0p": x0,
        "dmask": dmask, "pmask": pmask, "cost": cost, "sint": sint,
        "rot_t": rot.astype(BF),
        "wq_t": wt(args["L0_ffn_wq"]), "wk_t": wt(args["L0_ffn_wk"]),
        "wv_t": wt(args["L0_ffn_wv"]),
        "wo_t": np.ascontiguousarray(np.asarray(args["L0_ffn_wo"], np.float32)),
        "bq_t": bt(args["L0_ffn_bq"]), "bk_t": bt(args["L0_ffn_bk"]),
        "bv_t": bt(args["L0_ffn_bv"]),
        "bo_b": np.asarray(args["L0_ffn_bo"], np.float32).reshape(1, D),
        "ln1w": np.asarray(args["L0_ffn_ln1_w"], np.float32).reshape(1, D),
        "ln1b": np.asarray(args["L0_ffn_ln1_b"], np.float32).reshape(1, D),
        "ln2w": np.asarray(args["L0_ffn_ln2_w"], np.float32).reshape(1, D),
        "ln2b": np.asarray(args["L0_ffn_ln2_b"], np.float32).reshape(1, D),
        "wg_t": wg_t, "be1_t": be1_t,
        "be2_t": np.ascontiguousarray(np.asarray(args["L0_ffn_be2"], np.float32)),
        "we1_t": we1_t,
        "we2_t": np.ascontiguousarray(np.asarray(args["L0_ffn_we2"], np.float32)),
        "ones_r": np.ones((1, 128), np.float32),
        "fconst": np.broadcast_to(np.array(
            [EPS, -M1 * SCALE, -M2 * SCALE, 0.0], np.float32), (128, 4)).copy(),
        "pmul_in": np.broadcast_to(
            np.array([1.0 if tg[s ^ 1] < tg[s] else 0.0 for s in range(NG)],
                     np.float32), (128, NG)).copy(),
        "ident_in": np.eye(128, dtype=BF),
    }, tp


def kernel(**inputs):
    nc = build_program()
    in_maps = []
    tps = []
    for core in range(8):
        m, tp = _host_inputs(core, inputs)
        in_maps.append(m)
        tps.append(tp)
    res = run_bass_kernel_spmd(nc, in_maps, core_ids=list(range(8)))
    full = np.empty((B, L, D), np.float32)
    own_storage = np.concatenate(
        [np.arange((2 * j + 1) * G, (2 * j + 2) * G) for j in range(4)])
    for core in range(8):
        b = core // 2
        full[b, tps[core][own_storage]] = res.results[core]["out"]
    return full


# version 13
# speedup vs baseline: 1.0562x; 1.0562x over previous
import math
import contextlib
import numpy as np
import ml_dtypes
import concourse.bass as bass
import concourse.mybir as mybir
from concourse.bass_utils import run_bass_kernel_spmd

F32 = mybir.dt.float32
BF16 = mybir.dt.bfloat16
AF = mybir.ActivationFunctionType
ALU = mybir.AluOpType
AX = mybir.AxisListType
BF = ml_dtypes.bfloat16

B, L, D, H, HD, NE = 4, 4096, 2048, 16, 128, 4
G = 512
NG = L // G
NT = L // 128
OWN = L // 2
ONT = OWN // 128
SCALE = 1.0 / math.sqrt(HD)
M1, M2 = 24.0, 14.0
NEG = -1.0e9
EPS = 1e-5

EXT = [4 * (s + 2) if s % 2 == 0 else 4 * (s + 1) for s in range(NG)]
OWN_SLOTS = [1, 3, 5, 7]
OT2G = [(ot // 4) * 8 + 4 + (ot % 4) for ot in range(ONT)]

ENGS = ("te", "sc", "ve", "gp", "sy")
DBG_T16 = None
DBG_PROJ = None
DBG_OC = None
DBG_H1 = None
DBG_H2 = None
DBG_OT = None
DBG_SL = None
DBG_E = None


class Sem:
    def __init__(self, h):
        self.h = h
        self.n = 0


def build_program(taps=(), max_phase=99):
    nc = bass.Bass("TRN2", target_bir_lowering=False)
    es = contextlib.ExitStack()

    def din(name, shape, dt=F32):
        return nc.dram_tensor(name, shape, dt, kind="ExternalInput").ap()

    x0p = din("x0p", [L, D])
    dmask = din("dmask", [4, 128, 512])
    pmask = din("pmask", [NG, 128, 512])
    cost = din("cost", [128, L])
    sint = din("sint", [128, L])
    rot_t = din("rot_t", [128, 128], BF16)
    wq_t = din("wq_t", [16, 128, 16, 128])
    wk_t = din("wk_t", [16, 128, 16, 128])
    wv_t = din("wv_t", [16, 128, 16, 128])
    wo_t = din("wo_t", [D, D])
    bq_t = din("bq_t", [128, 16])
    bk_t = din("bk_t", [128, 16])
    bv_t = din("bv_t", [128, 16])
    bo_b = din("bo_b", [1, D])
    ln1w = din("ln1w", [1, D])
    ln1b = din("ln1b", [1, D])
    ln2w = din("ln2w", [1, D])
    ln2b = din("ln2b", [1, D])
    wg_t = din("wg_t", [128, 64])
    be1_t = din("be1_t", [128, 64])
    be2_t = din("be2_t", [NE, D])
    we1_t = din("we1_t", [NE, 16, 128, 16, 128])
    we2_t = din("we2_t", [NE, D, D])
    ones_r = din("ones_r", [1, 128])
    fconst = din("fconst", [128, 4])
    pmul_in = din("pmul_in", [128, NG])
    ident_in = din("ident_in", [128, 128], BF16)

    out = nc.dram_tensor("out", [OWN, D], F32, kind="ExternalOutput").ap()

    def dsc(name, shape, dt):
        return nc.dram_tensor(name, shape, dt, kind="Internal").ap()

    n1_d = dsc("n1_d", [L, D], BF16)
    a1_d = dsc("a1_d", [D, L], BF16)
    x1_d = dsc("x1_d", [OWN, D], F32)
    hsbo_d = dsc("hsbo_d", [L, D], BF16)
    ln1_d = dsc("ln1_d", [L, D], BF16)
    k2_d = dsc("k2_d", [D, L], BF16)
    v2_d = dsc("v2_d", [D, L], BF16)
    q2_d = dsc("q2_d", [D, OWN], BF16)
    h_d = dsc("h_d", [OWN, D], F32)
    ln2h_d = dsc("ln2h_d", [OWN, D], BF16)
    ln2lo_d = dsc("ln2lo_d", [OWN, D], BF16)
    m_d = dsc("m_d", [OWN, D], F32)

    tap_outs = {}
    _tap_src = {"n1": (n1_d, BF16), "a1": (a1_d, BF16), "x1": (x1_d, F32),
                "hsbo": (hsbo_d, BF16), "ln1": (ln1_d, BF16),
                "k2": (k2_d, BF16), "v2": (v2_d, BF16), "q2": (q2_d, BF16),
                "h": (h_d, F32), "ln2h": (ln2h_d, BF16), "m": (m_d, F32)}
    for t in taps:
        src, dt = _tap_src[t]
        tap_outs[t] = (nc.dram_tensor("tap_" + t, list(src.shape), dt,
                                      kind="ExternalOutput").ap(), src)

    segs = []

    def seg(name, dt, elems, p0, p1=None):
        ph = frozenset(p0) if isinstance(p0, (list, tuple, set)) else \
            frozenset(range(p0, (p1 if p1 is not None else p0) + 1))
        segs.append([name, dt, elems, ph, None])

    seg("N1T", BF16, 4 * D, 1, 1)
    seg("X0Q", F32, 4 * D, 1, 1)
    seg("K1F", BF16, 2 * L, 2, 2)
    seg("V1S", BF16, 2 * L, 2, 2)
    seg("PT", BF16, 3 * 512, (2, 6))
    seg("AVN", BF16, 2 * 512, (2, 6))
    seg("AO", BF16, 2 * 512, 2, 2)
    seg("A1T", BF16, 2 * D, (3, 4))
    seg("HSB", BF16, 2 * D, (3, 4))
    seg("LN1T", BF16, 2 * D, (3, 4))
    seg("COS", BF16, L, 0, 5)
    seg("SIN", BF16, L, 0, 5)
    seg("BOC", BF16, D, 0, 4)
    seg("LW1", BF16, D, 0, 4)
    seg("LB1", BF16, D, 0, 4)
    seg("LW2", BF16, D, 7, 7)
    seg("LB2", BF16, D, 7, 7)
    seg("L1F", BF16, 16 * OWN, (4, 5))
    seg("WB", BF16, 3 * D, (4, 5))
    seg("KRO", BF16, 2 * 512, (4, 5))
    seg("KB", BF16, 2 * 512, (4, 5))
    seg("ROT", BF16, 128, 0, 5)
    seg("K2F", BF16, 2 * L, 6, 6)
    seg("V2S", BF16, 2 * L, 6, 6)
    seg("Q2F", BF16, 2 * OWN, 6, 6)
    seg("V2F", BF16, 2 * L, 6, 6)
    seg("A2P", BF16, 16 * OWN, 6, 7)
    seg("WO", BF16, 16 * D, 7, 7)
    seg("HSBT", BF16, 2 * D, 7, 7)
    seg("LN2T", BF16, 2 * D, 7, 7)
    seg("L2F", BF16, 2 * 16 * 512, 8, 9)
    seg("WG", BF16, 64, 0, 8)
    seg("WGLO", BF16, 64, 0, 8)
    seg("L2FLO", BF16, 16 * 512, 8, 8)
    seg("LN2LO", BF16, 2 * D, 7, 7)
    seg("CMB", BF16, 4, 8, 8)
    seg("CROW", BF16, NE * OWN, 8, 9)
    seg("WE2", BF16, 16 * D, 9, 9)
    seg("HMW", BF16, 16 * 512, 9, 9)
    seg("W1B", BF16, 2 * D, 9, 9)
    seg("GT", BF16, 512, 9, 9)
    seg("CB", BF16, 512, 9, 9)
    seg("BE2", BF16, D, 9, 9)
    seg("ONEB", BF16, 128, 0, 10)
    seg("IDN", BF16, 128, 0, 10)
    seg("OCB", BF16, 1, 0, 10)

    seg("DM1", F32, 4 * 512, 0, 2)
    seg("PM1", F32, NG * 512, 0, 2)
    seg("DM2", F32, 4 * 512, 6, 6)
    seg("PM2", F32, NG * 512, 6, 6)
    seg("PMUL", F32, NG, 0, 6)
    seg("X0T", F32, 2 * D, (1, 3, 4))
    seg("ST", F32, 64, 0, 10)
    seg("RZ", F32, 2048, (2, 6))
    seg("ZB", F32, 2 * 512, (2, 6))
    seg("X1T", F32, 2 * D, (3, 4))
    seg("HST", F32, D, (3, 4))
    seg("TMP1", F32, D, (3, 4, 7))
    seg("TMP2", F32, D, (3, 4, 7))
    seg("RT1", F32, 512, (4, 5))
    seg("RT2", F32, 512, (4, 5))
    seg("HT", F32, 2 * D, 7, 7)
    seg("GE4", F32, 8, 8, 8)
    seg("GRW", F32, 16, 8, 8)
    seg("MST", F32, 2 * 512, 9, 9)
    seg("FIN", F32, 3 * 2 * D, 10, 10)
    seg("OUTT", F32, 2 * D, 10, 10)
    seg("ONER", F32, 128, 0, 10)
    seg("FCONST", F32, 4, 0, 10)
    seg("BQ", F32, 16, 0, 5)
    seg("BK", F32, 16, 0, 5)
    seg("BV", F32, 16, 0, 5)
    seg("BE1", F32, 64, 0, 9)
    seg("WGF", F32, 64, 0, 8)

    placed = []
    hi = [0]
    for s in sorted(segs, key=lambda q: -(q[2] * (2 if q[1] is BF16 else 4))):
        esize = 2 if s[1] is BF16 else 4
        size = (s[2] * esize + 3) & ~3
        cand = 0
        moved = True
        while moved:
            moved = False
            for q in placed:
                if s[3] & q[3]:
                    if not (cand + size <= q[4] or q[4] + q[5] <= cand):
                        cand = (q[4] + q[5] + 3) & ~3
                        moved = True
        s[4] = cand
        placed.append([s[0], s[1], s[2], s[3], cand, size])
        hi[0] = max(hi[0], cand + size)
    total = hi[0]
    assert total <= 206000, f"SBUF over budget: {total} bytes/partition"

    ab = es.enter_context(nc.sbuf_tensor("AB", [128, (total + 1) // 2], BF16))
    SEG = {}
    for name, dt, elems, ph, boff in segs:
        SEG[name] = (dt, boff, elems)

    def V(name, a=0, n=None, rows=(0, 128)):
        dt, boff, elems = SEG[name]
        if n is None:
            n = elems - a
        assert a + n <= elems, (name, a, n, elems)
        if dt is BF16:
            o = boff // 2 + a
            return ab[rows[0]:rows[1], o:o + n]
        o = boff // 2 + 2 * a
        return ab[rows[0]:rows[1], o:o + 2 * n].bitcast(F32)

    pbk = [es.enter_context(nc.psum_tensor(f"PB{i}", [128, 512], F32))
           for i in range(7)]
    pt_bf = es.enter_context(nc.psum_tensor("PTB", [128, 1024], BF16))

    sems = {}
    for name in ("phb", "mm", "ex", "vv",
                 "dG0", "dG1", "dS", "dK0", "dK1", "dX0", "dX1",
                 "dA0", "dA1", "dN0", "dN1", "dLn0", "dLn1", "dSt0", "dSt1",
                 "dAo0", "dAo1", "dKr0", "dKr1", "dM0", "dM1", "dWE2",
                 "dWO", "dO0", "dO1", "dV0", "dV1", "dG2", "gpv",
                 "dX2", "dX3"):
        sems[name] = Sem(es.enter_context(nc.semaphore(name)))

    plan = {k: [] for k in ENGS}

    def A(ek, f):
        plan[ek].append(f)

    def W(ek, sem, val=None):
        v = sem.n if val is None else val
        if v is None or v <= 0:
            return
        plan[ek].append(lambda e, s=sem.h, v=v: e.wait_ge(s, v))

    _self_sem = {}

    def I(ek, sem, k, f, chain=True):
        if ek in ("sc", "ve") and chain:
            prev = _self_sem.get(ek)
            if prev is not None and prev[1] > 0:
                plan[ek].append(
                    lambda e, s=prev[0], v=prev[1]: e.wait_ge(s, v))
        sem.n += k
        plan[ek].append(lambda e, s=sem.h, k=k, f=f: f(e).then_inc(s, k))
        _self_sem[ek] = (sem.h, sem.n)

    phase_no = [0]
    cuts = {}

    SY_SEMS = ("dS", "dK0", "dK1", "dX0", "dX1", "dA0", "dA1", "dN0", "dN1",
               "dLn0", "dLn1", "dSt0", "dSt1", "dAo0", "dAo1", "dKr0", "dKr1",
               "dO0", "dO1")
    SC_SEMS = ("dV0", "dV1")
    GP_SEMS = ("dG0", "dG1", "dG2", "dM0", "dM1", "dWE2", "dWO")

    def barrier():
        for nm in SY_SEMS:
            s = sems[nm]
            if s.n > 0:
                plan["sy"].append(lambda e, h=s.h, v=s.n: e.wait_ge(h, v))
        for nm in GP_SEMS:
            s = sems[nm]
            if s.n > 0:
                plan["gp"].append(lambda e, h=s.h, v=s.n: e.wait_ge(h, v))
        for nm in SC_SEMS:
            s = sems[nm]
            if s.n > 0:
                plan["sc"].append(lambda e, h=s.h, v=s.n: e.wait_ge(h, v))
        for ek, s in (("te", mm), ("sc", ex), ("ve", vv)):
            if s.n > 0:
                plan[ek].append(lambda e, h=s.h, v=s.n: e.wait_ge(h, v))
        phase_no[0] += 1
        tgt = 5 * phase_no[0]
        for ek in ENGS:
            plan[ek].append(lambda e, s=sems["phb"].h: e.sem_inc(s, 1))
            plan[ek].append(lambda e, s=sems["phb"].h, t=tgt: e.wait_ge(s, t))
        cuts[phase_no[0]] = {ek: len(plan[ek]) for ek in ENGS}

    mm, ex, vv = sems["mm"], sems["ex"], sems["vv"]
    dS = sems["dS"]
    dLn = (sems["dLn0"], sems["dLn1"])
    dO = (sems["dO0"], sems["dO1"])
    dM = (sems["dM0"], sems["dM1"])
    dG0, dG1 = sems["dG0"], sems["dG1"]
    dX = (sems["dX0"], sems["dX1"])
    dA = (sems["dA0"], sems["dA1"])
    dN = (sems["dN0"], sems["dN1"])
    dK = (sems["dK0"], sems["dK1"])
    dSt = (sems["dSt0"], sems["dSt1"])
    dAo = (sems["dAo0"], sems["dAo1"])
    dKr = (sems["dKr0"], sems["dKr1"])
    dGp = (dG0, dG1, sems["dG2"])

    I("sy", dS, 16, lambda e: e.dma_start(
        out=V("DM1").rearrange("p (c q) -> p c q", c=4), in_=dmask.transpose([1, 0, 2])))
    I("sy", dS, 16, lambda e: e.dma_start(
        out=V("PM1").rearrange("p (c q) -> p c q", c=NG), in_=pmask.transpose([1, 0, 2])))
    I("sy", dS, 16, lambda e: e.dma_start(out=V("PMUL"), in_=pmul_in[:, :]))
    I("sy", dS, 16, lambda e: e.dma_start(out=V("ONER", rows=(0, 1)), in_=ones_r[:, :]))
    I("sy", dS, 16, lambda e: e.dma_start(out=V("BQ"), in_=bq_t[:, :]))
    I("sy", dS, 16, lambda e: e.dma_start(out=V("BK"), in_=bk_t[:, :]))
    I("sy", dS, 16, lambda e: e.dma_start(out=V("BV"), in_=bv_t[:, :]))
    I("sy", dS, 16, lambda e: e.dma_start(out=V("BE1"), in_=be1_t[:, :]))
    I("sy", dS, 16, lambda e: e.dma_start(out=V("IDN"), in_=ident_in[:, :]))
    I("sy", dS, 16, lambda e: e.dma_start(out=V("WGF"), in_=wg_t[:, :]))
    I("sy", dS, 16, lambda e: e.dma_start(out=V("FCONST"), in_=fconst[:, :]))
    consts_sy = dS.n
    nc.const_aps.register(F32, float(EPS), V("FCONST", a=0, n=1))
    nc.const_aps.register(F32, float(-M1 * SCALE), V("FCONST", a=1, n=1))
    nc.const_aps.register(F32, float(-M2 * SCALE), V("FCONST", a=2, n=1))
    I("gp", dG0, 16, lambda e: e.dma_start(out=V("COS"), in_=cost[:, :]))
    I("gp", dG0, 16, lambda e: e.dma_start(out=V("SIN"), in_=sint[:, :]))
    I("gp", dG0, 16, lambda e: e.dma_start(out=V("ROT"), in_=rot_t[:, :]))
    I("gp", dG0, 16, lambda e: e.dma_start(
        out=V("ONEB", rows=(0, 1), n=128), in_=ones_r[:, :]))
    I("gp", dG0, 16, lambda e: e.dma_start(
        out=V("OCB"), in_=ones_r.rearrange("o p -> p o")))
    I("gp", dG0, 16, lambda e: e.dma_start(out=V("BOC", rows=(0, 1)), in_=bo_b[:, :]))
    I("gp", dG0, 16, lambda e: e.dma_start(out=V("LW1", rows=(0, 1)), in_=ln1w[:, :]))
    I("gp", dG0, 16, lambda e: e.dma_start(out=V("LB1", rows=(0, 1)), in_=ln1b[:, :]))
    I("gp", dG0, 16, lambda e: e.dma_start(out=V("WG"), in_=wg_t[:, :]))
    consts_gp = dG0.n
    W("te", dS, consts_sy)
    W("te", dG0, consts_gp)
    W("ve", dS, consts_sy)
    W("ve", dG0, consts_gp)
    I("ve", vv, 1, lambda e: e.tensor_tensor(
        V("WGLO"), V("WGF"), V("WG"), ALU.subtract))
    for nm in ("BOC", "LW1", "LB1"):
        for h2 in range(4):
            I("te", mm, 1, lambda e, h2=h2, nm=nm: e.matmul(
                pbk[0][:, :], V("ONEB", rows=(0, 1), n=128),
                V(nm, a=h2 * 512, n=512, rows=(0, 1)), start=True, stop=True))
            W("sc", mm)
            I("sc", ex, 1, lambda e, h2=h2, nm=nm: e.copy(
                V(nm, a=h2 * 512, n=512), pbk[0][:, :]))
            W("te", ex)
    barrier()

    def stats_chain(src_fn, par=0):
        def st(a, n):
            return V("ST", a=32 * par + a, n=n)
        for i in range(4):
            I("ve", vv, 1, lambda e, i=i: e.bn_stats(
                st(6 * i, 6), src_fn()[:, 512 * i:512 * (i + 1)]))
        I("ve", vv, 1, lambda e: e.bn_aggr(st(24, 2), st(0, 24)))
        W("sc", vv)
        I("sc", ex, 1, lambda e: e.activation(
            st(26, 1), st(25, 1), AF.Sqrt, bias=EPS, scale=1.0))
        W("ve", ex)
        I("ve", vv, 1, lambda e: e.reciprocal(st(27, 1), st(26, 1)))
        I("ve", vv, 1, lambda e: e.scalar_tensor_tensor(
            st(28, 1), st(24, 1), -1.0, st(27, 1), ALU.mult, ALU.mult))
        return (lambda: st(27, 1)), (lambda: st(28, 1))

    def rope_emit(kb_ap, cos_a, out_ap):
        W("te", vv)
        W("te", ex)
        I("te", mm, 1, lambda e, kb=kb_ap: e.matmul(
            pbk[4][:, :], V("ROT"), kb, start=True, stop=True))
        rq_mm = mm.n
        cos = V("COS", a=cos_a, n=512)
        sin = V("SIN", a=cos_a, n=512)
        t1 = V("RT1")
        t2 = V("RT2")
        I("ve", vv, 1, lambda e, kb=kb_ap: e.tensor_tensor(t1, kb, cos, ALU.mult))
        W("ve", mm, rq_mm)
        I("ve", vv, 1, lambda e: e.tensor_tensor(t2, pbk[4][:, :], sin, ALU.mult))
        I("ve", vv, 1, lambda e, o=out_ap: e.tensor_tensor(o, t1, t2, ALU.add))

    attn_state = {"blk": 0, "stt": [None, None], "zbc": None, "rz": 0,
                  "zln": [None, None]}

    def emit_attn(s, ext, kf_ap, qf_ap, vs_ap, bias_imm, out_sb, dm="DM1",
                  pm="PM1"):
        par = attn_state["blk"] % 2
        attn_state["blk"] += 1
        AVB = pbk[2] if par == 0 else pbk[4]
        ZB_ = pbk[3] if par == 0 else pbk[6]
        SB3 = (pbk[0], pbk[1], pbk[5])
        M0 = ext - 8
        W("te", vv, attn_state["stt"][par])
        W("te", ex, attn_state["zln"][par])
        exp_base = ex.n
        pt_ready = {}

        def emit_av(c):
            sb = c % 3
            kind, cnt = pt_ready[c]
            W("te", ex if kind == "ex" else vv, cnt)
            I("te", mm, 1, lambda e, c=c, sb=sb, last=(c == ext - 1): e.matmul(
                AVB[:, :], vs_ap[:, c * 128:(c + 1) * 128],
                V("PT", a=sb * 512, n=512), start=(c == 0), stop=last))
            I("te", mm, 1, lambda e, c=c, sb=sb, last=(c == ext - 1): e.matmul(
                ZB_[0:1, :], V("OCB"), V("PT", a=sb * 512, n=512),
                start=(c == 0), stop=last))

        for c in range(ext + 3):
            if c >= 3:
                emit_av(c - 3)
            if c >= ext:
                continue
            sb = c % 3
            if c >= 3:
                W("te", ex, exp_base + (c - 2))
            I("te", mm, 1, lambda e, c=c, sb=sb: e.matmul(
                SB3[sb][:, :], kf_ap[:, c * 128:(c + 1) * 128], qf_ap,
                start=True, stop=True))
            mm_after_s = mm.n
            ci = c - M0
            diag = ci >= 0 and ((s % 2 == 0) == (ci < 4))
            partner = ci >= 0 and not diag
            if diag:
                di = ci if ci < 4 else ci - 4
                W("ve", mm, mm_after_s)
                I("ve", vv, 1, lambda e, sb=sb, di=di: e.tensor_tensor(
                    SB3[sb][:, :], SB3[sb][:, :], V(dm, a=di * 512, n=512),
                    ALU.add))
                W("sc", vv)
            else:
                W("sc", mm, mm_after_s)
            I("sc", ex, 1, lambda e, sb=sb: e.activation(
                V("PT", a=sb * 512, n=512), SB3[sb][:, :], AF.Exp,
                bias=bias_imm, scale=SCALE), chain=False)
            if partner:
                W("ve", ex)
                I("ve", vv, 1, lambda e, sb=sb, s=s: e.tensor_scalar_mul(
                    V("PT", a=sb * 512, n=512), V("PT", a=sb * 512, n=512),
                    V("PMUL", a=s, n=1)))
                pt_ready[c] = ("ve", vv.n)
            else:
                pt_ready[c] = ("ex", ex.n)
        z_done = mm.n

        rzpar = attn_state["rz"] % 2
        attn_state["rz"] += 1
        zt = V("RZ", a=rzpar * 1024, n=512, rows=(0, 1))
        rz = V("RZ", a=rzpar * 1024 + 512, n=512, rows=(0, 1))
        W("sc", mm, z_done)
        I("sc", ex, 1, lambda e, zt=zt, Z=ZB_: e.activation(
            zt, Z[0:1, :], AF.Ln, bias=0.0, scale=1.0), chain=False)
        I("sc", ex, 1, lambda e, zt=zt, rz=rz: e.activation(
            rz, zt, AF.Exp, bias=0.0, scale=-1.0))
        attn_state["zln"][par] = ex.n
        recip_done = ex.n
        avn = V("AVN", a=par * 512, n=512)
        W("ve", mm, z_done)
        I("ve", vv, 1, lambda e, avn=avn, A=AVB: e.tensor_copy(avn, A[:, :]),
          chain=False)
        attn_state["stt"][par] = vv.n

        ptf = pt_bf[:, 0:1024].bitcast(F32)

        def epilogue():
            W("te", ex, attn_state["zbc"])
            W("te", ex, recip_done)
            I("te", mm, 1, lambda e, rz=rz, ptf=ptf: e.matmul(
                ptf, V("ONER", rows=(0, 1), n=128), rz,
                start=True, stop=True))
            W("sc", mm)
            zb = V("ZB", a=(par % 2) * 512, n=512)
            I("sc", ex, 1, lambda e, zb=zb, ptf=ptf: e.copy(zb, ptf))
            attn_state["zbc"] = ex.n
            W("ve", ex)
            I("ve", vv, 1, lambda e, zb=zb, o=out_sb, avn=avn:
                e.scalar_tensor_tensor(o, avn, 1.0, zb, ALU.mult, ALU.mult))
        return epilogue

    x0_loads = [0, 0]
    x0q_loads = [0, 0, 0, 0]
    n1q_free = [None, None, None, None]
    n1t_store = [None, None, None, None]
    dXq = (dX[0], dX[1], sems["dX2"], sems["dX3"])
    dNq = (dN[0], dN[1], sems["dAo0"], sems["dAo1"])
    for tt in range(NT):
        pp = tt % 4
        W("sy", ex, n1q_free[pp])
        I("sy", dXq[pp], 16, lambda e, tt=tt, pp=pp: e.dma_start(
            out=V("X0Q", a=pp * D, n=D), in_=x0p[tt * 128:(tt + 1) * 128, :]))
        x0q_loads[pp] += 1
        W("ve", dXq[pp], 16 * x0q_loads[pp])
        rstd, negmur = stats_chain(lambda pp=pp: V("X0Q", a=pp * D, n=D),
                                   par=pp % 2)
        W("sc", vv)
        W("sc", dNq[pp], n1t_store[pp])
        I("sc", ex, 1, lambda e, pp=pp, r=rstd, nm=negmur: e.activation(
            V("N1T", a=pp * D, n=D), V("X0Q", a=pp * D, n=D), AF.Identity,
            bias=nm(), scale=r()))
        n1q_free[pp] = ex.n
        W("sy", ex, ex.n)
        I("sy", dNq[pp], 16, lambda e, tt=tt, pp=pp: e.dma_start(
            out=n1_d[tt * 128:(tt + 1) * 128, :], in_=V("N1T", a=pp * D, n=D)))
        n1t_store[pp] = dNq[pp].n
    barrier()

    kv_loads = [0, 0]
    head_free = [None, None]
    ao_store = [None, None]
    pend_epi = [None]
    pend_store = [lambda: None]
    blk = 0
    for h in range(DBG_H1 if DBG_H1 else H):
        pp = h % 2
        W("sy", mm, head_free[pp])
        I("sy", dK[pp], 16, lambda e, h=h, pp=pp: e.dma_start(
            out=V("K1F", a=pp * L, n=L), in_=n1_d[:, h * 128:(h + 1) * 128],
            transpose=True))
        I("sy", dK[pp], 16, lambda e, h=h, pp=pp: e.dma_start(
            out=V("V1S", a=pp * L, n=L).rearrange("p (kc i) -> p kc i", kc=32),
            in_=n1_d[:, h * 128:(h + 1) * 128].rearrange("(kc p) i -> kc p i", p=128).transpose([1, 0, 2])))
        kv_loads[pp] += 2
        W("te", dK[pp], 16 * kv_loads[pp])
        for s in range(NG):
            ob = blk % 2
            W("ve", dAo[ob], ao_store[ob])
            epi = emit_attn(s, EXT[s], V("K1F", a=pp * L, n=L),
                            V("K1F", a=pp * L + s * 512, n=512),
                            V("V1S", a=pp * L, n=L), -M1 * SCALE,
                            V("AO", a=ob * 512, n=512))
            if pend_epi[0] is not None:
                pend_epi[0]()
                pend_store[0]()
            pend_epi[0] = epi

            def mk_store(h=h, s=s, ob=ob):
                def st():
                    W("sy", vv)
                    I("sy", dAo[ob], 16, lambda e: e.dma_start(
                        out=a1_d[h * 128:(h + 1) * 128, s * 512:(s + 1) * 512],
                        in_=V("AO", a=ob * 512, n=512)))
                    ao_store[ob] = dAo[ob].n
                return st
            pend_store[0] = mk_store()
            blk += 1
        head_free[pp] = mm.n
    pend_epi[0]()
    pend_store[0]()
    pend_epi[0] = None
    barrier()

    a1_loads = [0, 0]
    x0t_free = [None, None]
    x1t_free = [None, None]
    hsb_store = [None, None]
    ln1_store = [None, None]
    x1_stores = [None, None]
    kro_store = [None, None]
    kt_free = [None, None]
    wband_free = [None, None, None]
    wband_mm = [None, None, None]
    evac_ex = [None, None, None, None]

    def emit_tile(tt):
        pp = tt % 2
        W("sy", vv, x0t_free[pp])
        I("sy", dX[pp], 16, lambda e, tt=tt, pp=pp: e.dma_start(
            out=V("X0T", a=pp * D, n=D), in_=x0p[tt * 128:(tt + 1) * 128, :]))
        x0_loads[pp] += 1
        I("sy", dA[pp], 16, lambda e, tt=tt, pp=pp: e.dma_start(
            out=V("A1T", a=pp * D, n=D), in_=a1_d[:, tt * 128:(tt + 1) * 128],
            transpose=True))
        a1_loads[pp] += 1
        W("ve", dX[pp], dX[pp].n)
        W("ve", dA[pp], dA[pp].n)
        W("ve", vv, x1t_free[pp])
        W("ve", dSt[pp], x1_stores[pp])
        I("ve", vv, 1, lambda e, pp=pp: e.tensor_tensor(
            V("X1T", a=pp * D, n=D), V("X0T", a=pp * D, n=D),
            V("A1T", a=pp * D, n=D), ALU.add))
        x0t_free[pp] = vv.n
        own = (tt // 4) % 2 == 1
        if own:
            grp = tt // 4
            j = (grp - 1) // 2
            orow = j * 512 + (tt % 4) * 128
            W("sy", vv)
            I("sy", dSt[pp], 16, lambda e, orow=orow, pp=pp: e.dma_start(
                out=x1_d[orow:orow + 128, :], in_=V("X1T", a=pp * D, n=D)))
            x1_stores[pp] = dSt[pp].n
        rstd, negmur = stats_chain(lambda pp=pp: V("X1T", a=pp * D, n=D), par=pp)
        W("ve", sems["gpv"], sems["gpv"].n)
        I("ve", vv, 1, lambda e, pp=pp, r=rstd, nm=negmur: e.tensor_scalar(
            V("HST"), V("X1T", a=pp * D, n=D), r(), nm(), ALU.mult, ALU.add))
        x1t_free[pp] = vv.n
        W("gp", vv)
        W("gp", dN[pp], hsb_store[pp])
        I("gp", sems["gpv"], 1, lambda e, pp=pp: e.tensor_tensor(
            V("HSB", a=pp * D, n=D), V("HST"), V("BOC"), ALU.add))
        W("sy", sems["gpv"], sems["gpv"].n)
        I("sy", dN[pp], 16, lambda e, tt=tt, pp=pp: e.dma_start(
            out=hsbo_d[tt * 128:(tt + 1) * 128, :], in_=V("HSB", a=pp * D, n=D)))
        hsb_store[pp] = dN[pp].n
        sb0 = 32 * pp
        I("ve", vv, 1, lambda e, sb0=sb0: e.tensor_scalar(
            V("ST", a=sb0, n=1), V("ST", a=sb0 + 25, n=1), 1.0 + EPS, EPS * EPS,
            ALU.mult, ALU.add))
        W("sc", vv)
        I("sc", ex, 1, lambda e, sb0=sb0: e.activation(
            V("ST", a=sb0 + 1, n=1), V("ST", a=sb0, n=1), AF.Sqrt,
            bias=0.0, scale=1.0))
        W("ve", ex)
        I("ve", vv, 1, lambda e, sb0=sb0: e.reciprocal(
            V("ST", a=sb0 + 2, n=1), V("ST", a=sb0 + 1, n=1)))
        I("ve", vv, 1, lambda e, sb0=sb0: e.tensor_tensor(
            V("ST", a=sb0 + 3, n=1), V("ST", a=sb0 + 26, n=1),
            V("ST", a=sb0 + 2, n=1), ALU.mult))
        I("ve", vv, 1, lambda e, sb0=sb0: e.tensor_scalar(
            V("TMP1"), V("HST"), V("ST", a=sb0 + 3, n=1), None, ALU.mult))
        W("gp", vv)
        I("gp", sems["gpv"], 1, lambda e: e.tensor_tensor(
            V("TMP2"), V("TMP1"), V("LW1"), ALU.mult))
        W("ve", sems["gpv"], sems["gpv"].n)
        W("ve", dLn[pp], ln1_store[pp])
        I("ve", vv, 1, lambda e, pp=pp: e.tensor_tensor(
            V("LN1T", a=pp * D, n=D), V("TMP2"), V("LB1"), ALU.add))
        W("sy", vv)
        I("sy", dLn[pp], 16, lambda e, tt=tt, pp=pp: e.dma_start(
            out=ln1_d[tt * 128:(tt + 1) * 128, :], in_=V("LN1T", a=pp * D, n=D)))
        ln1_store[pp] = dLn[pp].n

    def emit_l1f_loads(half):
        for fc in range(16):
            I("sy", dS, 16, lambda e, fc=fc, half=half: e.dma_start(
                out=V("L1F", a=fc * OWN, n=OWN),
                in_=ln1_d[half * OWN:(half + 1) * OWN, fc * 128:(fc + 1) * 128],
                transpose=True))
        W("te", dS, dS.n)

    def emit_proj_oc(wt, bias_name, tgs, rope, dst, half, oc):
        pp2 = oc % 3
        W("gp", ex, wband_free[pp2])
        W("gp", mm, wband_mm[pp2])
        I("gp", dGp[pp2], 16, lambda e, oc=oc, pp2=pp2, wt=wt: e.dma_start(
            out=V("WB", a=pp2 * D, n=D),
            in_=wt[oc].rearrange("p e i -> p (e i)")))
        W("te", dGp[pp2], dGp[pp2].n)
        for ec in range(16):
            for bi, tg in enumerate(tgs):
                if ec == 0:
                    W("te", ex, evac_ex[bi])
                I("te", mm, 1, lambda e, ec=ec, bi=bi, tg=tg, pp2=pp2:
                    e.matmul(pbk[bi][:, :],
                             V("WB", a=pp2 * D + ec * 128, n=128),
                             V("L1F", a=ec * OWN + tg * 512, n=512),
                             start=(ec == 0), stop=(ec == 15)))
        mm_done = mm.n
        wband_mm[pp2] = mm.n
        for bi, tg in enumerate(tgs):
            kb = (oc * len(tgs) + bi) % 2
            W("sc", mm, mm_done)
            evac_mark = True
            if rope:
                W("sc", vv, kt_free[kb])
                I("sc", ex, 1, lambda e, bi=bi, kb=kb, oc=oc: e.activation(
                    V("KB", a=kb * 512, n=512), pbk[bi][:, :], AF.Identity,
                    bias=V(bias_name, a=oc, n=1), scale=1.0))
                evac_ex[bi] = ex.n
                W("ve", ex)
                W("ve", dKr[kb], kro_store[kb])
                rope_emit(V("KB", a=kb * 512, n=512),
                          half * OWN + tg * 512,
                          V("KRO", a=kb * 512, n=512))
                kt_free[kb] = vv.n
                W("sy", vv)
            else:
                W("sc", dKr[kb], kro_store[kb])
                I("sc", ex, 1, lambda e, bi=bi, kb=kb, oc=oc: e.activation(
                    V("KRO", a=kb * 512, n=512), pbk[bi][:, :], AF.Identity,
                    bias=V(bias_name, a=oc, n=1), scale=1.0))
                evac_ex[bi] = ex.n
                W("sy", ex)
            if dst is q2_d:
                col = (2 * half + (1 if tg == 3 else 0)) * 512
            else:
                col = half * OWN + tg * 512
            I("sy", dKr[kb], 16, lambda e, oc=oc, col=col, kb=kb, dst=dst:
                e.dma_start(
                    out=dst[oc * 128:(oc + 1) * 128, col:col + 512],
                    in_=V("KRO", a=kb * 512, n=512)))
            kro_store[kb] = dKr[kb].n
        wband_free[pp2] = ex.n

    def proj_units(half):
        sel = DBG_PROJ if DBG_PROJ is not None else "kvq"
        units = []
        noc = DBG_OC if DBG_OC else 16
        if "k" in sel:
            for oc in range(noc):
                units.append(lambda oc=oc, h=half: emit_proj_oc(
                    wk_t, "BK", [0, 1, 2, 3], True, k2_d, h, oc))
        if "v" in sel:
            for oc in range(noc):
                units.append(lambda oc=oc, h=half: emit_proj_oc(
                    wv_t, "BV", [0, 1, 2, 3], False, v2_d, h, oc))
        if "q" in sel:
            for oc in range(noc):
                units.append(lambda oc=oc, h=half: emit_proj_oc(
                    wq_t, "BQ", [1, 3], True, q2_d, h, oc))
        return units

    nt3 = DBG_T16 if DBG_T16 else 16
    for t16 in range(nt3):
        emit_tile(t16)
    barrier()
    emit_l1f_loads(0)
    units0 = proj_units(0)
    ti = 0
    for i, u in enumerate(units0):
        u()
        if i % 3 == 2 and ti < nt3:
            emit_tile(16 + ti)
            ti += 1
    while ti < nt3:
        emit_tile(16 + ti)
        ti += 1
    barrier()
    emit_l1f_loads(1)
    for u in proj_units(1):
        u()
    barrier()

    I("sy", dS, 16, lambda e: e.dma_start(
        out=V("DM2").rearrange("p (c q) -> p c q", c=4), in_=dmask.transpose([1, 0, 2])))
    I("sy", dS, 16, lambda e: e.dma_start(
        out=V("PM2").rearrange("p (c q) -> p c q", c=NG), in_=pmask.transpose([1, 0, 2])))
    dm2_ready = dS.n
    W("te", dS, dm2_ready)
    W("ve", dS, dm2_ready)
    kv2_loads = [0, 0]
    head_free2 = [None, None]
    for h in range(DBG_H2 if DBG_H2 else H):
        pp = h % 2
        W("sy", mm, head_free2[pp])
        I("sy", dK[pp], 16, lambda e, h=h, pp=pp: e.dma_start(
            out=V("K2F", a=pp * L, n=L), in_=k2_d[h * 128:(h + 1) * 128, :]))
        I("sy", dK[pp], 16, lambda e, h=h, pp=pp: e.dma_start(
            out=V("V2F", a=pp * L, n=L), in_=v2_d[h * 128:(h + 1) * 128, :]))
        I("sy", dK[pp], 16, lambda e, h=h, pp=pp: e.dma_start(
            out=V("Q2F", a=pp * OWN, n=OWN), in_=q2_d[h * 128:(h + 1) * 128, :]))
        kv2_loads[pp] += 3
        W("te", dK[pp], 16 * (kv_loads[pp] + kv2_loads[pp]))
        W("te", ex, attn_state["zbc"])
        pb0_bf = pbk[0][:, 0:64].bitcast(BF16)
        cpv = {}
        for kc in range(32):
            if kc >= 2:
                W("te", vv, cpv[kc - 2])
            I("te", mm, 1, lambda e, pp=pp, kc=kc: e.transpose(
                pt_bf[:, 0:128] if kc % 2 == 0 else pb0_bf,
                V("V2F", a=pp * L + kc * 128, n=128), V("IDN")))
            W("ve", mm, mm.n)
            I("ve", vv, 1, lambda e, pp=pp, kc=kc: e.tensor_copy(
                V("V2S", a=pp * L + kc * 128, n=128),
                pt_bf[:, 0:128] if kc % 2 == 0 else pb0_bf),
              chain=False)
            cpv[kc] = vv.n
        v2s_ready = vv.n
        W("te", vv, v2s_ready)
        for j, s in enumerate(OWN_SLOTS):
            epi = emit_attn(s, EXT[s], V("K2F", a=pp * L, n=L),
                            V("Q2F", a=pp * OWN + j * 512, n=512),
                            V("V2S", a=pp * L, n=L), -M2 * SCALE,
                            V("A2P", a=h * OWN + j * 512, n=512),
                            dm="DM2", pm="PM2")
            if pend_epi[0] is not None:
                pend_epi[0]()
            pend_epi[0] = epi
        head_free2[pp] = mm.n
    pend_epi[0]()
    pend_epi[0] = None
    barrier()

    for dc in range(16):
        I("gp", sems["dWO"], 16, lambda e, dc=dc: e.dma_start(
            out=V("WO", a=dc * D, n=D), in_=wo_t[dc * 128:(dc + 1) * 128, :]))
    I("gp", dG0, 16, lambda e: e.dma_start(out=V("LW2", rows=(0, 1)), in_=ln2w[:, :]))
    I("gp", dG0, 16, lambda e: e.dma_start(out=V("LB2", rows=(0, 1)), in_=ln2b[:, :]))
    W("te", dG0, dG0.n)
    for nm in ("LW2", "LB2"):
        for h2 in range(4):
            I("te", mm, 1, lambda e, h2=h2, nm=nm: e.matmul(
                pbk[0][:, :], V("ONEB", rows=(0, 1), n=128),
                V(nm, a=h2 * 512, n=512, rows=(0, 1)), start=True, stop=True))
            W("sc", mm)
            I("sc", ex, 1, lambda e, h2=h2, nm=nm: e.copy(
                V(nm, a=h2 * 512, n=512), pbk[0][:, :]))
            W("te", ex)
    wo_ready = sems["dWO"].n
    W("te", sems["dWO"], wo_ready)
    hsb_loads = [0, 0]
    ln2_store = [None, None]
    hsbt_free = [None, None]
    ht_free = [None, None]
    h_stores = [None, None]
    for ot in range(DBG_OT if DBG_OT else ONT):
        pp = ot % 2
        gt = OT2G[ot]
        W("sy", vv, hsbt_free[pp])
        I("sy", dA[pp], 16, lambda e, gt=gt, pp=pp: e.dma_start(
            out=V("HSBT", a=pp * D, n=D), in_=hsbo_d[gt * 128:(gt + 1) * 128, :]))
        hsb_loads[pp] += 1
        for dc in range(16):
            for eg in range(4):
                I("te", mm, 1, lambda e, dc=dc, eg=eg, ot=ot: e.matmul(
                    pbk[eg][:, :], V("A2P", a=dc * D + ot * 128, n=128),
                    V("WO", a=dc * D + eg * 512, n=512),
                    start=(dc == 0), stop=(dc == 15)))
        mm_done = mm.n
        W("ve", mm, mm_done)
        W("ve", dA[pp], 16 * (a1_loads[pp] + hsb_loads[pp]))
        W("ve", ex, ht_free[pp])
        W("ve", dSt[pp], h_stores[pp])
        for eg in range(4):
            I("ve", vv, 1, lambda e, eg=eg, pp=pp: e.tensor_tensor(
                V("HT", a=pp * D + eg * 512, n=512), pbk[eg][:, :],
                V("HSBT", a=pp * D + eg * 512, n=512), ALU.add))
        hsbt_free[pp] = vv.n
        W("te", vv)
        W("sy", vv)
        I("sy", dSt[pp], 16, lambda e, ot=ot, pp=pp: e.dma_start(
            out=h_d[ot * 128:(ot + 1) * 128, :], in_=V("HT", a=pp * D, n=D)))
        h_stores[pp] = dSt[pp].n
        rstd3, negmur3 = stats_chain(lambda pp=pp: V("HT", a=pp * D, n=D), par=pp)
        W("sc", vv)
        W("sc", dN[pp], ln2_store[pp])
        I("sc", ex, 1, lambda e, pp=pp, r=rstd3, nm=negmur3: e.activation(
            V("TMP1"), V("HT", a=pp * D, n=D), AF.Identity, bias=nm(), scale=r()))
        ht_free[pp] = ex.n
        W("ve", ex)
        I("ve", vv, 1, lambda e: e.tensor_tensor(
            V("TMP2"), V("TMP1"), V("LW2"), ALU.mult))
        I("ve", vv, 1, lambda e: e.tensor_tensor(
            V("TMP1"), V("TMP2"), V("LB2"), ALU.add))
        I("ve", vv, 1, lambda e, pp=pp: e.tensor_copy(
            V("LN2T", a=pp * D, n=D), V("TMP1")))
        I("ve", vv, 1, lambda e, pp=pp: e.tensor_tensor(
            V("LN2LO", a=pp * D, n=D), V("TMP1"), V("LN2T", a=pp * D, n=D),
            ALU.subtract))
        W("sy", vv)
        I("sy", dN[pp], 16, lambda e, ot=ot, pp=pp: e.dma_start(
            out=ln2h_d[ot * 128:(ot + 1) * 128, :], in_=V("LN2T", a=pp * D, n=D)))
        I("sy", dN[pp], 16, lambda e, ot=ot, pp=pp: e.dma_start(
            out=ln2lo_d[ot * 128:(ot + 1) * 128, :], in_=V("LN2LO", a=pp * D, n=D)))
        ln2_store[pp] = dN[pp].n
    barrier()

    for sl in range(DBG_SL if DBG_SL else 4):
        pp = sl % 2
        if sl >= 1:
            W("sy", mm)
        for fc in range(16):
            I("sy", dK[pp], 16, lambda e, sl=sl, fc=fc, pp=pp: e.dma_start(
                out=V("L2F", a=pp * 16 * 512 + fc * 512, n=512),
                in_=ln2h_d[sl * 512:(sl + 1) * 512, fc * 128:(fc + 1) * 128],
                transpose=True))
            I("sy", dK[pp], 16, lambda e, sl=sl, fc=fc, pp=pp: e.dma_start(
                out=V("L2FLO", a=fc * 512, n=512),
                in_=ln2lo_d[sl * 512:(sl + 1) * 512, fc * 128:(fc + 1) * 128],
                transpose=True))
        kv2_loads[pp] += 32
        W("te", dK[pp], 16 * (kv_loads[pp] + kv2_loads[pp]))
        for twi in range(4):
            tt = sl * 4 + twi
            W("te", ex)
            for fc in range(16):
                I("te", mm, 1, lambda e, fc=fc, twi=twi, pp=pp: e.matmul(
                    pbk[4][:, 0:4],
                    V("L2F", a=pp * 16 * 512 + fc * 512 + twi * 128, n=128),
                    V("WG", a=fc * 4, n=4), start=(fc == 0), stop=False))
                I("te", mm, 1, lambda e, fc=fc, twi=twi, pp=pp: e.matmul(
                    pbk[4][:, 0:4],
                    V("L2F", a=pp * 16 * 512 + fc * 512 + twi * 128, n=128),
                    V("WGLO", a=fc * 4, n=4), start=False, stop=False))
                I("te", mm, 1, lambda e, fc=fc, twi=twi, pp=pp: e.matmul(
                    pbk[4][:, 0:4],
                    V("L2FLO", a=fc * 512 + twi * 128, n=128),
                    V("WG", a=fc * 4, n=4), start=False,
                    stop=(fc == 15)))
            gmm = mm.n
            W("ve", mm, gmm)
            I("ve", vv, 1, lambda e: e.reduce_max(
                V("ST", a=32, n=1), pbk[4][:, 0:4], axis=AX.X))
            I("ve", vv, 1, lambda e: e.tensor_scalar(
                V("ST", a=33, n=1), V("ST", a=32, n=1), -1.0, None, ALU.mult))
            W("sc", vv)
            I("sc", ex, 1, lambda e: e.activation(
                V("GE4", a=0, n=4), pbk[4][:, 0:4], AF.Exp,
                bias=V("ST", a=33, n=1), scale=1.0, accum_out=V("ST", a=34, n=1)))
            W("ve", ex)
            I("ve", vv, 1, lambda e: e.reciprocal(V("ST", a=35, n=1), V("ST", a=34, n=1)))
            I("ve", vv, 1, lambda e: e.tensor_scalar(
                V("GRW", a=0, n=4), V("GE4", a=0, n=4), V("ST", a=35, n=1),
                None, ALU.mult))
            I("ve", vv, 1, lambda e: e.reduce_max(
                V("ST", a=36, n=1), V("GRW", a=0, n=4), axis=AX.X))
            I("ve", vv, 1, lambda e: e.tensor_scalar(
                V("GRW", a=4, n=4), V("GRW", a=0, n=4), V("ST", a=36, n=1),
                None, ALU.is_lt))
            I("ve", vv, 1, lambda e: e.tensor_tensor(
                V("GRW", a=8, n=4), V("GRW", a=0, n=4), V("GRW", a=4, n=4), ALU.mult))
            I("ve", vv, 1, lambda e: e.reduce_max(
                V("ST", a=37, n=1), V("GRW", a=8, n=4), axis=AX.X))
            I("ve", vv, 1, lambda e: e.tensor_scalar(
                V("GRW", a=4, n=4), V("GRW", a=0, n=4), V("ST", a=37, n=1),
                None, ALU.is_ge))
            W("sc", vv)
            I("sc", ex, 1, lambda e: e.activation(
                V("ST", a=38, n=1), V("ST", a=36, n=1), AF.Exp, bias=0.0, scale=1.0))
            I("sc", ex, 1, lambda e: e.activation(
                V("ST", a=39, n=1), V("ST", a=37, n=1), AF.Exp, bias=0.0, scale=1.0))
            I("sc", ex, 1, lambda e: e.activation(
                V("GE4", a=4, n=4), V("GRW", a=0, n=4), AF.Exp, bias=0.0, scale=1.0))
            W("ve", ex)
            I("ve", vv, 1, lambda e: e.tensor_tensor(
                V("ST", a=40, n=1), V("ST", a=38, n=1), V("ST", a=39, n=1), ALU.add))
            I("ve", vv, 1, lambda e: e.reciprocal(V("ST", a=41, n=1), V("ST", a=40, n=1)))
            I("ve", vv, 1, lambda e: e.tensor_tensor(
                V("GRW", a=12, n=4), V("GE4", a=4, n=4), V("GRW", a=4, n=4), ALU.mult))
            I("ve", vv, 1, lambda e: e.tensor_scalar(
                V("CMB"), V("GRW", a=12, n=4), V("ST", a=41, n=1), None, ALU.mult))
            W("te", vv)
            for e4 in range(NE):
                I("te", mm, 1, lambda e, e4=e4: e.transpose(
                    pt_bf[0:1, e4 * 128:(e4 + 1) * 128], V("CMB", a=e4, n=1),
                    V("IDN")))
            W("ve", mm)
            for e4 in range(NE):
                I("ve", vv, 1, lambda e, e4=e4, tt=tt: e.tensor_copy(
                    V("CROW", a=e4 * OWN + tt * 128, n=128, rows=(0, 1)),
                    pt_bf[0:1, e4 * 128:(e4 + 1) * 128]))
            W("te", vv)
    barrier()

    w1b_free = [None, None]
    w1b_mm = [None, None]
    gelu_ex = [None, None]
    mst_free = [None, None]
    m_prev_expert = [0, 0]
    for e4 in range(DBG_E if DBG_E else NE):
        for ic in range(16):
            I("gp", sems["dWE2"], 16, lambda e, e4=e4, ic=ic: e.dma_start(
                out=V("WE2", a=ic * D, n=D),
                in_=we2_t[e4, ic * 128:(ic + 1) * 128, :]))
        I("gp", sems["dWE2"], 16, lambda e, e4=e4: e.dma_start(
            out=V("BE2", rows=(0, 1)), in_=be2_t[e4:e4 + 1, :]))
        we2_ready = sems["dWE2"].n
        if e4 > 0:
            W("gp", dM[0], m_prev_expert[0])
            W("gp", dM[1], m_prev_expert[1])
        for sl in range(DBG_SL if DBG_SL else 4):
            pp = sl % 2
            if e4 > 0 or sl >= 2:
                W("sy", mm)
            for fc in range(16):
                I("sy", dK[pp], 16, lambda e, sl=sl, fc=fc, pp=pp: e.dma_start(
                    out=V("L2F", a=pp * 16 * 512 + fc * 512, n=512),
                    in_=ln2h_d[sl * 512:(sl + 1) * 512, fc * 128:(fc + 1) * 128],
                    transpose=True))
            kv2_loads[pp] += 16
            l2f_ready = 16 * (kv_loads[pp] + kv2_loads[pp])
            W("te", sems["dWE2"], we2_ready)
            W("te", vv)
            I("te", mm, 1, lambda e, e4=e4, sl=sl: e.matmul(
                pbk[5][:, :], V("ONEB", rows=(0, 1), n=128),
                V("CROW", a=e4 * OWN + sl * 512, n=512, rows=(0, 1)),
                start=True, stop=True))
            W("sc", mm)
            I("sc", ex, 1, lambda e: e.copy(V("CB"), pbk[5][:, :]))
            cb_ready = ex.n
            for ic in range(16):
                pp2 = ic % 2
                bi = ic % 2
                W("gp", ex, w1b_free[pp2])
                W("gp", mm, w1b_mm[pp2])
                I("gp", dGp[pp2], 16, lambda e, e4=e4, ic=ic, pp2=pp2, sl=sl: e.dma_start(
                    out=V("W1B", a=pp2 * D, n=D),
                    in_=we1_t[e4, ic].rearrange("p f i -> p (f i)")).annotate(
                        f"w1b e{e4} sl{sl} ic{ic} pp{pp2}"))
                W("te", dGp[pp2], dGp[pp2].n)
                W("te", dK[pp], l2f_ready)
                W("te", ex, gelu_ex[bi])
                for fc in range(16):
                    I("te", mm, 1, lambda e, fc=fc, bi=bi, pp=pp, pp2=pp2,
                      e4=e4, sl=sl, ic=ic: e.matmul(
                        pbk[bi][:, :], V("W1B", a=pp2 * D + fc * 128, n=128),
                        V("L2F", a=pp * 16 * 512 + fc * 512, n=512),
                        start=(fc == 0), stop=(fc == 15)).annotate(
                            f"mm1 e{e4} sl{sl} ic{ic} fc{fc}"))
                w1b_mm[pp2] = mm.n
                W("sc", mm)
                W("sc", vv)
                I("sc", ex, 1, lambda e, bi=bi, ic=ic, e4=e4: e.activation(
                    V("GT"), pbk[bi][:, :], AF.Gelu,
                    bias=V("BE1", a=ic * 4 + e4, n=1), scale=1.0))
                gelu_ex[bi] = ex.n
                w1b_free[pp2] = ex.n
                W("ve", ex)
                W("ve", ex, cb_ready)
                I("ve", vv, 1, lambda e, ic=ic: e.tensor_tensor(
                    V("HMW", a=ic * 512, n=512), V("GT"), V("CB"), ALU.mult))
            hm_ready = vv.n
            W("te", vv, hm_ready)
            for twi in range(4):
                tt = sl * 4 + twi
                for ic in range(16):
                    for dg in range(4):
                        I("te", mm, 1, lambda e, ic=ic, dg=dg, twi=twi: e.matmul(
                            pbk[dg][:, :], V("HMW", a=ic * 512 + twi * 128, n=128),
                            V("WE2", a=ic * D + dg * 512, n=512),
                            start=(ic == 0), stop=False))
                for dg in range(4):
                    I("te", mm, 1, lambda e, dg=dg, e4=e4, tt=tt: e.matmul(
                        pbk[dg][:, :],
                        V("CROW", a=e4 * OWN + tt * 128, n=128, rows=(0, 1)),
                        V("BE2", a=dg * 512, n=512, rows=(0, 1)),
                        start=False, stop=True))
                m2done = mm.n
                for dg in range(4):
                    mb = (tt * 4 + dg) % 2
                    W("ve", mm, m2done)
                    W("ve", dM[mb], mst_free[mb])
                    I("ve", vv, 1, lambda e, dg=dg, mb=mb: e.tensor_copy(
                        V("MST", a=mb * 512, n=512), pbk[dg][:, :]))
                    W("gp", vv)
                    if e4 == 0:
                        I("gp", dM[mb], 16, lambda e, tt=tt, dg=dg, mb=mb: e.dma_start(
                            out=m_d[tt * 128:(tt + 1) * 128, dg * 512:(dg + 1) * 512],
                            in_=V("MST", a=mb * 512, n=512)))
                    else:
                        I("gp", dM[mb], 16, lambda e, tt=tt, dg=dg, mb=mb: e.dma_start(
                            out=m_d[tt * 128:(tt + 1) * 128, dg * 512:(dg + 1) * 512],
                            in_=V("MST", a=mb * 512, n=512), accum_op=ALU.add))
                    mst_free[mb] = dM[mb].n
                W("te", vv)
        m_prev_expert[0] = dM[0].n
        m_prev_expert[1] = dM[1].n
    barrier()

    fin_free = [None, None]
    out_store = [None, None]
    fin_loads = [0, 0]
    for ot in range(ONT):
        pp = ot % 2
        base = pp * 3 * D
        W("sy", vv, fin_free[pp])
        I("sy", dX[pp], 16, lambda e, ot=ot, base=base: e.dma_start(
            out=V("FIN", a=base, n=D), in_=x1_d[ot * 128:(ot + 1) * 128, :]))
        I("sy", dX[pp], 16, lambda e, ot=ot, base=base: e.dma_start(
            out=V("FIN", a=base + D, n=D), in_=h_d[ot * 128:(ot + 1) * 128, :]))
        I("sy", dX[pp], 16, lambda e, ot=ot, base=base: e.dma_start(
            out=V("FIN", a=base + 2 * D, n=D), in_=m_d[ot * 128:(ot + 1) * 128, :]))
        x0_loads[pp] += 3
        W("ve", dX[pp], dX[pp].n)
        W("ve", dO[pp], out_store[pp])
        I("ve", vv, 1, lambda e, base=base: e.tensor_tensor(
            V("FIN", a=base, n=D), V("FIN", a=base, n=D),
            V("FIN", a=base + D, n=D), ALU.add))
        I("ve", vv, 1, lambda e, base=base, pp=pp: e.tensor_tensor(
            V("OUTT", a=pp * D, n=D), V("FIN", a=base, n=D),
            V("FIN", a=base + 2 * D, n=D), ALU.add))
        fin_free[pp] = vv.n
        W("sy", vv)
        I("sy", dO[pp], 16, lambda e, ot=ot, pp=pp: e.dma_start(
            out=out[ot * 128:(ot + 1) * 128, :], in_=V("OUTT", a=pp * D, n=D)))
        out_store[pp] = dO[pp].n
    if tap_outs:
        barrier()
        for t, (dst, src) in tap_outs.items():
            I("sy", dO[0], 16, lambda e, dst=dst, src=src: e.dma_start(
                out=dst[:, :], in_=src[:, :]))
    W("sy", dO[0], dO[0].n)
    W("sy", dO[1], dO[1].n)

    if max_phase < 99:
        cut_ph = min(k for k in cuts if k > max_phase) if any(
            k > max_phase for k in cuts) else None
        if cut_ph is not None:
            for ek in ENGS:
                plan[ek] = plan[ek][:cuts[cut_ph][ek]]
    with nc.Block() as block:
        @block.tensor
        def _(e):
            for f in plan["te"]:
                f(e)

        @block.scalar
        def _(e):
            for f in plan["sc"]:
                f(e)

        @block.vector
        def _(e):
            for f in plan["ve"]:
                f(e)

        @block.gpsimd
        def _(e):
            for f in plan["gp"]:
                f(e)

        @block.sync
        def _(e):
            for f in plan["sy"]:
                f(e)

    es.close()
    return nc



def _host_inputs(core, args):
    b, p = core // 2, core % 2
    g = np.arange(NG)
    tg = (g ^ 1) if p == 0 else g
    tp = (tg[:, None] * G + np.arange(G)[None, :]).reshape(-1)

    x0 = np.ascontiguousarray(np.asarray(args["L0_attn_x"], np.float32)[b][tp])

    ki = np.arange(128)
    qj = np.arange(512)
    dmask = np.empty((4, 128, 512), np.float32)
    for k in range(4):
        dmask[k] = np.where((k * 128 + ki)[:, None] <= qj[None, :], 0.0, NEG)
    pmask = np.empty((NG, 128, 512), np.float32)
    for s in range(NG):
        pmask[s] = 0.0 if tg[s ^ 1] < tg[s] else NEG

    inv = 1.0 / (10000.0 ** (np.arange(0, HD, 2) / HD))
    ang = tp[None, :].astype(np.float64) * inv[:, None]
    cost = np.concatenate([np.cos(ang), np.cos(ang)], 0).astype(np.float32)
    sint = np.concatenate([np.sin(ang), np.sin(ang)], 0).astype(np.float32)
    rot = np.zeros((128, 128), np.float32)
    rot[64:, :64] = -np.eye(64)
    rot[:64, 64:] = np.eye(64)

    def wt(w):
        return np.ascontiguousarray(
            np.asarray(w, np.float32).reshape(16, 128, 16, 128).transpose(2, 1, 0, 3))

    def bt(v):
        return np.ascontiguousarray(np.asarray(v, np.float32).reshape(16, 128).T)

    we1 = np.asarray(args["L0_ffn_we1"], np.float32)
    we1_t = np.ascontiguousarray(
        we1.reshape(NE, 16, 128, 16, 128).transpose(0, 3, 2, 1, 4))
    wg = np.asarray(args["L0_ffn_w_gate"], np.float32)
    wg_t = np.ascontiguousarray(
        wg.reshape(16, 128, 4).transpose(1, 0, 2).reshape(128, 64))
    be1 = np.asarray(args["L0_ffn_be1"], np.float32)
    be1_t = np.ascontiguousarray(
        be1.reshape(NE, 16, 128).transpose(2, 1, 0).reshape(128, 64))

    return {
        "x0p": x0,
        "dmask": dmask, "pmask": pmask, "cost": cost, "sint": sint,
        "rot_t": rot.astype(BF),
        "wq_t": wt(args["L0_ffn_wq"]), "wk_t": wt(args["L0_ffn_wk"]),
        "wv_t": wt(args["L0_ffn_wv"]),
        "wo_t": np.ascontiguousarray(np.asarray(args["L0_ffn_wo"], np.float32)),
        "bq_t": bt(args["L0_ffn_bq"]), "bk_t": bt(args["L0_ffn_bk"]),
        "bv_t": bt(args["L0_ffn_bv"]),
        "bo_b": np.asarray(args["L0_ffn_bo"], np.float32).reshape(1, D),
        "ln1w": np.asarray(args["L0_ffn_ln1_w"], np.float32).reshape(1, D),
        "ln1b": np.asarray(args["L0_ffn_ln1_b"], np.float32).reshape(1, D),
        "ln2w": np.asarray(args["L0_ffn_ln2_w"], np.float32).reshape(1, D),
        "ln2b": np.asarray(args["L0_ffn_ln2_b"], np.float32).reshape(1, D),
        "wg_t": wg_t, "be1_t": be1_t,
        "be2_t": np.ascontiguousarray(np.asarray(args["L0_ffn_be2"], np.float32)),
        "we1_t": we1_t,
        "we2_t": np.ascontiguousarray(np.asarray(args["L0_ffn_we2"], np.float32)),
        "ones_r": np.ones((1, 128), np.float32),
        "fconst": np.broadcast_to(np.array(
            [EPS, -M1 * SCALE, -M2 * SCALE, 0.0], np.float32), (128, 4)).copy(),
        "pmul_in": np.broadcast_to(
            np.array([1.0 if tg[s ^ 1] < tg[s] else 0.0 for s in range(NG)],
                     np.float32), (128, NG)).copy(),
        "ident_in": np.eye(128, dtype=BF),
    }, tp


def kernel(**inputs):
    nc = build_program()
    in_maps = []
    tps = []
    for core in range(8):
        m, tp = _host_inputs(core, inputs)
        in_maps.append(m)
        tps.append(tp)
    res = run_bass_kernel_spmd(nc, in_maps, core_ids=list(range(8)))
    full = np.empty((B, L, D), np.float32)
    own_storage = np.concatenate(
        [np.arange((2 * j + 1) * G, (2 * j + 2) * G) for j in range(4)])
    for core in range(8):
        b = core // 2
        full[b, tps[core][own_storage]] = res.results[core]["out"]
    return full


# version 14
# speedup vs baseline: 1.1229x; 1.0632x over previous
import math
import contextlib
import numpy as np
import ml_dtypes
import concourse.bass as bass
import concourse.mybir as mybir
from concourse.bass_utils import run_bass_kernel_spmd

F32 = mybir.dt.float32
BF16 = mybir.dt.bfloat16
AF = mybir.ActivationFunctionType
ALU = mybir.AluOpType
AX = mybir.AxisListType
BF = ml_dtypes.bfloat16

B, L, D, H, HD, NE = 4, 4096, 2048, 16, 128, 4
G = 512
NG = L // G
NT = L // 128
OWN = L // 2
ONT = OWN // 128
SCALE = 1.0 / math.sqrt(HD)
M1, M2 = 24.0, 14.0
NEG = -1.0e9
EPS = 1e-5

EXT = [4 * (s + 2) if s % 2 == 0 else 4 * (s + 1) for s in range(NG)]
OWN_SLOTS = [1, 3, 5, 7]
OT2G = [(ot // 4) * 8 + 4 + (ot % 4) for ot in range(ONT)]

ENGS = ("te", "sc", "ve", "gp", "sy")
DBG_T16 = None
DBG_PROJ = None
DBG_OC = None
DBG_H1 = None
DBG_H2 = None
DBG_OT = None
DBG_SL = None
DBG_E = None


class Sem:
    def __init__(self, h):
        self.h = h
        self.n = 0


def build_program(taps=(), max_phase=99):
    nc = bass.Bass("TRN2", target_bir_lowering=False)
    es = contextlib.ExitStack()

    def din(name, shape, dt=F32):
        return nc.dram_tensor(name, shape, dt, kind="ExternalInput").ap()

    x0p = din("x0p", [L, D])
    dmask = din("dmask", [4, 128, 512])
    pmask = din("pmask", [NG, 128, 512])
    cost = din("cost", [128, L])
    sint = din("sint", [128, L])
    rot_t = din("rot_t", [128, 128], BF16)
    wq_t = din("wq_t", [16, 128, 16, 128])
    wk_t = din("wk_t", [16, 128, 16, 128])
    wv_t = din("wv_t", [16, 128, 16, 128])
    wo_t = din("wo_t", [D, D])
    bq_t = din("bq_t", [128, 16])
    bk_t = din("bk_t", [128, 16])
    bv_t = din("bv_t", [128, 16])
    bo_b = din("bo_b", [1, D])
    ln1w = din("ln1w", [1, D])
    ln1b = din("ln1b", [1, D])
    ln2w = din("ln2w", [1, D])
    ln2b = din("ln2b", [1, D])
    wg_t = din("wg_t", [128, 64])
    be1_t = din("be1_t", [128, 64])
    be2_t = din("be2_t", [NE, D])
    we1_t = din("we1_t", [NE, 16, 128, 16, 128])
    we2_t = din("we2_t", [NE, D, D])
    ones_r = din("ones_r", [1, 128])
    fconst = din("fconst", [128, 4])
    pmul_in = din("pmul_in", [128, NG])
    ident_in = din("ident_in", [128, 128], BF16)

    out = nc.dram_tensor("out", [OWN, D], F32, kind="ExternalOutput").ap()

    def dsc(name, shape, dt):
        return nc.dram_tensor(name, shape, dt, kind="Internal").ap()

    n1_d = dsc("n1_d", [L, D], BF16)
    a1_d = dsc("a1_d", [D, L], BF16)
    x1_d = dsc("x1_d", [OWN, D], F32)
    hsbo_d = dsc("hsbo_d", [L, D], BF16)
    ln1_d = dsc("ln1_d", [L, D], BF16)
    k2_d = dsc("k2_d", [D, L], BF16)
    v2_d = dsc("v2_d", [D, L], BF16)
    q2_d = dsc("q2_d", [D, OWN], BF16)
    h_d = dsc("h_d", [OWN, D], F32)
    ln2h_d = dsc("ln2h_d", [OWN, D], BF16)
    ln2lo_d = dsc("ln2lo_d", [OWN, D], BF16)
    m_d = dsc("m_d", [OWN, D], F32)
    crow_d = dsc("crow_d", [NE, OWN], BF16)

    tap_outs = {}
    _tap_src = {"n1": (n1_d, BF16), "a1": (a1_d, BF16), "x1": (x1_d, F32),
                "hsbo": (hsbo_d, BF16), "ln1": (ln1_d, BF16),
                "k2": (k2_d, BF16), "v2": (v2_d, BF16), "q2": (q2_d, BF16),
                "h": (h_d, F32), "ln2h": (ln2h_d, BF16), "m": (m_d, F32)}
    for t in taps:
        src, dt = _tap_src[t]
        tap_outs[t] = (nc.dram_tensor("tap_" + t, list(src.shape), dt,
                                      kind="ExternalOutput").ap(), src)

    segs = []

    def seg(name, dt, elems, p0, p1=None):
        ph = frozenset(p0) if isinstance(p0, (list, tuple, set)) else \
            frozenset(range(p0, (p1 if p1 is not None else p0) + 1))
        segs.append([name, dt, elems, ph, None])

    seg("N1T", BF16, 4 * D, 1, 1)
    seg("X0Q", F32, 4 * D, 1, 1)
    seg("K1F", BF16, 2 * L, 2, 2)
    seg("V1S", BF16, 2 * L, 2, 2)
    seg("PT", BF16, 3 * 512, (2, 6))
    seg("AVN", BF16, 2 * 512, (2, 6))
    seg("AO", BF16, 2 * 512, 2, 2)
    seg("A1T", BF16, 2 * D, (3, 4))
    seg("HSB", BF16, 2 * D, (3, 4))
    seg("LN1T", BF16, 2 * D, (3, 4))
    seg("COS", BF16, L, 0, 5)
    seg("SIN", BF16, L, 0, 5)
    seg("BOC", BF16, D, 0, 4)
    seg("LW1", BF16, D, 0, 4)
    seg("LB1", BF16, D, 0, 4)
    seg("LW2", BF16, D, 7, 7)
    seg("LB2", BF16, D, 7, 7)
    seg("L1F", BF16, 16 * OWN, (4, 5))
    seg("WB", BF16, 3 * D, (4, 5))
    seg("KRO", BF16, 2 * 512, (4, 5))
    seg("KB", BF16, 2 * 512, (4, 5))
    seg("ROT", BF16, 128, 0, 5)
    seg("K2F", BF16, 2 * L, 6, 6)
    seg("V2S", BF16, 2 * L, 6, 6)
    seg("Q2F", BF16, 2 * OWN, 6, 6)
    seg("V2F", BF16, 2 * L, 6, 6)
    seg("A2P", BF16, 16 * OWN, 6, 7)
    seg("WO", BF16, 16 * D, 7, 7)
    seg("HSBT", BF16, 2 * D, 7, 7)
    seg("LN2T", BF16, 2 * D, 7, 7)
    seg("L2F", BF16, 2 * 16 * 512, 8, 9)
    seg("WG", BF16, 64, 0, 8)
    seg("WGLO", BF16, 64, 0, 8)
    seg("L2FLO", BF16, 16 * 512, 8, 8)
    seg("LN2LO", BF16, 2 * D, 7, 7)
    seg("CMB", BF16, 4, 8, 8)
    seg("CROWS", BF16, 4 * 128, 8, 8)
    seg("CROW1", BF16, OWN, 9, 9)
    seg("WE2", BF16, 16 * D, 9, 9)
    seg("HMW", BF16, 16 * 512, 9, 9)
    seg("W1E", BF16, 16 * D, 8, 9)
    seg("GT", BF16, 512, 9, 9)
    seg("CB", BF16, 512, 9, 9)
    seg("BE2", BF16, D, 9, 9)
    seg("MP", F32, 2 * 512, 9, 9)
    seg("ONEB", BF16, 128, 0, 10)
    seg("IDN", BF16, 128, 0, 10)
    seg("OCB", BF16, 1, 0, 10)

    seg("DM1", F32, 4 * 512, 0, 2)
    seg("PM1", F32, NG * 512, 0, 2)
    seg("DM2", F32, 4 * 512, 6, 6)
    seg("PM2", F32, NG * 512, 6, 6)
    seg("PMUL", F32, NG, 0, 6)
    seg("X0T", F32, 2 * D, (1, 3, 4))
    seg("ST", F32, 64, 0, 10)
    seg("RZ", F32, 2048, (2, 6))
    seg("ZB", F32, 2 * 512, (2, 6))
    seg("X1T", F32, 2 * D, (3, 4))
    seg("HST", F32, D, (3, 4))
    seg("TMP1", F32, D, (3, 4, 7))
    seg("TMP2", F32, D, (3, 4, 7))
    seg("RT1", F32, 512, (4, 5))
    seg("RT2", F32, 512, (4, 5))
    seg("HT", F32, 2 * D, 7, 7)
    seg("GE4", F32, 8, 8, 8)
    seg("GRW", F32, 16, 8, 8)
    seg("MST", F32, 2 * 512, 9, 9)
    seg("FIN", F32, 3 * 2 * D, 10, 10)
    seg("OUTT", F32, 2 * D, 10, 10)
    seg("ONER", F32, 128, 0, 10)
    seg("FCONST", F32, 4, 0, 10)
    seg("BQ", F32, 16, 0, 5)
    seg("BK", F32, 16, 0, 5)
    seg("BV", F32, 16, 0, 5)
    seg("BE1", F32, 64, 0, 9)
    seg("WGF", F32, 64, 0, 8)

    placed = []
    hi = [0]
    for s in sorted(segs, key=lambda q: -(q[2] * (2 if q[1] is BF16 else 4))):
        esize = 2 if s[1] is BF16 else 4
        size = (s[2] * esize + 3) & ~3
        cand = 0
        moved = True
        while moved:
            moved = False
            for q in placed:
                if s[3] & q[3]:
                    if not (cand + size <= q[4] or q[4] + q[5] <= cand):
                        cand = (q[4] + q[5] + 3) & ~3
                        moved = True
        s[4] = cand
        placed.append([s[0], s[1], s[2], s[3], cand, size])
        hi[0] = max(hi[0], cand + size)
    total = hi[0]
    assert total <= 206000, f"SBUF over budget: {total} bytes/partition"

    ab = es.enter_context(nc.sbuf_tensor("AB", [128, (total + 1) // 2], BF16))
    SEG = {}
    for name, dt, elems, ph, boff in segs:
        SEG[name] = (dt, boff, elems)

    def V(name, a=0, n=None, rows=(0, 128)):
        dt, boff, elems = SEG[name]
        if n is None:
            n = elems - a
        assert a + n <= elems, (name, a, n, elems)
        if dt is BF16:
            o = boff // 2 + a
            return ab[rows[0]:rows[1], o:o + n]
        o = boff // 2 + 2 * a
        return ab[rows[0]:rows[1], o:o + 2 * n].bitcast(F32)

    pbk = [es.enter_context(nc.psum_tensor(f"PB{i}", [128, 512], F32))
           for i in range(7)]
    pt_bf = es.enter_context(nc.psum_tensor("PTB", [128, 1024], BF16))

    sems = {}
    for name in ("phb", "mm", "ex", "vv",
                 "dG0", "dG1", "dS", "dK0", "dK1", "dX0", "dX1",
                 "dA0", "dA1", "dN0", "dN1", "dLn0", "dLn1", "dSt0", "dSt1",
                 "dAo0", "dAo1", "dKr0", "dKr1", "dM0", "dM1", "dWE2",
                 "dWO", "dO0", "dO1", "dV0", "dV1", "dG2", "gpv",
                 "dX2", "dX3", "dWO2", "dW1a", "dW1b", "dWE2b",
                 "vM0", "vM1", "dCr"):
        sems[name] = Sem(es.enter_context(nc.semaphore(name)))

    plan = {k: [] for k in ENGS}

    def A(ek, f):
        plan[ek].append(f)

    def W(ek, sem, val=None):
        v = sem.n if val is None else val
        if v is None or v <= 0:
            return
        plan[ek].append(lambda e, s=sem.h, v=v: e.wait_ge(s, v))

    _self_sem = {}

    def I(ek, sem, k, f, chain=True, record=True):
        if ek in ("sc", "ve") and chain:
            prev = _self_sem.get(ek)
            if prev is not None and prev[1] > 0:
                plan[ek].append(
                    lambda e, s=prev[0], v=prev[1]: e.wait_ge(s, v))
        sem.n += k
        plan[ek].append(lambda e, s=sem.h, k=k, f=f: f(e).then_inc(s, k))
        if record:
            _self_sem[ek] = (sem.h, sem.n)

    phase_no = [0]
    cuts = {}

    SY_SEMS = ("dS", "dK0", "dK1", "dX0", "dX1", "dA0", "dA1", "dN0", "dN1",
               "dLn0", "dLn1", "dSt0", "dSt1", "dAo0", "dAo1", "dKr0", "dKr1",
               "dO0", "dO1", "dCr", "dM0", "dM1")
    SC_SEMS = ("dV0", "dV1", "dWO2", "dW1b", "dWE2b", "vM0", "vM1")
    GP_SEMS = ("dG0", "dG1", "dG2", "dWE2", "dWO", "dW1a")
    VE_SEMS = ()

    def barrier():
        for nm in SY_SEMS:
            s = sems[nm]
            if s.n > 0:
                plan["sy"].append(lambda e, h=s.h, v=s.n: e.wait_ge(h, v))
        for nm in GP_SEMS:
            s = sems[nm]
            if s.n > 0:
                plan["gp"].append(lambda e, h=s.h, v=s.n: e.wait_ge(h, v))
        for nm in SC_SEMS:
            s = sems[nm]
            if s.n > 0:
                plan["sc"].append(lambda e, h=s.h, v=s.n: e.wait_ge(h, v))
        for nm in VE_SEMS:
            s = sems[nm]
            if s.n > 0:
                plan["ve"].append(lambda e, h=s.h, v=s.n: e.wait_ge(h, v))
        for ek, s in (("te", mm), ("sc", ex), ("ve", vv)):
            if s.n > 0:
                plan[ek].append(lambda e, h=s.h, v=s.n: e.wait_ge(h, v))
        phase_no[0] += 1
        tgt = 5 * phase_no[0]
        for ek in ENGS:
            plan[ek].append(lambda e, s=sems["phb"].h: e.sem_inc(s, 1))
            plan[ek].append(lambda e, s=sems["phb"].h, t=tgt: e.wait_ge(s, t))
        cuts[phase_no[0]] = {ek: len(plan[ek]) for ek in ENGS}

    mm, ex, vv = sems["mm"], sems["ex"], sems["vv"]
    dS = sems["dS"]
    dLn = (sems["dLn0"], sems["dLn1"])
    dO = (sems["dO0"], sems["dO1"])
    dM = (sems["dM0"], sems["dM1"])
    dG0, dG1 = sems["dG0"], sems["dG1"]
    dX = (sems["dX0"], sems["dX1"])
    dA = (sems["dA0"], sems["dA1"])
    dN = (sems["dN0"], sems["dN1"])
    dK = (sems["dK0"], sems["dK1"])
    dSt = (sems["dSt0"], sems["dSt1"])
    dAo = (sems["dAo0"], sems["dAo1"])
    dKr = (sems["dKr0"], sems["dKr1"])
    dGp = (dG0, dG1, sems["dG2"])

    I("sy", dS, 16, lambda e: e.dma_start(
        out=V("DM1").rearrange("p (c q) -> p c q", c=4), in_=dmask.transpose([1, 0, 2])))
    I("sy", dS, 16, lambda e: e.dma_start(
        out=V("PM1").rearrange("p (c q) -> p c q", c=NG), in_=pmask.transpose([1, 0, 2])))
    I("sy", dS, 16, lambda e: e.dma_start(out=V("PMUL"), in_=pmul_in[:, :]))
    I("sy", dS, 16, lambda e: e.dma_start(out=V("ONER", rows=(0, 1)), in_=ones_r[:, :]))
    I("sy", dS, 16, lambda e: e.dma_start(out=V("BQ"), in_=bq_t[:, :]))
    I("sy", dS, 16, lambda e: e.dma_start(out=V("BK"), in_=bk_t[:, :]))
    I("sy", dS, 16, lambda e: e.dma_start(out=V("BV"), in_=bv_t[:, :]))
    I("sy", dS, 16, lambda e: e.dma_start(out=V("BE1"), in_=be1_t[:, :]))
    I("sy", dS, 16, lambda e: e.dma_start(out=V("IDN"), in_=ident_in[:, :]))
    I("sy", dS, 16, lambda e: e.dma_start(out=V("WGF"), in_=wg_t[:, :]))
    I("sy", dS, 16, lambda e: e.dma_start(out=V("FCONST"), in_=fconst[:, :]))
    consts_sy = dS.n
    nc.const_aps.register(F32, float(EPS), V("FCONST", a=0, n=1))
    nc.const_aps.register(F32, float(-M1 * SCALE), V("FCONST", a=1, n=1))
    nc.const_aps.register(F32, float(-M2 * SCALE), V("FCONST", a=2, n=1))
    I("gp", dG0, 16, lambda e: e.dma_start(out=V("COS"), in_=cost[:, :]))
    I("gp", dG0, 16, lambda e: e.dma_start(out=V("SIN"), in_=sint[:, :]))
    I("gp", dG0, 16, lambda e: e.dma_start(out=V("ROT"), in_=rot_t[:, :]))
    I("gp", dG0, 16, lambda e: e.dma_start(
        out=V("ONEB", rows=(0, 1), n=128), in_=ones_r[:, :]))
    I("gp", dG0, 16, lambda e: e.dma_start(
        out=V("OCB"), in_=ones_r.rearrange("o p -> p o")))
    I("gp", dG0, 16, lambda e: e.dma_start(out=V("BOC", rows=(0, 1)), in_=bo_b[:, :]))
    I("gp", dG0, 16, lambda e: e.dma_start(out=V("LW1", rows=(0, 1)), in_=ln1w[:, :]))
    I("gp", dG0, 16, lambda e: e.dma_start(out=V("LB1", rows=(0, 1)), in_=ln1b[:, :]))
    I("gp", dG0, 16, lambda e: e.dma_start(out=V("WG"), in_=wg_t[:, :]))
    consts_gp = dG0.n
    W("te", dS, consts_sy)
    W("te", dG0, consts_gp)
    W("ve", dS, consts_sy)
    W("ve", dG0, consts_gp)
    I("ve", vv, 1, lambda e: e.tensor_tensor(
        V("WGLO"), V("WGF"), V("WG"), ALU.subtract))
    for nm in ("BOC", "LW1", "LB1"):
        for h2 in range(4):
            I("te", mm, 1, lambda e, h2=h2, nm=nm: e.matmul(
                pbk[0][:, :], V("ONEB", rows=(0, 1), n=128),
                V(nm, a=h2 * 512, n=512, rows=(0, 1)), start=True, stop=True))
            W("sc", mm)
            I("sc", ex, 1, lambda e, h2=h2, nm=nm: e.copy(
                V(nm, a=h2 * 512, n=512), pbk[0][:, :]))
            W("te", ex)
    barrier()

    def stats_chain(src_fn, par=0):
        def st(a, n):
            return V("ST", a=32 * par + a, n=n)
        for i in range(4):
            I("ve", vv, 1, lambda e, i=i: e.bn_stats(
                st(6 * i, 6), src_fn()[:, 512 * i:512 * (i + 1)]))
        I("ve", vv, 1, lambda e: e.bn_aggr(st(24, 2), st(0, 24)))
        W("sc", vv)
        I("sc", ex, 1, lambda e: e.activation(
            st(26, 1), st(25, 1), AF.Sqrt, bias=EPS, scale=1.0))
        W("ve", ex)
        I("ve", vv, 1, lambda e: e.reciprocal(st(27, 1), st(26, 1)))
        I("ve", vv, 1, lambda e: e.scalar_tensor_tensor(
            st(28, 1), st(24, 1), -1.0, st(27, 1), ALU.mult, ALU.mult))
        return (lambda: st(27, 1)), (lambda: st(28, 1))

    def rope_emit(kb_ap, cos_a, out_ap):
        W("te", vv)
        W("te", ex)
        I("te", mm, 1, lambda e, kb=kb_ap: e.matmul(
            pbk[4][:, :], V("ROT"), kb, start=True, stop=True))
        rq_mm = mm.n
        cos = V("COS", a=cos_a, n=512)
        sin = V("SIN", a=cos_a, n=512)
        t1 = V("RT1")
        t2 = V("RT2")
        I("ve", vv, 1, lambda e, kb=kb_ap: e.tensor_tensor(t1, kb, cos, ALU.mult))
        W("ve", mm, rq_mm)
        I("ve", vv, 1, lambda e: e.tensor_tensor(t2, pbk[4][:, :], sin, ALU.mult))
        I("ve", vv, 1, lambda e, o=out_ap: e.tensor_tensor(o, t1, t2, ALU.add))

    attn_state = {"blk": 0, "stt": [None, None], "zbc": None, "rz": 0,
                  "zln": [None, None]}

    def emit_attn(s, ext, kf_ap, qf_ap, vs_ap, bias_imm, out_sb, dm="DM1",
                  pm="PM1"):
        par = attn_state["blk"] % 2
        attn_state["blk"] += 1
        AVB = pbk[2] if par == 0 else pbk[4]
        ZB_ = pbk[3] if par == 0 else pbk[6]
        SB3 = (pbk[0], pbk[1], pbk[5])
        M0 = ext - 8
        W("te", vv, attn_state["stt"][par])
        W("te", ex, attn_state["zln"][par])
        exp_base = ex.n
        pt_ready = {}

        def emit_av(c):
            sb = c % 3
            kind, cnt = pt_ready[c]
            W("te", ex if kind == "ex" else vv, cnt)
            I("te", mm, 1, lambda e, c=c, sb=sb, last=(c == ext - 1): e.matmul(
                AVB[:, :], vs_ap[:, c * 128:(c + 1) * 128],
                V("PT", a=sb * 512, n=512), start=(c == 0), stop=last))
            I("te", mm, 1, lambda e, c=c, sb=sb, last=(c == ext - 1): e.matmul(
                ZB_[0:1, :], V("OCB"), V("PT", a=sb * 512, n=512),
                start=(c == 0), stop=last))

        for c in range(ext + 3):
            if c >= 3:
                emit_av(c - 3)
            if c >= ext:
                continue
            sb = c % 3
            if c >= 3:
                W("te", ex, exp_base + (c - 2))
            I("te", mm, 1, lambda e, c=c, sb=sb: e.matmul(
                SB3[sb][:, :], kf_ap[:, c * 128:(c + 1) * 128], qf_ap,
                start=True, stop=True))
            mm_after_s = mm.n
            ci = c - M0
            diag = ci >= 0 and ((s % 2 == 0) == (ci < 4))
            partner = ci >= 0 and not diag
            if diag:
                di = ci if ci < 4 else ci - 4
                W("ve", mm, mm_after_s)
                I("ve", vv, 1, lambda e, sb=sb, di=di: e.tensor_tensor(
                    SB3[sb][:, :], SB3[sb][:, :], V(dm, a=di * 512, n=512),
                    ALU.add))
                W("sc", vv)
            else:
                W("sc", mm, mm_after_s)
            I("sc", ex, 1, lambda e, sb=sb: e.activation(
                V("PT", a=sb * 512, n=512), SB3[sb][:, :], AF.Exp,
                bias=bias_imm, scale=SCALE), chain=False)
            if partner:
                W("ve", ex)
                I("ve", vv, 1, lambda e, sb=sb, s=s: e.tensor_scalar_mul(
                    V("PT", a=sb * 512, n=512), V("PT", a=sb * 512, n=512),
                    V("PMUL", a=s, n=1)))
                pt_ready[c] = ("ve", vv.n)
            else:
                pt_ready[c] = ("ex", ex.n)
        z_done = mm.n

        rzpar = attn_state["rz"] % 2
        attn_state["rz"] += 1
        zt = V("RZ", a=rzpar * 1024, n=512, rows=(0, 1))
        rz = V("RZ", a=rzpar * 1024 + 512, n=512, rows=(0, 1))
        W("sc", mm, z_done)
        I("sc", ex, 1, lambda e, zt=zt, Z=ZB_: e.activation(
            zt, Z[0:1, :], AF.Ln, bias=0.0, scale=1.0), chain=False)
        I("sc", ex, 1, lambda e, zt=zt, rz=rz: e.activation(
            rz, zt, AF.Exp, bias=0.0, scale=-1.0))
        attn_state["zln"][par] = ex.n
        recip_done = ex.n
        avn = V("AVN", a=par * 512, n=512)
        W("ve", mm, z_done)
        I("ve", vv, 1, lambda e, avn=avn, A=AVB: e.tensor_copy(avn, A[:, :]),
          chain=False)
        attn_state["stt"][par] = vv.n

        ptf = pt_bf[:, 0:1024].bitcast(F32)

        def epilogue():
            W("te", ex, attn_state["zbc"])
            W("te", ex, recip_done)
            I("te", mm, 1, lambda e, rz=rz, ptf=ptf: e.matmul(
                ptf, V("ONER", rows=(0, 1), n=128), rz,
                start=True, stop=True))
            W("sc", mm)
            zb = V("ZB", a=(par % 2) * 512, n=512)
            I("sc", ex, 1, lambda e, zb=zb, ptf=ptf: e.copy(zb, ptf))
            attn_state["zbc"] = ex.n
            W("ve", ex)
            I("ve", vv, 1, lambda e, zb=zb, o=out_sb, avn=avn:
                e.scalar_tensor_tensor(o, avn, 1.0, zb, ALU.mult, ALU.mult))
        return epilogue

    x0_loads = [0, 0]
    x0q_loads = [0, 0, 0, 0]
    n1q_free = [None, None, None, None]
    n1t_store = [None, None, None, None]
    dXq = (dX[0], dX[1], sems["dX2"], sems["dX3"])
    dNq = (dN[0], dN[1], sems["dAo0"], sems["dAo1"])
    for tt in range(NT):
        pp = tt % 4
        W("sy", ex, n1q_free[pp])
        I("sy", dXq[pp], 16, lambda e, tt=tt, pp=pp: e.dma_start(
            out=V("X0Q", a=pp * D, n=D), in_=x0p[tt * 128:(tt + 1) * 128, :]))
        x0q_loads[pp] += 1
        W("ve", dXq[pp], 16 * x0q_loads[pp])
        rstd, negmur = stats_chain(lambda pp=pp: V("X0Q", a=pp * D, n=D),
                                   par=pp % 2)
        W("sc", vv)
        W("sc", dNq[pp], n1t_store[pp])
        I("sc", ex, 1, lambda e, pp=pp, r=rstd, nm=negmur: e.activation(
            V("N1T", a=pp * D, n=D), V("X0Q", a=pp * D, n=D), AF.Identity,
            bias=nm(), scale=r()))
        n1q_free[pp] = ex.n
        W("sy", ex, ex.n)
        I("sy", dNq[pp], 16, lambda e, tt=tt, pp=pp: e.dma_start(
            out=n1_d[tt * 128:(tt + 1) * 128, :], in_=V("N1T", a=pp * D, n=D)))
        n1t_store[pp] = dNq[pp].n
    barrier()

    kv_loads = [0, 0]
    head_free = [None, None]
    ao_store = [None, None]
    pend_epi = [None]
    pend_store = [lambda: None]
    blk = 0
    for h in range(DBG_H1 if DBG_H1 else H):
        pp = h % 2
        W("sy", mm, head_free[pp])
        I("sy", dK[pp], 16, lambda e, h=h, pp=pp: e.dma_start(
            out=V("K1F", a=pp * L, n=L), in_=n1_d[:, h * 128:(h + 1) * 128],
            transpose=True))
        I("sy", dK[pp], 16, lambda e, h=h, pp=pp: e.dma_start(
            out=V("V1S", a=pp * L, n=L).rearrange("p (kc i) -> p kc i", kc=32),
            in_=n1_d[:, h * 128:(h + 1) * 128].rearrange("(kc p) i -> kc p i", p=128).transpose([1, 0, 2])))
        kv_loads[pp] += 2
        W("te", dK[pp], 16 * kv_loads[pp])
        for s in range(NG):
            ob = blk % 2
            W("ve", dAo[ob], ao_store[ob])
            epi = emit_attn(s, EXT[s], V("K1F", a=pp * L, n=L),
                            V("K1F", a=pp * L + s * 512, n=512),
                            V("V1S", a=pp * L, n=L), -M1 * SCALE,
                            V("AO", a=ob * 512, n=512))
            if pend_epi[0] is not None:
                pend_epi[0]()
                pend_store[0]()
            pend_epi[0] = epi

            def mk_store(h=h, s=s, ob=ob):
                def st():
                    W("sy", vv)
                    I("sy", dAo[ob], 16, lambda e: e.dma_start(
                        out=a1_d[h * 128:(h + 1) * 128, s * 512:(s + 1) * 512],
                        in_=V("AO", a=ob * 512, n=512)))
                    ao_store[ob] = dAo[ob].n
                return st
            pend_store[0] = mk_store()
            blk += 1
        head_free[pp] = mm.n
    pend_epi[0]()
    pend_store[0]()
    pend_epi[0] = None
    barrier()

    a1_loads = [0, 0]
    x0t_free = [None, None]
    x1t_free = [None, None]
    hsb_store = [None, None]
    ln1_store = [None, None]
    x1_stores = [None, None]
    kro_store = [None, None]
    kt_free = [None, None]
    wband_free = [None, None, None]
    wband_mm = [None, None, None]
    evac_ex = [None, None, None, None]

    def emit_tile(tt):
        pp = tt % 2
        W("sy", vv, x0t_free[pp])
        I("sy", dX[pp], 16, lambda e, tt=tt, pp=pp: e.dma_start(
            out=V("X0T", a=pp * D, n=D), in_=x0p[tt * 128:(tt + 1) * 128, :]))
        x0_loads[pp] += 1
        I("sy", dA[pp], 16, lambda e, tt=tt, pp=pp: e.dma_start(
            out=V("A1T", a=pp * D, n=D), in_=a1_d[:, tt * 128:(tt + 1) * 128],
            transpose=True))
        a1_loads[pp] += 1
        W("ve", dX[pp], dX[pp].n)
        W("ve", dA[pp], dA[pp].n)
        W("ve", vv, x1t_free[pp])
        W("ve", dSt[pp], x1_stores[pp])
        I("ve", vv, 1, lambda e, pp=pp: e.tensor_tensor(
            V("X1T", a=pp * D, n=D), V("X0T", a=pp * D, n=D),
            V("A1T", a=pp * D, n=D), ALU.add))
        x0t_free[pp] = vv.n
        own = (tt // 4) % 2 == 1
        if own:
            grp = tt // 4
            j = (grp - 1) // 2
            orow = j * 512 + (tt % 4) * 128
            W("sy", vv)
            I("sy", dSt[pp], 16, lambda e, orow=orow, pp=pp: e.dma_start(
                out=x1_d[orow:orow + 128, :], in_=V("X1T", a=pp * D, n=D)))
            x1_stores[pp] = dSt[pp].n
        rstd, negmur = stats_chain(lambda pp=pp: V("X1T", a=pp * D, n=D), par=pp)
        W("ve", sems["gpv"], sems["gpv"].n)
        I("ve", vv, 1, lambda e, pp=pp, r=rstd, nm=negmur: e.tensor_scalar(
            V("HST"), V("X1T", a=pp * D, n=D), r(), nm(), ALU.mult, ALU.add))
        x1t_free[pp] = vv.n
        W("gp", vv)
        W("gp", dN[pp], hsb_store[pp])
        I("gp", sems["gpv"], 1, lambda e, pp=pp: e.tensor_tensor(
            V("HSB", a=pp * D, n=D), V("HST"), V("BOC"), ALU.add))
        W("sy", sems["gpv"], sems["gpv"].n)
        I("sy", dN[pp], 16, lambda e, tt=tt, pp=pp: e.dma_start(
            out=hsbo_d[tt * 128:(tt + 1) * 128, :], in_=V("HSB", a=pp * D, n=D)))
        hsb_store[pp] = dN[pp].n
        sb0 = 32 * pp
        I("ve", vv, 1, lambda e, sb0=sb0: e.tensor_scalar(
            V("ST", a=sb0, n=1), V("ST", a=sb0 + 25, n=1), 1.0 + EPS, EPS * EPS,
            ALU.mult, ALU.add))
        W("sc", vv)
        I("sc", ex, 1, lambda e, sb0=sb0: e.activation(
            V("ST", a=sb0 + 1, n=1), V("ST", a=sb0, n=1), AF.Sqrt,
            bias=0.0, scale=1.0))
        W("ve", ex)
        I("ve", vv, 1, lambda e, sb0=sb0: e.reciprocal(
            V("ST", a=sb0 + 2, n=1), V("ST", a=sb0 + 1, n=1)))
        I("ve", vv, 1, lambda e, sb0=sb0: e.tensor_tensor(
            V("ST", a=sb0 + 3, n=1), V("ST", a=sb0 + 26, n=1),
            V("ST", a=sb0 + 2, n=1), ALU.mult))
        I("ve", vv, 1, lambda e, sb0=sb0: e.tensor_scalar(
            V("TMP1"), V("HST"), V("ST", a=sb0 + 3, n=1), None, ALU.mult))
        W("gp", vv)
        I("gp", sems["gpv"], 1, lambda e: e.tensor_tensor(
            V("TMP2"), V("TMP1"), V("LW1"), ALU.mult))
        W("ve", sems["gpv"], sems["gpv"].n)
        W("ve", dLn[pp], ln1_store[pp])
        I("ve", vv, 1, lambda e, pp=pp: e.tensor_tensor(
            V("LN1T", a=pp * D, n=D), V("TMP2"), V("LB1"), ALU.add))
        W("sy", vv)
        I("sy", dLn[pp], 16, lambda e, tt=tt, pp=pp: e.dma_start(
            out=ln1_d[tt * 128:(tt + 1) * 128, :], in_=V("LN1T", a=pp * D, n=D)))
        ln1_store[pp] = dLn[pp].n

    def emit_l1f_loads(half):
        for fc in range(16):
            I("sy", dS, 16, lambda e, fc=fc, half=half: e.dma_start(
                out=V("L1F", a=fc * OWN, n=OWN),
                in_=ln1_d[half * OWN:(half + 1) * OWN, fc * 128:(fc + 1) * 128],
                transpose=True))
        W("te", dS, dS.n)

    def emit_proj_oc(wt, bias_name, tgs, rope, dst, half, oc):
        pp2 = oc % 3
        W("gp", ex, wband_free[pp2])
        W("gp", mm, wband_mm[pp2])
        I("gp", dGp[pp2], 16, lambda e, oc=oc, pp2=pp2, wt=wt: e.dma_start(
            out=V("WB", a=pp2 * D, n=D),
            in_=wt[oc].rearrange("p e i -> p (e i)")))
        W("te", dGp[pp2], dGp[pp2].n)
        for ec in range(16):
            for bi, tg in enumerate(tgs):
                if ec == 0:
                    W("te", ex, evac_ex[bi])
                I("te", mm, 1, lambda e, ec=ec, bi=bi, tg=tg, pp2=pp2:
                    e.matmul(pbk[bi][:, :],
                             V("WB", a=pp2 * D + ec * 128, n=128),
                             V("L1F", a=ec * OWN + tg * 512, n=512),
                             start=(ec == 0), stop=(ec == 15)))
        mm_done = mm.n
        wband_mm[pp2] = mm.n
        for bi, tg in enumerate(tgs):
            kb = (oc * len(tgs) + bi) % 2
            W("sc", mm, mm_done)
            evac_mark = True
            if rope:
                W("sc", vv, kt_free[kb])
                I("sc", ex, 1, lambda e, bi=bi, kb=kb, oc=oc: e.activation(
                    V("KB", a=kb * 512, n=512), pbk[bi][:, :], AF.Identity,
                    bias=V(bias_name, a=oc, n=1), scale=1.0))
                evac_ex[bi] = ex.n
                W("ve", ex)
                W("ve", dKr[kb], kro_store[kb])
                rope_emit(V("KB", a=kb * 512, n=512),
                          half * OWN + tg * 512,
                          V("KRO", a=kb * 512, n=512))
                kt_free[kb] = vv.n
                W("sy", vv)
            else:
                W("sc", dKr[kb], kro_store[kb])
                I("sc", ex, 1, lambda e, bi=bi, kb=kb, oc=oc: e.activation(
                    V("KRO", a=kb * 512, n=512), pbk[bi][:, :], AF.Identity,
                    bias=V(bias_name, a=oc, n=1), scale=1.0))
                evac_ex[bi] = ex.n
                W("sy", ex)
            if dst is q2_d:
                col = (2 * half + (1 if tg == 3 else 0)) * 512
            else:
                col = half * OWN + tg * 512
            I("sy", dKr[kb], 16, lambda e, oc=oc, col=col, kb=kb, dst=dst:
                e.dma_start(
                    out=dst[oc * 128:(oc + 1) * 128, col:col + 512],
                    in_=V("KRO", a=kb * 512, n=512)))
            kro_store[kb] = dKr[kb].n
        wband_free[pp2] = ex.n

    def proj_units(half):
        sel = DBG_PROJ if DBG_PROJ is not None else "kvq"
        units = []
        noc = DBG_OC if DBG_OC else 16
        if "k" in sel:
            for oc in range(noc):
                units.append(lambda oc=oc, h=half: emit_proj_oc(
                    wk_t, "BK", [0, 1, 2, 3], True, k2_d, h, oc))
        if "v" in sel:
            for oc in range(noc):
                units.append(lambda oc=oc, h=half: emit_proj_oc(
                    wv_t, "BV", [0, 1, 2, 3], False, v2_d, h, oc))
        if "q" in sel:
            for oc in range(noc):
                units.append(lambda oc=oc, h=half: emit_proj_oc(
                    wq_t, "BQ", [1, 3], True, q2_d, h, oc))
        return units

    nt3 = DBG_T16 if DBG_T16 else 16
    for t16 in range(nt3):
        emit_tile(t16)
    barrier()
    emit_l1f_loads(0)
    units0 = proj_units(0)
    ti = 0
    for i, u in enumerate(units0):
        u()
        if i % 3 == 2 and ti < nt3:
            emit_tile(16 + ti)
            ti += 1
    while ti < nt3:
        emit_tile(16 + ti)
        ti += 1
    barrier()
    emit_l1f_loads(1)
    for u in proj_units(1):
        u()
    barrier()

    I("sy", dS, 16, lambda e: e.dma_start(
        out=V("DM2").rearrange("p (c q) -> p c q", c=4), in_=dmask.transpose([1, 0, 2])))
    I("sy", dS, 16, lambda e: e.dma_start(
        out=V("PM2").rearrange("p (c q) -> p c q", c=NG), in_=pmask.transpose([1, 0, 2])))
    dm2_ready = dS.n
    W("te", dS, dm2_ready)
    W("ve", dS, dm2_ready)
    kv2_loads = [0, 0]
    head_free2 = [None, None]
    for h in range(DBG_H2 if DBG_H2 else H):
        pp = h % 2
        W("sy", mm, head_free2[pp])
        I("sy", dK[pp], 16, lambda e, h=h, pp=pp: e.dma_start(
            out=V("K2F", a=pp * L, n=L), in_=k2_d[h * 128:(h + 1) * 128, :]))
        I("sy", dK[pp], 16, lambda e, h=h, pp=pp: e.dma_start(
            out=V("V2F", a=pp * L, n=L), in_=v2_d[h * 128:(h + 1) * 128, :]))
        I("sy", dK[pp], 16, lambda e, h=h, pp=pp: e.dma_start(
            out=V("Q2F", a=pp * OWN, n=OWN), in_=q2_d[h * 128:(h + 1) * 128, :]))
        kv2_loads[pp] += 3
        W("te", dK[pp], 16 * (kv_loads[pp] + kv2_loads[pp]))
        W("te", ex, attn_state["zbc"])
        pb0_bf = pbk[0][:, 0:64].bitcast(BF16)
        cpv = {}
        for kc in range(32):
            if kc >= 2:
                W("te", vv, cpv[kc - 2])
            I("te", mm, 1, lambda e, pp=pp, kc=kc: e.transpose(
                pt_bf[:, 0:128] if kc % 2 == 0 else pb0_bf,
                V("V2F", a=pp * L + kc * 128, n=128), V("IDN")))
            W("ve", mm, mm.n)
            I("ve", vv, 1, lambda e, pp=pp, kc=kc: e.tensor_copy(
                V("V2S", a=pp * L + kc * 128, n=128),
                pt_bf[:, 0:128] if kc % 2 == 0 else pb0_bf),
              chain=False)
            cpv[kc] = vv.n
        v2s_ready = vv.n
        W("te", vv, v2s_ready)
        for j, s in enumerate(OWN_SLOTS):
            epi = emit_attn(s, EXT[s], V("K2F", a=pp * L, n=L),
                            V("Q2F", a=pp * OWN + j * 512, n=512),
                            V("V2S", a=pp * L, n=L), -M2 * SCALE,
                            V("A2P", a=h * OWN + j * 512, n=512),
                            dm="DM2", pm="PM2")
            if pend_epi[0] is not None:
                pend_epi[0]()
            pend_epi[0] = epi
        head_free2[pp] = mm.n
    pend_epi[0]()
    pend_epi[0] = None
    barrier()

    for dc in range(16):
        I("gp", sems["dWO"], 16, lambda e, dc=dc: e.dma_start(
            out=V("WO", a=dc * D, n=D), in_=wo_t[dc * 128:(dc + 1) * 128, :]))
    I("gp", dG0, 16, lambda e: e.dma_start(out=V("LW2", rows=(0, 1)), in_=ln2w[:, :]))
    I("gp", dG0, 16, lambda e: e.dma_start(out=V("LB2", rows=(0, 1)), in_=ln2b[:, :]))
    W("te", dG0, dG0.n)
    for nm in ("LW2", "LB2"):
        for h2 in range(4):
            I("te", mm, 1, lambda e, h2=h2, nm=nm: e.matmul(
                pbk[0][:, :], V("ONEB", rows=(0, 1), n=128),
                V(nm, a=h2 * 512, n=512, rows=(0, 1)), start=True, stop=True))
            W("sc", mm)
            I("sc", ex, 1, lambda e, h2=h2, nm=nm: e.copy(
                V(nm, a=h2 * 512, n=512), pbk[0][:, :]))
            W("te", ex)
    wo_ready = sems["dWO"].n
    W("te", sems["dWO"], wo_ready)
    W("te", sems["dWO2"], sems["dWO2"].n)
    hsb_loads = [0, 0]
    ln2_store = [None, None]
    hsbt_free = [None, None]
    ht_free = [None, None]
    h_stores = [None, None]
    for ot in range(DBG_OT if DBG_OT else ONT):
        pp = ot % 2
        gt = OT2G[ot]
        W("sy", vv, hsbt_free[pp])
        I("sy", dA[pp], 16, lambda e, gt=gt, pp=pp: e.dma_start(
            out=V("HSBT", a=pp * D, n=D), in_=hsbo_d[gt * 128:(gt + 1) * 128, :]))
        hsb_loads[pp] += 1
        for dc in range(16):
            for eg in range(4):
                I("te", mm, 1, lambda e, dc=dc, eg=eg, ot=ot: e.matmul(
                    pbk[eg][:, :], V("A2P", a=dc * D + ot * 128, n=128),
                    V("WO", a=dc * D + eg * 512, n=512),
                    start=(dc == 0), stop=(dc == 15)))
        mm_done = mm.n
        W("ve", mm, mm_done)
        W("ve", dA[pp], 16 * (a1_loads[pp] + hsb_loads[pp]))
        W("ve", ex, ht_free[pp])
        W("ve", dSt[pp], h_stores[pp])
        for eg in range(4):
            I("ve", vv, 1, lambda e, eg=eg, pp=pp: e.tensor_tensor(
                V("HT", a=pp * D + eg * 512, n=512), pbk[eg][:, :],
                V("HSBT", a=pp * D + eg * 512, n=512), ALU.add))
        hsbt_free[pp] = vv.n
        W("te", vv)
        W("sy", vv)
        I("sy", dSt[pp], 16, lambda e, ot=ot, pp=pp: e.dma_start(
            out=h_d[ot * 128:(ot + 1) * 128, :], in_=V("HT", a=pp * D, n=D)))
        h_stores[pp] = dSt[pp].n
        rstd3, negmur3 = stats_chain(lambda pp=pp: V("HT", a=pp * D, n=D), par=pp)
        W("sc", vv)
        W("sc", dN[pp], ln2_store[pp])
        I("sc", ex, 1, lambda e, pp=pp, r=rstd3, nm=negmur3: e.activation(
            V("TMP1"), V("HT", a=pp * D, n=D), AF.Identity, bias=nm(), scale=r()))
        ht_free[pp] = ex.n
        W("ve", ex)
        I("ve", vv, 1, lambda e: e.tensor_tensor(
            V("TMP2"), V("TMP1"), V("LW2"), ALU.mult))
        I("ve", vv, 1, lambda e: e.tensor_tensor(
            V("TMP1"), V("TMP2"), V("LB2"), ALU.add))
        I("ve", vv, 1, lambda e, pp=pp: e.tensor_copy(
            V("LN2T", a=pp * D, n=D), V("TMP1")))
        I("ve", vv, 1, lambda e, pp=pp: e.tensor_tensor(
            V("LN2LO", a=pp * D, n=D), V("TMP1"), V("LN2T", a=pp * D, n=D),
            ALU.subtract))
        W("sy", vv)
        I("sy", dN[pp], 16, lambda e, ot=ot, pp=pp: e.dma_start(
            out=ln2h_d[ot * 128:(ot + 1) * 128, :], in_=V("LN2T", a=pp * D, n=D)))
        I("sy", dN[pp], 16, lambda e, ot=ot, pp=pp: e.dma_start(
            out=ln2lo_d[ot * 128:(ot + 1) * 128, :], in_=V("LN2LO", a=pp * D, n=D)))
        ln2_store[pp] = dN[pp].n
    barrier()

    def emit_w1e_loads(e4):
        for ic in range(16):
            I("gp", sems["dW1a"], 16, lambda e, e4=e4, ic=ic: e.dma_start(
                out=V("W1E", a=ic * D, n=D),
                in_=we1_t[e4, ic].rearrange("p f i -> p (f i)")))

    emit_w1e_loads(0)
    crs_store = [None, None, None, None]
    for sl in range(DBG_SL if DBG_SL else 4):
        pp = sl % 2
        if sl >= 1:
            W("sy", mm)
        for fc in range(16):
            I("sy", dK[pp], 16, lambda e, sl=sl, fc=fc, pp=pp: e.dma_start(
                out=V("L2F", a=pp * 16 * 512 + fc * 512, n=512),
                in_=ln2h_d[sl * 512:(sl + 1) * 512, fc * 128:(fc + 1) * 128],
                transpose=True))
            I("sy", dK[pp], 16, lambda e, sl=sl, fc=fc, pp=pp: e.dma_start(
                out=V("L2FLO", a=fc * 512, n=512),
                in_=ln2lo_d[sl * 512:(sl + 1) * 512, fc * 128:(fc + 1) * 128],
                transpose=True))
        kv2_loads[pp] += 32
        W("te", dK[pp], 16 * (kv_loads[pp] + kv2_loads[pp]))
        for twi in range(4):
            tt = sl * 4 + twi
            W("te", ex)
            for fc in range(16):
                I("te", mm, 1, lambda e, fc=fc, twi=twi, pp=pp: e.matmul(
                    pbk[4][:, 0:4],
                    V("L2F", a=pp * 16 * 512 + fc * 512 + twi * 128, n=128),
                    V("WG", a=fc * 4, n=4), start=(fc == 0), stop=False))
                I("te", mm, 1, lambda e, fc=fc, twi=twi, pp=pp: e.matmul(
                    pbk[4][:, 0:4],
                    V("L2F", a=pp * 16 * 512 + fc * 512 + twi * 128, n=128),
                    V("WGLO", a=fc * 4, n=4), start=False, stop=False))
                I("te", mm, 1, lambda e, fc=fc, twi=twi, pp=pp: e.matmul(
                    pbk[4][:, 0:4],
                    V("L2FLO", a=fc * 512 + twi * 128, n=128),
                    V("WG", a=fc * 4, n=4), start=False,
                    stop=(fc == 15)))
            gmm = mm.n
            W("ve", mm, gmm)
            I("ve", vv, 1, lambda e: e.reduce_max(
                V("ST", a=32, n=1), pbk[4][:, 0:4], axis=AX.X))
            I("ve", vv, 1, lambda e: e.tensor_scalar(
                V("ST", a=33, n=1), V("ST", a=32, n=1), -1.0, None, ALU.mult))
            W("sc", vv)
            I("sc", ex, 1, lambda e: e.activation(
                V("GE4", a=0, n=4), pbk[4][:, 0:4], AF.Exp,
                bias=V("ST", a=33, n=1), scale=1.0, accum_out=V("ST", a=34, n=1)))
            W("ve", ex)
            I("ve", vv, 1, lambda e: e.reciprocal(V("ST", a=35, n=1), V("ST", a=34, n=1)))
            I("ve", vv, 1, lambda e: e.tensor_scalar(
                V("GRW", a=0, n=4), V("GE4", a=0, n=4), V("ST", a=35, n=1),
                None, ALU.mult))
            I("ve", vv, 1, lambda e: e.reduce_max(
                V("ST", a=36, n=1), V("GRW", a=0, n=4), axis=AX.X))
            I("ve", vv, 1, lambda e: e.tensor_scalar(
                V("GRW", a=4, n=4), V("GRW", a=0, n=4), V("ST", a=36, n=1),
                None, ALU.is_lt))
            I("ve", vv, 1, lambda e: e.tensor_tensor(
                V("GRW", a=8, n=4), V("GRW", a=0, n=4), V("GRW", a=4, n=4), ALU.mult))
            I("ve", vv, 1, lambda e: e.reduce_max(
                V("ST", a=37, n=1), V("GRW", a=8, n=4), axis=AX.X))
            I("ve", vv, 1, lambda e: e.tensor_scalar(
                V("GRW", a=4, n=4), V("GRW", a=0, n=4), V("ST", a=37, n=1),
                None, ALU.is_ge))
            W("sc", vv)
            I("sc", ex, 1, lambda e: e.activation(
                V("ST", a=38, n=1), V("ST", a=36, n=1), AF.Exp, bias=0.0, scale=1.0))
            I("sc", ex, 1, lambda e: e.activation(
                V("ST", a=39, n=1), V("ST", a=37, n=1), AF.Exp, bias=0.0, scale=1.0))
            I("sc", ex, 1, lambda e: e.activation(
                V("GE4", a=4, n=4), V("GRW", a=0, n=4), AF.Exp, bias=0.0, scale=1.0))
            W("ve", ex)
            I("ve", vv, 1, lambda e: e.tensor_tensor(
                V("ST", a=40, n=1), V("ST", a=38, n=1), V("ST", a=39, n=1), ALU.add))
            I("ve", vv, 1, lambda e: e.reciprocal(V("ST", a=41, n=1), V("ST", a=40, n=1)))
            I("ve", vv, 1, lambda e: e.tensor_tensor(
                V("GRW", a=12, n=4), V("GE4", a=4, n=4), V("GRW", a=4, n=4), ALU.mult))
            I("ve", vv, 1, lambda e: e.tensor_scalar(
                V("CMB"), V("GRW", a=12, n=4), V("ST", a=41, n=1), None, ALU.mult))
            W("te", vv)
            for e4 in range(NE):
                I("te", mm, 1, lambda e, e4=e4: e.transpose(
                    pt_bf[0:1, e4 * 128:(e4 + 1) * 128], V("CMB", a=e4, n=1),
                    V("IDN")))
            W("ve", mm)
            for e4 in range(NE):
                W("ve", sems["dCr"], crs_store[e4])
                I("ve", vv, 1, lambda e, e4=e4: e.tensor_copy(
                    V("CROWS", a=e4 * 128, n=128, rows=(0, 1)),
                    pt_bf[0:1, e4 * 128:(e4 + 1) * 128]))
                W("sy", vv)
                I("sy", sems["dCr"], 16, lambda e, e4=e4, tt=tt: e.dma_start(
                    out=crow_d[e4:e4 + 1, tt * 128:(tt + 1) * 128],
                    in_=V("CROWS", a=e4 * 128, n=128, rows=(0, 1))))
                crs_store[e4] = sems["dCr"].n
            W("te", vv)
    barrier()

    gelu_ex = [None, None]
    mst_free = [None, None]
    mp_used = [None, None]
    m_prev_expert = [0, 0]
    vM = (sems["vM0"], sems["vM1"])
    n_exp = DBG_E if DBG_E else NE
    for e4 in range(n_exp):
        if e4 > 0:
            W("gp", vM[0], m_prev_expert[0])
            W("gp", vM[1], m_prev_expert[1])
        for ic in range(16):
            I("gp", sems["dWE2"], 16, lambda e, e4=e4, ic=ic: e.dma_start(
                out=V("WE2", a=ic * D, n=D),
                in_=we2_t[e4, ic * 128:(ic + 1) * 128, :]))
        I("gp", sems["dWE2"], 16, lambda e, e4=e4: e.dma_start(
            out=V("BE2", rows=(0, 1)), in_=be2_t[e4:e4 + 1, :]))
        I("gp", sems["dWE2"], 16, lambda e, e4=e4: e.dma_start(
            out=V("CROW1", rows=(0, 1)), in_=crow_d[e4:e4 + 1, :]))
        we2_ready = sems["dWE2"].n
        we2b_ready = sems["dWE2b"].n
        for sl in range(DBG_SL if DBG_SL else 4):
            pp = sl % 2
            if e4 > 0 or sl >= 2:
                W("sy", mm)
            for fc in range(16):
                I("sy", dK[pp], 16, lambda e, sl=sl, fc=fc, pp=pp: e.dma_start(
                    out=V("L2F", a=pp * 16 * 512 + fc * 512, n=512),
                    in_=ln2h_d[sl * 512:(sl + 1) * 512, fc * 128:(fc + 1) * 128],
                    transpose=True))
            kv2_loads[pp] += 16
            l2f_ready = 16 * (kv_loads[pp] + kv2_loads[pp])
            if e4 > 0 and sl == 0:
                W("sy", vM[0], m_prev_expert[0])
                W("sy", vM[1], m_prev_expert[1])
            W("te", sems["dWE2"], we2_ready)
            W("te", sems["dWE2b"], we2b_ready)
            if sl == 0:
                W("te", sems["dW1a"], 256 * (e4 + 1))
            W("te", vv)
            I("te", mm, 1, lambda e, e4=e4, sl=sl: e.matmul(
                pbk[5][:, :], V("ONEB", rows=(0, 1), n=128),
                V("CROW1", a=sl * 512, n=512, rows=(0, 1)),
                start=True, stop=True))
            W("sc", mm)
            I("sc", ex, 1, lambda e: e.copy(V("CB"), pbk[5][:, :]))
            cb_ready = ex.n
            for ic in range(16):
                bi = ic % 2
                W("te", dK[pp], l2f_ready)
                W("te", ex, gelu_ex[bi])
                for fc in range(16):
                    I("te", mm, 1, lambda e, fc=fc, bi=bi, pp=pp,
                      e4=e4, sl=sl, ic=ic: e.matmul(
                        pbk[bi][:, :], V("W1E", a=ic * D + fc * 128, n=128),
                        V("L2F", a=pp * 16 * 512 + fc * 512, n=512),
                        start=(fc == 0), stop=(fc == 15)).annotate(
                            f"mm1 e{e4} sl{sl} ic{ic} fc{fc}"))
                W("sc", mm)
                W("sc", vv)
                I("sc", ex, 1, lambda e, bi=bi, ic=ic, e4=e4: e.activation(
                    V("GT"), pbk[bi][:, :], AF.Gelu,
                    bias=V("BE1", a=ic * 4 + e4, n=1), scale=1.0))
                gelu_ex[bi] = ex.n
                W("ve", ex)
                W("ve", ex, cb_ready)
                I("ve", vv, 1, lambda e, ic=ic: e.tensor_tensor(
                    V("HMW", a=ic * 512, n=512), V("GT"), V("CB"), ALU.mult))
            if sl == (DBG_SL if DBG_SL else 4) - 1 and e4 + 1 < n_exp:
                W("gp", mm, mm.n)
                emit_w1e_loads(e4 + 1)
            hm_ready = vv.n
            W("te", vv, hm_ready)
            for twi in range(4):
                tt = sl * 4 + twi
                for ic in range(16):
                    for dg in range(4):
                        I("te", mm, 1, lambda e, ic=ic, dg=dg, twi=twi: e.matmul(
                            pbk[dg][:, :], V("HMW", a=ic * 512 + twi * 128, n=128),
                            V("WE2", a=ic * D + dg * 512, n=512),
                            start=(ic == 0), stop=False))
                for dg in range(4):
                    I("te", mm, 1, lambda e, dg=dg, e4=e4, tt=tt: e.matmul(
                        pbk[dg][:, :],
                        V("CROW1", a=tt * 128, n=128, rows=(0, 1)),
                        V("BE2", a=dg * 512, n=512, rows=(0, 1)),
                        start=False, stop=True))
                m2done = mm.n
                for dg in range(4):
                    mb = (tt * 4 + dg) % 2
                    if e4 > 0:
                        W("sy", vv, mp_used[mb])
                        I("sy", dM[mb], 16, lambda e, tt=tt, dg=dg, mb=mb:
                            e.dma_start(
                                out=V("MP", a=mb * 512, n=512),
                                in_=m_d[tt * 128:(tt + 1) * 128,
                                        dg * 512:(dg + 1) * 512]))
                    W("ve", mm, m2done)
                    W("ve", vM[mb], mst_free[mb])
                    if e4 == 0:
                        I("ve", vv, 1, lambda e, dg=dg, mb=mb: e.tensor_copy(
                            V("MST", a=mb * 512, n=512), pbk[dg][:, :]))
                    else:
                        W("ve", dM[mb], dM[mb].n)
                        I("ve", vv, 1, lambda e, dg=dg, mb=mb: e.tensor_tensor(
                            V("MST", a=mb * 512, n=512), pbk[dg][:, :],
                            V("MP", a=mb * 512, n=512), ALU.add))
                        mp_used[mb] = vv.n
                    W("sc", vv)
                    I("sc", vM[mb], 16, lambda e, tt=tt, dg=dg, mb=mb: e.dma_start(
                        out=m_d[tt * 128:(tt + 1) * 128, dg * 512:(dg + 1) * 512],
                        in_=V("MST", a=mb * 512, n=512)),
                      chain=False, record=False)
                    mst_free[mb] = vM[mb].n
                W("te", vv)
        m_prev_expert[0] = vM[0].n
        m_prev_expert[1] = vM[1].n
    barrier()

    fin_free = [None, None]
    out_store = [None, None]
    fin_loads = [0, 0]
    for ot in range(ONT):
        pp = ot % 2
        base = pp * 3 * D
        W("sy", vv, fin_free[pp])
        I("sy", dX[pp], 16, lambda e, ot=ot, base=base: e.dma_start(
            out=V("FIN", a=base, n=D), in_=x1_d[ot * 128:(ot + 1) * 128, :]))
        I("sy", dX[pp], 16, lambda e, ot=ot, base=base: e.dma_start(
            out=V("FIN", a=base + D, n=D), in_=h_d[ot * 128:(ot + 1) * 128, :]))
        I("sy", dX[pp], 16, lambda e, ot=ot, base=base: e.dma_start(
            out=V("FIN", a=base + 2 * D, n=D), in_=m_d[ot * 128:(ot + 1) * 128, :]))
        x0_loads[pp] += 3
        W("ve", dX[pp], dX[pp].n)
        W("ve", dO[pp], out_store[pp])
        I("ve", vv, 1, lambda e, base=base: e.tensor_tensor(
            V("FIN", a=base, n=D), V("FIN", a=base, n=D),
            V("FIN", a=base + D, n=D), ALU.add))
        I("ve", vv, 1, lambda e, base=base, pp=pp: e.tensor_tensor(
            V("OUTT", a=pp * D, n=D), V("FIN", a=base, n=D),
            V("FIN", a=base + 2 * D, n=D), ALU.add))
        fin_free[pp] = vv.n
        W("sy", vv)
        I("sy", dO[pp], 16, lambda e, ot=ot, pp=pp: e.dma_start(
            out=out[ot * 128:(ot + 1) * 128, :], in_=V("OUTT", a=pp * D, n=D)))
        out_store[pp] = dO[pp].n
    if tap_outs:
        barrier()
        for t, (dst, src) in tap_outs.items():
            I("sy", dO[0], 16, lambda e, dst=dst, src=src: e.dma_start(
                out=dst[:, :], in_=src[:, :]))
    W("sy", dO[0], dO[0].n)
    W("sy", dO[1], dO[1].n)

    if max_phase < 99:
        cut_ph = min(k for k in cuts if k > max_phase) if any(
            k > max_phase for k in cuts) else None
        if cut_ph is not None:
            for ek in ENGS:
                plan[ek] = plan[ek][:cuts[cut_ph][ek]]
    with nc.Block() as block:
        @block.tensor
        def _(e):
            for f in plan["te"]:
                f(e)

        @block.scalar
        def _(e):
            for f in plan["sc"]:
                f(e)

        @block.vector
        def _(e):
            for f in plan["ve"]:
                f(e)

        @block.gpsimd
        def _(e):
            for f in plan["gp"]:
                f(e)

        @block.sync
        def _(e):
            for f in plan["sy"]:
                f(e)

    es.close()
    return nc



def _host_inputs(core, args):
    b, p = core // 2, core % 2
    g = np.arange(NG)
    tg = (g ^ 1) if p == 0 else g
    tp = (tg[:, None] * G + np.arange(G)[None, :]).reshape(-1)

    x0 = np.ascontiguousarray(np.asarray(args["L0_attn_x"], np.float32)[b][tp])

    ki = np.arange(128)
    qj = np.arange(512)
    dmask = np.empty((4, 128, 512), np.float32)
    for k in range(4):
        dmask[k] = np.where((k * 128 + ki)[:, None] <= qj[None, :], 0.0, NEG)
    pmask = np.empty((NG, 128, 512), np.float32)
    for s in range(NG):
        pmask[s] = 0.0 if tg[s ^ 1] < tg[s] else NEG

    inv = 1.0 / (10000.0 ** (np.arange(0, HD, 2) / HD))
    ang = tp[None, :].astype(np.float64) * inv[:, None]
    cost = np.concatenate([np.cos(ang), np.cos(ang)], 0).astype(np.float32)
    sint = np.concatenate([np.sin(ang), np.sin(ang)], 0).astype(np.float32)
    rot = np.zeros((128, 128), np.float32)
    rot[64:, :64] = -np.eye(64)
    rot[:64, 64:] = np.eye(64)

    def wt(w):
        return np.ascontiguousarray(
            np.asarray(w, np.float32).reshape(16, 128, 16, 128).transpose(2, 1, 0, 3))

    def bt(v):
        return np.ascontiguousarray(np.asarray(v, np.float32).reshape(16, 128).T)

    we1 = np.asarray(args["L0_ffn_we1"], np.float32)
    we1_t = np.ascontiguousarray(
        we1.reshape(NE, 16, 128, 16, 128).transpose(0, 3, 2, 1, 4))
    wg = np.asarray(args["L0_ffn_w_gate"], np.float32)
    wg_t = np.ascontiguousarray(
        wg.reshape(16, 128, 4).transpose(1, 0, 2).reshape(128, 64))
    be1 = np.asarray(args["L0_ffn_be1"], np.float32)
    be1_t = np.ascontiguousarray(
        be1.reshape(NE, 16, 128).transpose(2, 1, 0).reshape(128, 64))

    return {
        "x0p": x0,
        "dmask": dmask, "pmask": pmask, "cost": cost, "sint": sint,
        "rot_t": rot.astype(BF),
        "wq_t": wt(args["L0_ffn_wq"]), "wk_t": wt(args["L0_ffn_wk"]),
        "wv_t": wt(args["L0_ffn_wv"]),
        "wo_t": np.ascontiguousarray(np.asarray(args["L0_ffn_wo"], np.float32)),
        "bq_t": bt(args["L0_ffn_bq"]), "bk_t": bt(args["L0_ffn_bk"]),
        "bv_t": bt(args["L0_ffn_bv"]),
        "bo_b": np.asarray(args["L0_ffn_bo"], np.float32).reshape(1, D),
        "ln1w": np.asarray(args["L0_ffn_ln1_w"], np.float32).reshape(1, D),
        "ln1b": np.asarray(args["L0_ffn_ln1_b"], np.float32).reshape(1, D),
        "ln2w": np.asarray(args["L0_ffn_ln2_w"], np.float32).reshape(1, D),
        "ln2b": np.asarray(args["L0_ffn_ln2_b"], np.float32).reshape(1, D),
        "wg_t": wg_t, "be1_t": be1_t,
        "be2_t": np.ascontiguousarray(np.asarray(args["L0_ffn_be2"], np.float32)),
        "we1_t": we1_t,
        "we2_t": np.ascontiguousarray(np.asarray(args["L0_ffn_we2"], np.float32)),
        "ones_r": np.ones((1, 128), np.float32),
        "fconst": np.broadcast_to(np.array(
            [EPS, -M1 * SCALE, -M2 * SCALE, 0.0], np.float32), (128, 4)).copy(),
        "pmul_in": np.broadcast_to(
            np.array([1.0 if tg[s ^ 1] < tg[s] else 0.0 for s in range(NG)],
                     np.float32), (128, NG)).copy(),
        "ident_in": np.eye(128, dtype=BF),
    }, tp


def kernel(**inputs):
    nc = build_program()
    in_maps = []
    tps = []
    for core in range(8):
        m, tp = _host_inputs(core, inputs)
        in_maps.append(m)
        tps.append(tp)
    res = run_bass_kernel_spmd(nc, in_maps, core_ids=list(range(8)))
    full = np.empty((B, L, D), np.float32)
    own_storage = np.concatenate(
        [np.arange((2 * j + 1) * G, (2 * j + 2) * G) for j in range(4)])
    for core in range(8):
        b = core // 2
        full[b, tps[core][own_storage]] = res.results[core]["out"]
    return full


# version 17 (final)
# speedup vs baseline: 1.1256x; 1.0024x over previous
import math
import contextlib
import numpy as np
import ml_dtypes
import concourse.bass as bass
import concourse.mybir as mybir
from concourse.bass_utils import run_bass_kernel_spmd

F32 = mybir.dt.float32
BF16 = mybir.dt.bfloat16
AF = mybir.ActivationFunctionType
ALU = mybir.AluOpType
AX = mybir.AxisListType
BF = ml_dtypes.bfloat16

B, L, D, H, HD, NE = 4, 4096, 2048, 16, 128, 4
G = 512
NG = L // G
NT = L // 128
OWN = L // 2
ONT = OWN // 128
SCALE = 1.0 / math.sqrt(HD)
M1, M2 = 24.0, 14.0
NEG = -1.0e9
EPS = 1e-5

EXT = [4 * (s + 2) if s % 2 == 0 else 4 * (s + 1) for s in range(NG)]
OWN_SLOTS = [1, 3, 5, 7]
OT2G = [(ot // 4) * 8 + 4 + (ot % 4) for ot in range(ONT)]

ENGS = ("te", "sc", "ve", "gp", "sy")
DBG_T16 = None
DBG_PROJ = None
DBG_OC = None
DBG_H1 = None
DBG_H2 = None
DBG_OT = None
DBG_SL = None
DBG_E = None


class Sem:
    def __init__(self, h):
        self.h = h
        self.n = 0


def build_program(taps=(), max_phase=99):
    nc = bass.Bass("TRN2", target_bir_lowering=False)
    es = contextlib.ExitStack()

    def din(name, shape, dt=F32):
        return nc.dram_tensor(name, shape, dt, kind="ExternalInput").ap()

    x0p = din("x0p", [L, D])
    dmask = din("dmask", [4, 128, 512])
    pmask = din("pmask", [NG, 128, 512])
    cost = din("cost", [128, L])
    sint = din("sint", [128, L])
    rot_t = din("rot_t", [128, 128], BF16)
    wq_t = din("wq_t", [16, 128, 16, 128])
    wk_t = din("wk_t", [16, 128, 16, 128])
    wv_t = din("wv_t", [16, 128, 16, 128])
    wo_t = din("wo_t", [D, D])
    bq_t = din("bq_t", [128, 16])
    bk_t = din("bk_t", [128, 16])
    bv_t = din("bv_t", [128, 16])
    bo_b = din("bo_b", [1, D])
    ln1w = din("ln1w", [1, D])
    ln1b = din("ln1b", [1, D])
    ln2w = din("ln2w", [1, D])
    ln2b = din("ln2b", [1, D])
    wg_t = din("wg_t", [128, 64])
    be1_t = din("be1_t", [128, 64])
    be2_t = din("be2_t", [NE, D])
    we1_t = din("we1_t", [NE, 16, 128, 16, 128])
    we2_t = din("we2_t", [NE, D, D])
    ones_r = din("ones_r", [1, 128])
    fconst = din("fconst", [128, 4])
    pmul_in = din("pmul_in", [128, NG])
    ident_in = din("ident_in", [128, 128], BF16)

    out = nc.dram_tensor("out", [OWN, D], F32, kind="ExternalOutput").ap()

    def dsc(name, shape, dt):
        return nc.dram_tensor(name, shape, dt, kind="Internal").ap()

    n1_d = dsc("n1_d", [L, D], BF16)
    a1_d = dsc("a1_d", [D, L], BF16)
    x1_d = dsc("x1_d", [OWN, D], F32)
    hsbo_d = dsc("hsbo_d", [L, D], BF16)
    ln1_d = dsc("ln1_d", [L, D], BF16)
    k2_d = dsc("k2_d", [D, L], BF16)
    v2_d = dsc("v2_d", [D, L], BF16)
    q2_d = dsc("q2_d", [D, OWN], BF16)
    h_d = dsc("h_d", [OWN, D], F32)
    ln2h_d = dsc("ln2h_d", [OWN, D], BF16)
    ln2lo_d = dsc("ln2lo_d", [OWN, D], BF16)
    m_d = dsc("m_d", [OWN, D], F32)
    crow_d = dsc("crow_d", [NE, OWN], BF16)

    tap_outs = {}
    _tap_src = {"n1": (n1_d, BF16), "a1": (a1_d, BF16), "x1": (x1_d, F32),
                "hsbo": (hsbo_d, BF16), "ln1": (ln1_d, BF16),
                "k2": (k2_d, BF16), "v2": (v2_d, BF16), "q2": (q2_d, BF16),
                "h": (h_d, F32), "ln2h": (ln2h_d, BF16), "m": (m_d, F32)}
    for t in taps:
        src, dt = _tap_src[t]
        tap_outs[t] = (nc.dram_tensor("tap_" + t, list(src.shape), dt,
                                      kind="ExternalOutput").ap(), src)

    segs = []

    def seg(name, dt, elems, p0, p1=None):
        ph = frozenset(p0) if isinstance(p0, (list, tuple, set)) else \
            frozenset(range(p0, (p1 if p1 is not None else p0) + 1))
        segs.append([name, dt, elems, ph, None])

    seg("N1T", BF16, 4 * D, 1, 1)
    seg("X0Q", F32, 4 * D, 1, 1)
    seg("K1F", BF16, 2 * L, 2, 2)
    seg("V1S", BF16, 2 * L, 2, 2)
    seg("PT", BF16, 3 * 512, (2, 6))
    seg("AVN", BF16, 2 * 512, (2, 6))
    seg("AO", BF16, 2 * 512, 2, 2)
    seg("A1T", BF16, 2 * D, (3, 4))
    seg("HSB", BF16, 2 * D, (3, 4))
    seg("LN1T", BF16, 2 * D, (3, 4))
    seg("COS", BF16, L, 0, 5)
    seg("SIN", BF16, L, 0, 5)
    seg("BOC", BF16, D, 0, 4)
    seg("LW1", BF16, D, 0, 4)
    seg("LB1", BF16, D, 0, 4)
    seg("LW2", BF16, D, 7, 7)
    seg("LB2", BF16, D, 7, 7)
    seg("L1F", BF16, 16 * OWN, (4, 5))
    seg("WB", BF16, 3 * D, (4, 5))
    seg("KRO", BF16, 2 * 512, (4, 5))
    seg("KB", BF16, 2 * 512, (4, 5))
    seg("ROT", BF16, 128, 0, 5)
    seg("K2F", BF16, 2 * L, 6, 6)
    seg("V2S", BF16, 2 * L, 6, 6)
    seg("Q2F", BF16, 2 * OWN, 6, 6)
    seg("V2F", BF16, 2 * L, 6, 6)
    seg("A2P", BF16, 16 * OWN, 6, 7)
    seg("WO", BF16, 16 * D, 7, 7)
    seg("HSBT", BF16, 2 * D, 7, 7)
    seg("LN2T", BF16, 2 * D, 7, 7)
    seg("L2F", BF16, 2 * 16 * 512, 8, 9)
    seg("WG", BF16, 64, 0, 8)
    seg("WGLO", BF16, 64, 0, 8)
    seg("L2FLO", BF16, 16 * 512, 8, 8)
    seg("LN2LO", BF16, 2 * D, 7, 7)
    seg("CMB", BF16, 4, 8, 8)
    seg("CROWS", BF16, 4 * 128, 8, 8)
    seg("CROW1", BF16, OWN, 9, 9)
    seg("WE2", BF16, 16 * D, 8, 9)
    seg("HMW", BF16, 16 * 512, 9, 9)
    seg("W1E", BF16, 16 * D, 8, 9)
    seg("GT", BF16, 512, 9, 9)
    seg("CB", BF16, 512, 9, 9)
    seg("BE2", BF16, D, 8, 9)
    seg("MP", F32, 2 * 512, 9, 9)
    seg("ONEB", BF16, 128, 0, 10)
    seg("IDN", BF16, 128, 0, 10)
    seg("OCB", BF16, 1, 0, 10)

    seg("DM1", F32, 4 * 512, 0, 2)
    seg("PM1", F32, NG * 512, 0, 2)
    seg("DM2", F32, 4 * 512, 6, 6)
    seg("PM2", F32, NG * 512, 6, 6)
    seg("PMUL", F32, NG, 0, 6)
    seg("X0T", F32, 2 * D, (1, 3, 4))
    seg("ST", F32, 64, 0, 10)
    seg("RZ", F32, 2048, (2, 6))
    seg("ZB", F32, 2 * 512, (2, 6))
    seg("X1T", F32, 2 * D, (3, 4))
    seg("HST", F32, D, (3, 4))
    seg("TMP1", F32, D, (3, 4, 7))
    seg("TMP2", F32, D, (3, 4, 7))
    seg("RT1", F32, 512, (4, 5))
    seg("RT2", F32, 512, (4, 5))
    seg("HT", F32, 2 * D, 7, 7)
    seg("GE4", F32, 8, 8, 8)
    seg("GRW", F32, 16, 8, 8)
    seg("MST", F32, 2 * 512, 9, 9)
    seg("FIN", F32, 3 * 2 * D, 10, 10)
    seg("OUTT", F32, 2 * D, 10, 10)
    seg("ONER", F32, 128, 0, 10)
    seg("FCONST", F32, 4, 0, 10)
    seg("BQ", F32, 16, 0, 5)
    seg("BK", F32, 16, 0, 5)
    seg("BV", F32, 16, 0, 5)
    seg("BE1", F32, 64, 0, 9)
    seg("WGF", F32, 64, 0, 8)

    placed = []
    hi = [0]
    for s in sorted(segs, key=lambda q: -(q[2] * (2 if q[1] is BF16 else 4))):
        esize = 2 if s[1] is BF16 else 4
        size = (s[2] * esize + 3) & ~3
        cand = 0
        moved = True
        while moved:
            moved = False
            for q in placed:
                if s[3] & q[3]:
                    if not (cand + size <= q[4] or q[4] + q[5] <= cand):
                        cand = (q[4] + q[5] + 3) & ~3
                        moved = True
        s[4] = cand
        placed.append([s[0], s[1], s[2], s[3], cand, size])
        hi[0] = max(hi[0], cand + size)
    total = hi[0]
    assert total <= 206000, f"SBUF over budget: {total} bytes/partition"

    ab = es.enter_context(nc.sbuf_tensor("AB", [128, (total + 1) // 2], BF16))
    SEG = {}
    for name, dt, elems, ph, boff in segs:
        SEG[name] = (dt, boff, elems)

    def V(name, a=0, n=None, rows=(0, 128)):
        dt, boff, elems = SEG[name]
        if n is None:
            n = elems - a
        assert a + n <= elems, (name, a, n, elems)
        if dt is BF16:
            o = boff // 2 + a
            return ab[rows[0]:rows[1], o:o + n]
        o = boff // 2 + 2 * a
        return ab[rows[0]:rows[1], o:o + 2 * n].bitcast(F32)

    pbk = [es.enter_context(nc.psum_tensor(f"PB{i}", [128, 512], F32))
           for i in range(7)]
    pt_bf = es.enter_context(nc.psum_tensor("PTB", [128, 1024], BF16))

    sems = {}
    for name in ("phb", "mm", "ex", "vv",
                 "dG0", "dG1", "dS", "dK0", "dK1", "dX0", "dX1",
                 "dA0", "dA1", "dN0", "dN1", "dLn0", "dLn1", "dSt0", "dSt1",
                 "dAo0", "dAo1", "dKr0", "dKr1", "dM0", "dM1", "dWE2",
                 "dWO", "dO0", "dO1", "dV0", "dV1", "dG2", "gpv",
                 "dX2", "dX3", "dWO2", "dW1a", "dW1b", "dWE2b",
                 "vM0", "vM1", "dCr"):
        sems[name] = Sem(es.enter_context(nc.semaphore(name)))

    plan = {k: [] for k in ENGS}

    def A(ek, f):
        plan[ek].append(f)

    def W(ek, sem, val=None):
        v = sem.n if val is None else val
        if v is None or v <= 0:
            return
        plan[ek].append(lambda e, s=sem.h, v=v: e.wait_ge(s, v))

    _self_sem = {}

    def I(ek, sem, k, f, chain=True, record=True):
        if ek in ("sc", "ve") and chain:
            prev = _self_sem.get(ek)
            if prev is not None and prev[1] > 0:
                plan[ek].append(
                    lambda e, s=prev[0], v=prev[1]: e.wait_ge(s, v))
        sem.n += k
        plan[ek].append(lambda e, s=sem.h, k=k, f=f: f(e).then_inc(s, k))
        if record:
            _self_sem[ek] = (sem.h, sem.n)

    phase_no = [0]
    cuts = {}

    SY_SEMS = ("dS", "dK0", "dK1", "dX0", "dX1", "dA0", "dA1", "dN0", "dN1",
               "dLn0", "dLn1", "dSt0", "dSt1", "dAo0", "dAo1", "dKr0", "dKr1",
               "dO0", "dO1", "dCr", "dM0", "dM1")
    SC_SEMS = ("dV0", "dV1", "dWO2", "dW1b", "dWE2b", "vM0", "vM1")
    GP_SEMS = ("dG0", "dG1", "dG2", "dWE2", "dWO", "dW1a")
    VE_SEMS = ()

    def barrier():
        for nm in SY_SEMS:
            s = sems[nm]
            if s.n > 0:
                plan["sy"].append(lambda e, h=s.h, v=s.n: e.wait_ge(h, v))
        for nm in GP_SEMS:
            s = sems[nm]
            if s.n > 0:
                plan["gp"].append(lambda e, h=s.h, v=s.n: e.wait_ge(h, v))
        for nm in SC_SEMS:
            s = sems[nm]
            if s.n > 0:
                plan["sc"].append(lambda e, h=s.h, v=s.n: e.wait_ge(h, v))
        for nm in VE_SEMS:
            s = sems[nm]
            if s.n > 0:
                plan["ve"].append(lambda e, h=s.h, v=s.n: e.wait_ge(h, v))
        for ek, s in (("te", mm), ("sc", ex), ("ve", vv)):
            if s.n > 0:
                plan[ek].append(lambda e, h=s.h, v=s.n: e.wait_ge(h, v))
        phase_no[0] += 1
        tgt = 5 * phase_no[0]
        for ek in ENGS:
            plan[ek].append(lambda e, s=sems["phb"].h: e.sem_inc(s, 1))
            plan[ek].append(lambda e, s=sems["phb"].h, t=tgt: e.wait_ge(s, t))
        cuts[phase_no[0]] = {ek: len(plan[ek]) for ek in ENGS}

    mm, ex, vv = sems["mm"], sems["ex"], sems["vv"]
    dS = sems["dS"]
    dLn = (sems["dLn0"], sems["dLn1"])
    dO = (sems["dO0"], sems["dO1"])
    dM = (sems["dM0"], sems["dM1"])
    dG0, dG1 = sems["dG0"], sems["dG1"]
    dX = (sems["dX0"], sems["dX1"])
    dA = (sems["dA0"], sems["dA1"])
    dN = (sems["dN0"], sems["dN1"])
    dK = (sems["dK0"], sems["dK1"])
    dSt = (sems["dSt0"], sems["dSt1"])
    dAo = (sems["dAo0"], sems["dAo1"])
    dKr = (sems["dKr0"], sems["dKr1"])
    dGp = (dG0, dG1, sems["dG2"])

    I("sy", dS, 16, lambda e: e.dma_start(
        out=V("DM1").rearrange("p (c q) -> p c q", c=4), in_=dmask.transpose([1, 0, 2])))
    I("sy", dS, 16, lambda e: e.dma_start(
        out=V("PM1").rearrange("p (c q) -> p c q", c=NG), in_=pmask.transpose([1, 0, 2])))
    I("sy", dS, 16, lambda e: e.dma_start(out=V("PMUL"), in_=pmul_in[:, :]))
    I("sy", dS, 16, lambda e: e.dma_start(out=V("ONER", rows=(0, 1)), in_=ones_r[:, :]))
    I("sy", dS, 16, lambda e: e.dma_start(out=V("BQ"), in_=bq_t[:, :]))
    I("sy", dS, 16, lambda e: e.dma_start(out=V("BK"), in_=bk_t[:, :]))
    I("sy", dS, 16, lambda e: e.dma_start(out=V("BV"), in_=bv_t[:, :]))
    I("sy", dS, 16, lambda e: e.dma_start(out=V("BE1"), in_=be1_t[:, :]))
    I("sy", dS, 16, lambda e: e.dma_start(out=V("IDN"), in_=ident_in[:, :]))
    I("sy", dS, 16, lambda e: e.dma_start(out=V("WGF"), in_=wg_t[:, :]))
    I("sy", dS, 16, lambda e: e.dma_start(out=V("FCONST"), in_=fconst[:, :]))
    consts_sy = dS.n
    nc.const_aps.register(F32, float(EPS), V("FCONST", a=0, n=1))
    nc.const_aps.register(F32, float(-M1 * SCALE), V("FCONST", a=1, n=1))
    nc.const_aps.register(F32, float(-M2 * SCALE), V("FCONST", a=2, n=1))
    I("gp", dG0, 16, lambda e: e.dma_start(out=V("COS"), in_=cost[:, :]))
    I("gp", dG0, 16, lambda e: e.dma_start(out=V("SIN"), in_=sint[:, :]))
    I("gp", dG0, 16, lambda e: e.dma_start(out=V("ROT"), in_=rot_t[:, :]))
    I("gp", dG0, 16, lambda e: e.dma_start(
        out=V("ONEB", rows=(0, 1), n=128), in_=ones_r[:, :]))
    I("gp", dG0, 16, lambda e: e.dma_start(
        out=V("OCB"), in_=ones_r.rearrange("o p -> p o")))
    I("gp", dG0, 16, lambda e: e.dma_start(out=V("BOC", rows=(0, 1)), in_=bo_b[:, :]))
    I("gp", dG0, 16, lambda e: e.dma_start(out=V("LW1", rows=(0, 1)), in_=ln1w[:, :]))
    I("gp", dG0, 16, lambda e: e.dma_start(out=V("LB1", rows=(0, 1)), in_=ln1b[:, :]))
    I("gp", dG0, 16, lambda e: e.dma_start(out=V("WG"), in_=wg_t[:, :]))
    consts_gp = dG0.n
    W("te", dS, consts_sy)
    W("te", dG0, consts_gp)
    W("ve", dS, consts_sy)
    W("ve", dG0, consts_gp)
    I("ve", vv, 1, lambda e: e.tensor_tensor(
        V("WGLO"), V("WGF"), V("WG"), ALU.subtract))
    for nm in ("BOC", "LW1", "LB1"):
        for h2 in range(4):
            I("te", mm, 1, lambda e, h2=h2, nm=nm: e.matmul(
                pbk[0][:, :], V("ONEB", rows=(0, 1), n=128),
                V(nm, a=h2 * 512, n=512, rows=(0, 1)), start=True, stop=True))
            W("sc", mm)
            I("sc", ex, 1, lambda e, h2=h2, nm=nm: e.copy(
                V(nm, a=h2 * 512, n=512), pbk[0][:, :]))
            W("te", ex)
    barrier()

    def stats_chain(src_fn, par=0):
        def st(a, n):
            return V("ST", a=32 * par + a, n=n)
        for i in range(4):
            I("ve", vv, 1, lambda e, i=i: e.bn_stats(
                st(6 * i, 6), src_fn()[:, 512 * i:512 * (i + 1)]))
        I("ve", vv, 1, lambda e: e.bn_aggr(st(24, 2), st(0, 24)))
        W("sc", vv)
        I("sc", ex, 1, lambda e: e.activation(
            st(26, 1), st(25, 1), AF.Sqrt, bias=EPS, scale=1.0))
        W("ve", ex)
        I("ve", vv, 1, lambda e: e.reciprocal(st(27, 1), st(26, 1)))
        I("ve", vv, 1, lambda e: e.scalar_tensor_tensor(
            st(28, 1), st(24, 1), -1.0, st(27, 1), ALU.mult, ALU.mult))
        return (lambda: st(27, 1)), (lambda: st(28, 1))

    def rope_emit(kb_ap, cos_a, out_ap):
        W("te", vv)
        W("te", ex)
        I("te", mm, 1, lambda e, kb=kb_ap: e.matmul(
            pbk[4][:, :], V("ROT"), kb, start=True, stop=True))
        rq_mm = mm.n
        cos = V("COS", a=cos_a, n=512)
        sin = V("SIN", a=cos_a, n=512)
        t1 = V("RT1")
        t2 = V("RT2")
        I("ve", vv, 1, lambda e, kb=kb_ap: e.tensor_tensor(t1, kb, cos, ALU.mult))
        W("ve", mm, rq_mm)
        I("ve", vv, 1, lambda e: e.tensor_tensor(t2, pbk[4][:, :], sin, ALU.mult))
        I("ve", vv, 1, lambda e, o=out_ap: e.tensor_tensor(o, t1, t2, ALU.add))

    attn_state = {"blk": 0, "stt": [None, None], "zbc": None, "rz": 0,
                  "zln": [None, None]}

    def emit_attn(s, ext, kf_ap, qf_ap, vs_ap, bias_imm, out_sb, dm="DM1",
                  pm="PM1"):
        par = attn_state["blk"] % 2
        attn_state["blk"] += 1
        AVB = pbk[2] if par == 0 else pbk[4]
        ZB_ = pbk[3] if par == 0 else pbk[6]
        SB3 = (pbk[0], pbk[1], pbk[5])
        M0 = ext - 8
        W("te", vv, attn_state["stt"][par])
        W("te", ex, attn_state["zln"][par])
        exp_base = ex.n
        pt_ready = {}

        def emit_av(c):
            sb = c % 3
            kind, cnt = pt_ready[c]
            W("te", ex if kind == "ex" else vv, cnt)
            I("te", mm, 1, lambda e, c=c, sb=sb, last=(c == ext - 1): e.matmul(
                AVB[:, :], vs_ap[:, c * 128:(c + 1) * 128],
                V("PT", a=sb * 512, n=512), start=(c == 0), stop=last))
            I("te", mm, 1, lambda e, c=c, sb=sb, last=(c == ext - 1): e.matmul(
                ZB_[0:1, :], V("OCB"), V("PT", a=sb * 512, n=512),
                start=(c == 0), stop=last))

        for c in range(ext + 3):
            if c >= 3:
                emit_av(c - 3)
            if c >= ext:
                continue
            sb = c % 3
            if c >= 3:
                W("te", ex, exp_base + (c - 2))
            I("te", mm, 1, lambda e, c=c, sb=sb: e.matmul(
                SB3[sb][:, :], kf_ap[:, c * 128:(c + 1) * 128], qf_ap,
                start=True, stop=True))
            mm_after_s = mm.n
            ci = c - M0
            diag = ci >= 0 and ((s % 2 == 0) == (ci < 4))
            partner = ci >= 0 and not diag
            if diag:
                di = ci if ci < 4 else ci - 4
                W("ve", mm, mm_after_s)
                I("ve", vv, 1, lambda e, sb=sb, di=di: e.tensor_tensor(
                    SB3[sb][:, :], SB3[sb][:, :], V(dm, a=di * 512, n=512),
                    ALU.add))
                W("sc", vv)
            else:
                W("sc", mm, mm_after_s)
            I("sc", ex, 1, lambda e, sb=sb: e.activation(
                V("PT", a=sb * 512, n=512), SB3[sb][:, :], AF.Exp,
                bias=bias_imm, scale=SCALE), chain=False)
            if partner:
                W("ve", ex)
                I("ve", vv, 1, lambda e, sb=sb, s=s: e.tensor_scalar_mul(
                    V("PT", a=sb * 512, n=512), V("PT", a=sb * 512, n=512),
                    V("PMUL", a=s, n=1)))
                pt_ready[c] = ("ve", vv.n)
            else:
                pt_ready[c] = ("ex", ex.n)
        z_done = mm.n

        rzpar = attn_state["rz"] % 2
        attn_state["rz"] += 1
        zt = V("RZ", a=rzpar * 1024, n=512, rows=(0, 1))
        rz = V("RZ", a=rzpar * 1024 + 512, n=512, rows=(0, 1))
        W("sc", mm, z_done)
        I("sc", ex, 1, lambda e, zt=zt, Z=ZB_: e.activation(
            zt, Z[0:1, :], AF.Ln, bias=0.0, scale=1.0), chain=False)
        I("sc", ex, 1, lambda e, zt=zt, rz=rz: e.activation(
            rz, zt, AF.Exp, bias=0.0, scale=-1.0))
        attn_state["zln"][par] = ex.n
        recip_done = ex.n
        avn = V("AVN", a=par * 512, n=512)
        W("ve", mm, z_done)
        I("ve", vv, 1, lambda e, avn=avn, A=AVB: e.tensor_copy(avn, A[:, :]),
          chain=False)
        attn_state["stt"][par] = vv.n

        ptf = pt_bf[:, 0:1024].bitcast(F32)

        def epilogue():
            W("te", ex, attn_state["zbc"])
            W("te", ex, recip_done)
            I("te", mm, 1, lambda e, rz=rz, ptf=ptf: e.matmul(
                ptf, V("ONER", rows=(0, 1), n=128), rz,
                start=True, stop=True))
            W("sc", mm)
            zb = V("ZB", a=(par % 2) * 512, n=512)
            I("sc", ex, 1, lambda e, zb=zb, ptf=ptf: e.copy(zb, ptf))
            attn_state["zbc"] = ex.n
            W("ve", ex)
            I("ve", vv, 1, lambda e, zb=zb, o=out_sb, avn=avn:
                e.scalar_tensor_tensor(o, avn, 1.0, zb, ALU.mult, ALU.mult))
        return epilogue

    x0_loads = [0, 0]
    x0q_loads = [0, 0, 0, 0]
    n1q_free = [None, None, None, None]
    n1t_store = [None, None, None, None]
    dXq = (dX[0], dX[1], sems["dX2"], sems["dX3"])
    dNq = (dN[0], dN[1], sems["dAo0"], sems["dAo1"])
    for tt in range(NT):
        pp = tt % 4
        W("sy", ex, n1q_free[pp])
        I("sy", dXq[pp], 16, lambda e, tt=tt, pp=pp: e.dma_start(
            out=V("X0Q", a=pp * D, n=D), in_=x0p[tt * 128:(tt + 1) * 128, :]))
        x0q_loads[pp] += 1
        W("ve", dXq[pp], 16 * x0q_loads[pp])
        rstd, negmur = stats_chain(lambda pp=pp: V("X0Q", a=pp * D, n=D),
                                   par=pp % 2)
        W("sc", vv)
        W("sc", dNq[pp], n1t_store[pp])
        I("sc", ex, 1, lambda e, pp=pp, r=rstd, nm=negmur: e.activation(
            V("N1T", a=pp * D, n=D), V("X0Q", a=pp * D, n=D), AF.Identity,
            bias=nm(), scale=r()))
        n1q_free[pp] = ex.n
        W("sy", ex, ex.n)
        I("sy", dNq[pp], 16, lambda e, tt=tt, pp=pp: e.dma_start(
            out=n1_d[tt * 128:(tt + 1) * 128, :], in_=V("N1T", a=pp * D, n=D)))
        n1t_store[pp] = dNq[pp].n
    barrier()

    kv_loads = [0, 0]
    head_free = [None, None]
    ao_store = [None, None]
    pend_epi = [None]
    pend_store = [lambda: None]
    blk = 0
    for h in range(DBG_H1 if DBG_H1 else H):
        pp = h % 2
        W("sy", mm, head_free[pp])
        I("sy", dK[pp], 16, lambda e, h=h, pp=pp: e.dma_start(
            out=V("K1F", a=pp * L, n=L), in_=n1_d[:, h * 128:(h + 1) * 128],
            transpose=True))
        I("sy", dK[pp], 16, lambda e, h=h, pp=pp: e.dma_start(
            out=V("V1S", a=pp * L, n=L).rearrange("p (kc i) -> p kc i", kc=32),
            in_=n1_d[:, h * 128:(h + 1) * 128].rearrange("(kc p) i -> kc p i", p=128).transpose([1, 0, 2])))
        kv_loads[pp] += 2
        W("te", dK[pp], 16 * kv_loads[pp])
        for s in range(NG):
            ob = blk % 2
            W("ve", dAo[ob], ao_store[ob])
            epi = emit_attn(s, EXT[s], V("K1F", a=pp * L, n=L),
                            V("K1F", a=pp * L + s * 512, n=512),
                            V("V1S", a=pp * L, n=L), -M1 * SCALE,
                            V("AO", a=ob * 512, n=512))
            if pend_epi[0] is not None:
                pend_epi[0]()
                pend_store[0]()
            pend_epi[0] = epi

            def mk_store(h=h, s=s, ob=ob):
                def st():
                    W("sy", vv)
                    I("sy", dAo[ob], 16, lambda e: e.dma_start(
                        out=a1_d[h * 128:(h + 1) * 128, s * 512:(s + 1) * 512],
                        in_=V("AO", a=ob * 512, n=512)))
                    ao_store[ob] = dAo[ob].n
                return st
            pend_store[0] = mk_store()
            blk += 1
        head_free[pp] = mm.n
    pend_epi[0]()
    pend_store[0]()
    pend_epi[0] = None
    barrier()

    a1_loads = [0, 0]
    x0t_free = [None, None]
    x1t_free = [None, None]
    hsb_store = [None, None]
    ln1_store = [None, None]
    x1_stores = [None, None]
    kro_store = [None, None]
    kt_free = [None, None]
    wband_free = [None, None, None]
    wband_mm = [None, None, None]
    evac_ex = [None, None, None, None]

    def emit_tile(tt):
        pp = tt % 2
        W("sy", vv, x0t_free[pp])
        I("sy", dX[pp], 16, lambda e, tt=tt, pp=pp: e.dma_start(
            out=V("X0T", a=pp * D, n=D), in_=x0p[tt * 128:(tt + 1) * 128, :]))
        x0_loads[pp] += 1
        a1s = (dA[0], dA[1], dAo[1], sems["dV0"])[tt % 4]
        I("sy", a1s, 16, lambda e, tt=tt, pp=pp: e.dma_start(
            out=V("A1T", a=pp * D, n=D), in_=a1_d[:, tt * 128:(tt + 1) * 128],
            transpose=True))
        a1_loads[pp] += 1
        W("ve", dX[pp], dX[pp].n)
        W("ve", a1s, a1s.n)
        W("ve", vv, x1t_free[pp])
        W("ve", dSt[pp], x1_stores[pp])
        I("ve", vv, 1, lambda e, pp=pp: e.tensor_tensor(
            V("X1T", a=pp * D, n=D), V("X0T", a=pp * D, n=D),
            V("A1T", a=pp * D, n=D), ALU.add))
        x0t_free[pp] = vv.n
        own = (tt // 4) % 2 == 1
        if own:
            grp = tt // 4
            j = (grp - 1) // 2
            orow = j * 512 + (tt % 4) * 128
            W("sy", vv)
            I("sy", dSt[pp], 16, lambda e, orow=orow, pp=pp: e.dma_start(
                out=x1_d[orow:orow + 128, :], in_=V("X1T", a=pp * D, n=D)))
            x1_stores[pp] = dSt[pp].n
        rstd, negmur = stats_chain(lambda pp=pp: V("X1T", a=pp * D, n=D), par=pp)
        W("ve", sems["gpv"], sems["gpv"].n)
        I("ve", vv, 1, lambda e, pp=pp, r=rstd, nm=negmur: e.tensor_scalar(
            V("HST"), V("X1T", a=pp * D, n=D), r(), nm(), ALU.mult, ALU.add))
        x1t_free[pp] = vv.n
        W("gp", vv)
        W("gp", dN[pp], hsb_store[pp])
        I("gp", sems["gpv"], 1, lambda e, pp=pp: e.tensor_tensor(
            V("HSB", a=pp * D, n=D), V("HST"), V("BOC"), ALU.add))
        W("sy", sems["gpv"], sems["gpv"].n)
        I("sy", dN[pp], 16, lambda e, tt=tt, pp=pp: e.dma_start(
            out=hsbo_d[tt * 128:(tt + 1) * 128, :], in_=V("HSB", a=pp * D, n=D)))
        hsb_store[pp] = dN[pp].n
        sb0 = 32 * pp
        I("ve", vv, 1, lambda e, sb0=sb0: e.tensor_scalar(
            V("ST", a=sb0, n=1), V("ST", a=sb0 + 25, n=1), 1.0 + EPS, EPS * EPS,
            ALU.mult, ALU.add))
        W("sc", vv)
        I("sc", ex, 1, lambda e, sb0=sb0: e.activation(
            V("ST", a=sb0 + 1, n=1), V("ST", a=sb0, n=1), AF.Sqrt,
            bias=0.0, scale=1.0))
        W("ve", ex)
        I("ve", vv, 1, lambda e, sb0=sb0: e.reciprocal(
            V("ST", a=sb0 + 2, n=1), V("ST", a=sb0 + 1, n=1)))
        I("ve", vv, 1, lambda e, sb0=sb0: e.tensor_tensor(
            V("ST", a=sb0 + 3, n=1), V("ST", a=sb0 + 26, n=1),
            V("ST", a=sb0 + 2, n=1), ALU.mult))
        I("ve", vv, 1, lambda e, sb0=sb0: e.tensor_scalar(
            V("TMP1"), V("HST"), V("ST", a=sb0 + 3, n=1), None, ALU.mult))
        W("gp", vv)
        I("gp", sems["gpv"], 1, lambda e: e.tensor_tensor(
            V("TMP2"), V("TMP1"), V("LW1"), ALU.mult))
        W("ve", sems["gpv"], sems["gpv"].n)
        W("ve", dLn[pp], ln1_store[pp])
        I("ve", vv, 1, lambda e, pp=pp: e.tensor_tensor(
            V("LN1T", a=pp * D, n=D), V("TMP2"), V("LB1"), ALU.add))
        W("sy", vv)
        I("sy", dLn[pp], 16, lambda e, tt=tt, pp=pp: e.dma_start(
            out=ln1_d[tt * 128:(tt + 1) * 128, :], in_=V("LN1T", a=pp * D, n=D)))
        ln1_store[pp] = dLn[pp].n

    def emit_l1f_loads(half):
        l1sems = (dS, dK[0], dK[1], dAo[0])
        for fc in range(16):
            I("sy", l1sems[fc % 4], 16, lambda e, fc=fc, half=half: e.dma_start(
                out=V("L1F", a=fc * OWN, n=OWN),
                in_=ln1_d[half * OWN:(half + 1) * OWN, fc * 128:(fc + 1) * 128],
                transpose=True))
        for sm in l1sems:
            W("te", sm, sm.n)

    def emit_proj_oc(wt, bias_name, tgs, rope, dst, half, oc):
        pp2 = oc % 3
        W("gp", ex, wband_free[pp2])
        W("gp", mm, wband_mm[pp2])
        I("gp", dGp[pp2], 16, lambda e, oc=oc, pp2=pp2, wt=wt: e.dma_start(
            out=V("WB", a=pp2 * D, n=D),
            in_=wt[oc].rearrange("p e i -> p (e i)")))
        W("te", dGp[pp2], dGp[pp2].n)
        for ec in range(16):
            for bi, tg in enumerate(tgs):
                if ec == 0:
                    W("te", ex, evac_ex[bi])
                I("te", mm, 1, lambda e, ec=ec, bi=bi, tg=tg, pp2=pp2:
                    e.matmul(pbk[bi][:, :],
                             V("WB", a=pp2 * D + ec * 128, n=128),
                             V("L1F", a=ec * OWN + tg * 512, n=512),
                             start=(ec == 0), stop=(ec == 15)))
        mm_done = mm.n
        wband_mm[pp2] = mm.n
        for bi, tg in enumerate(tgs):
            kb = (oc * len(tgs) + bi) % 2
            W("sc", mm, mm_done)
            evac_mark = True
            if rope:
                W("sc", vv, kt_free[kb])
                I("sc", ex, 1, lambda e, bi=bi, kb=kb, oc=oc: e.activation(
                    V("KB", a=kb * 512, n=512), pbk[bi][:, :], AF.Identity,
                    bias=V(bias_name, a=oc, n=1), scale=1.0))
                evac_ex[bi] = ex.n
                W("ve", ex)
                W("ve", dKr[kb], kro_store[kb])
                rope_emit(V("KB", a=kb * 512, n=512),
                          half * OWN + tg * 512,
                          V("KRO", a=kb * 512, n=512))
                kt_free[kb] = vv.n
                W("sy", vv)
            else:
                W("sc", dKr[kb], kro_store[kb])
                I("sc", ex, 1, lambda e, bi=bi, kb=kb, oc=oc: e.activation(
                    V("KRO", a=kb * 512, n=512), pbk[bi][:, :], AF.Identity,
                    bias=V(bias_name, a=oc, n=1), scale=1.0))
                evac_ex[bi] = ex.n
                W("sy", ex)
            if dst is q2_d:
                col = (2 * half + (1 if tg == 3 else 0)) * 512
            else:
                col = half * OWN + tg * 512
            I("sy", dKr[kb], 16, lambda e, oc=oc, col=col, kb=kb, dst=dst:
                e.dma_start(
                    out=dst[oc * 128:(oc + 1) * 128, col:col + 512],
                    in_=V("KRO", a=kb * 512, n=512)))
            kro_store[kb] = dKr[kb].n
        wband_free[pp2] = ex.n

    def proj_units(half):
        sel = DBG_PROJ if DBG_PROJ is not None else "kvq"
        units = []
        noc = DBG_OC if DBG_OC else 16
        if "k" in sel:
            for oc in range(noc):
                units.append(lambda oc=oc, h=half: emit_proj_oc(
                    wk_t, "BK", [0, 1, 2, 3], True, k2_d, h, oc))
        if "v" in sel:
            for oc in range(noc):
                units.append(lambda oc=oc, h=half: emit_proj_oc(
                    wv_t, "BV", [0, 1, 2, 3], False, v2_d, h, oc))
        if "q" in sel:
            for oc in range(noc):
                units.append(lambda oc=oc, h=half: emit_proj_oc(
                    wq_t, "BQ", [1, 3], True, q2_d, h, oc))
        return units

    nt3 = DBG_T16 if DBG_T16 else 16
    for t16 in range(nt3):
        emit_tile(t16)
    barrier()
    emit_l1f_loads(0)
    units0 = proj_units(0)
    ti = 0
    for i, u in enumerate(units0):
        u()
        if i % 3 == 2 and ti < nt3:
            emit_tile(16 + ti)
            ti += 1
    while ti < nt3:
        emit_tile(16 + ti)
        ti += 1
    barrier()
    emit_l1f_loads(1)
    for u in proj_units(1):
        u()
    barrier()

    I("sy", dS, 16, lambda e: e.dma_start(
        out=V("DM2").rearrange("p (c q) -> p c q", c=4), in_=dmask.transpose([1, 0, 2])))
    I("sy", dS, 16, lambda e: e.dma_start(
        out=V("PM2").rearrange("p (c q) -> p c q", c=NG), in_=pmask.transpose([1, 0, 2])))
    dm2_ready = dS.n
    W("te", dS, dm2_ready)
    W("ve", dS, dm2_ready)
    kv2_loads = [0, 0]
    head_free2 = [None, None]
    for h in range(DBG_H2 if DBG_H2 else H):
        pp = h % 2
        W("sy", mm, head_free2[pp])
        I("sy", dK[pp], 16, lambda e, h=h, pp=pp: e.dma_start(
            out=V("K2F", a=pp * L, n=L), in_=k2_d[h * 128:(h + 1) * 128, :]))
        I("sy", dK[pp], 16, lambda e, h=h, pp=pp: e.dma_start(
            out=V("V2F", a=pp * L, n=L), in_=v2_d[h * 128:(h + 1) * 128, :]))
        I("sy", dK[pp], 16, lambda e, h=h, pp=pp: e.dma_start(
            out=V("Q2F", a=pp * OWN, n=OWN), in_=q2_d[h * 128:(h + 1) * 128, :]))
        kv2_loads[pp] += 3
        W("te", dK[pp], dK[pp].n)
        W("te", ex, attn_state["zbc"])
        pb0_bf = pbk[0][:, 0:64].bitcast(BF16)
        cpv = {}
        for kc in range(32):
            if kc >= 2:
                W("te", vv, cpv[kc - 2])
            I("te", mm, 1, lambda e, pp=pp, kc=kc: e.transpose(
                pt_bf[:, 0:128] if kc % 2 == 0 else pb0_bf,
                V("V2F", a=pp * L + kc * 128, n=128), V("IDN")))
            W("ve", mm, mm.n)
            I("ve", vv, 1, lambda e, pp=pp, kc=kc: e.tensor_copy(
                V("V2S", a=pp * L + kc * 128, n=128),
                pt_bf[:, 0:128] if kc % 2 == 0 else pb0_bf),
              chain=False)
            cpv[kc] = vv.n
        v2s_ready = vv.n
        W("te", vv, v2s_ready)
        for j, s in enumerate(OWN_SLOTS):
            epi = emit_attn(s, EXT[s], V("K2F", a=pp * L, n=L),
                            V("Q2F", a=pp * OWN + j * 512, n=512),
                            V("V2S", a=pp * L, n=L), -M2 * SCALE,
                            V("A2P", a=h * OWN + j * 512, n=512),
                            dm="DM2", pm="PM2")
            if pend_epi[0] is not None:
                pend_epi[0]()
            pend_epi[0] = epi
        head_free2[pp] = mm.n
    pend_epi[0]()
    pend_epi[0] = None
    barrier()

    wo_sems = (sems["dWO"], dG0, dG1, sems["dG2"])
    for dc in range(16):
        I("gp", wo_sems[dc % 4], 16, lambda e, dc=dc: e.dma_start(
            out=V("WO", a=dc * D, n=D), in_=wo_t[dc * 128:(dc + 1) * 128, :]))
    wo_snap = [sm.n for sm in wo_sems]
    I("gp", dG0, 16, lambda e: e.dma_start(out=V("LW2", rows=(0, 1)), in_=ln2w[:, :]))
    I("gp", dG0, 16, lambda e: e.dma_start(out=V("LB2", rows=(0, 1)), in_=ln2b[:, :]))
    W("te", dG0, dG0.n)
    for nm in ("LW2", "LB2"):
        for h2 in range(4):
            I("te", mm, 1, lambda e, h2=h2, nm=nm: e.matmul(
                pbk[0][:, :], V("ONEB", rows=(0, 1), n=128),
                V(nm, a=h2 * 512, n=512, rows=(0, 1)), start=True, stop=True))
            W("sc", mm)
            I("sc", ex, 1, lambda e, h2=h2, nm=nm: e.copy(
                V(nm, a=h2 * 512, n=512), pbk[0][:, :]))
            W("te", ex)
    for sm, v in zip(wo_sems, wo_snap):
        W("te", sm, v)
    hsb_loads = [0, 0]
    ln2_store = [None, None]
    hsbt_free = [None, None]
    ht_free = [None, None]
    h_stores = [None, None]
    for ot in range(DBG_OT if DBG_OT else ONT):
        pp = ot % 2
        gt = OT2G[ot]
        W("sy", vv, hsbt_free[pp])
        I("sy", dA[pp], 16, lambda e, gt=gt, pp=pp: e.dma_start(
            out=V("HSBT", a=pp * D, n=D), in_=hsbo_d[gt * 128:(gt + 1) * 128, :]))
        hsb_loads[pp] += 1
        hs_ready = dA[pp].n
        for dc in range(16):
            for eg in range(4):
                I("te", mm, 1, lambda e, dc=dc, eg=eg, ot=ot: e.matmul(
                    pbk[eg][:, :], V("A2P", a=dc * D + ot * 128, n=128),
                    V("WO", a=dc * D + eg * 512, n=512),
                    start=(dc == 0), stop=(dc == 15)))
        mm_done = mm.n
        W("ve", mm, mm_done)
        W("ve", dA[pp], hs_ready)
        W("ve", ex, ht_free[pp])
        W("ve", dSt[pp], h_stores[pp])
        for eg in range(4):
            I("ve", vv, 1, lambda e, eg=eg, pp=pp: e.tensor_tensor(
                V("HT", a=pp * D + eg * 512, n=512), pbk[eg][:, :],
                V("HSBT", a=pp * D + eg * 512, n=512), ALU.add))
        hsbt_free[pp] = vv.n
        W("te", vv)
        W("sy", vv)
        I("sy", dSt[pp], 16, lambda e, ot=ot, pp=pp: e.dma_start(
            out=h_d[ot * 128:(ot + 1) * 128, :], in_=V("HT", a=pp * D, n=D)))
        h_stores[pp] = dSt[pp].n
        rstd3, negmur3 = stats_chain(lambda pp=pp: V("HT", a=pp * D, n=D), par=pp)
        W("sc", vv)
        W("sc", dN[pp], ln2_store[pp])
        I("sc", ex, 1, lambda e, pp=pp, r=rstd3, nm=negmur3: e.activation(
            V("TMP1"), V("HT", a=pp * D, n=D), AF.Identity, bias=nm(), scale=r()))
        ht_free[pp] = ex.n
        W("ve", ex)
        I("ve", vv, 1, lambda e: e.tensor_tensor(
            V("TMP2"), V("TMP1"), V("LW2"), ALU.mult))
        I("ve", vv, 1, lambda e: e.tensor_tensor(
            V("TMP1"), V("TMP2"), V("LB2"), ALU.add))
        I("ve", vv, 1, lambda e, pp=pp: e.tensor_copy(
            V("LN2T", a=pp * D, n=D), V("TMP1")))
        I("ve", vv, 1, lambda e, pp=pp: e.tensor_tensor(
            V("LN2LO", a=pp * D, n=D), V("TMP1"), V("LN2T", a=pp * D, n=D),
            ALU.subtract))
        W("sy", vv)
        I("sy", dN[pp], 16, lambda e, ot=ot, pp=pp: e.dma_start(
            out=ln2h_d[ot * 128:(ot + 1) * 128, :], in_=V("LN2T", a=pp * D, n=D)))
        I("sy", dN[pp], 16, lambda e, ot=ot, pp=pp: e.dma_start(
            out=ln2lo_d[ot * 128:(ot + 1) * 128, :], in_=V("LN2LO", a=pp * D, n=D)))
        ln2_store[pp] = dN[pp].n
    barrier()

    w1e_snap = {}
    we2_snap = {}

    def emit_w1e_loads(e4):
        for ic in range(16):
            sm = sems["dW1a"] if ic % 2 == 0 else sems["dW1b"]
            I("gp", sm, 16, lambda e, e4=e4, ic=ic: e.dma_start(
                out=V("W1E", a=ic * D, n=D),
                in_=we1_t[e4, ic].rearrange("p f i -> p (f i)")))
        w1e_snap[e4] = (sems["dW1a"].n, sems["dW1b"].n)

    def emit_we2_loads(e4, crow=True):
        for ic in range(16):
            sm = sems["dWE2"] if ic % 2 == 0 else sems["dWE2b"]
            I("gp", sm, 16, lambda e, e4=e4, ic=ic: e.dma_start(
                out=V("WE2", a=ic * D, n=D),
                in_=we2_t[e4, ic * 128:(ic + 1) * 128, :]))
        I("gp", sems["dWE2"], 16, lambda e, e4=e4: e.dma_start(
            out=V("BE2", rows=(0, 1)), in_=be2_t[e4:e4 + 1, :]))
        if crow:
            I("gp", sems["dWE2"], 16, lambda e, e4=e4: e.dma_start(
                out=V("CROW1", rows=(0, 1)), in_=crow_d[e4:e4 + 1, :]))
        we2_snap[e4] = (sems["dWE2"].n, sems["dWE2b"].n)

    emit_w1e_loads(0)
    emit_we2_loads(0, crow=False)
    crs_store = [None, None, None, None]
    for sl in range(DBG_SL if DBG_SL else 4):
        pp = sl % 2
        if sl >= 1:
            W("sy", mm)
        for fc in range(16):
            I("sy", dK[pp], 16, lambda e, sl=sl, fc=fc, pp=pp: e.dma_start(
                out=V("L2F", a=pp * 16 * 512 + fc * 512, n=512),
                in_=ln2h_d[sl * 512:(sl + 1) * 512, fc * 128:(fc + 1) * 128],
                transpose=True))
            I("sy", dLn[pp], 16, lambda e, sl=sl, fc=fc, pp=pp: e.dma_start(
                out=V("L2FLO", a=fc * 512, n=512),
                in_=ln2lo_d[sl * 512:(sl + 1) * 512, fc * 128:(fc + 1) * 128],
                transpose=True))
        kv2_loads[pp] += 16
        W("te", dK[pp], dK[pp].n)
        W("te", dLn[pp], dLn[pp].n)
        for twi in range(4):
            tt = sl * 4 + twi
            W("te", ex)
            for fc in range(16):
                I("te", mm, 1, lambda e, fc=fc, twi=twi, pp=pp: e.matmul(
                    pbk[4][:, 0:4],
                    V("L2F", a=pp * 16 * 512 + fc * 512 + twi * 128, n=128),
                    V("WG", a=fc * 4, n=4), start=(fc == 0), stop=False))
                I("te", mm, 1, lambda e, fc=fc, twi=twi, pp=pp: e.matmul(
                    pbk[4][:, 0:4],
                    V("L2F", a=pp * 16 * 512 + fc * 512 + twi * 128, n=128),
                    V("WGLO", a=fc * 4, n=4), start=False, stop=False))
                I("te", mm, 1, lambda e, fc=fc, twi=twi, pp=pp: e.matmul(
                    pbk[4][:, 0:4],
                    V("L2FLO", a=fc * 512 + twi * 128, n=128),
                    V("WG", a=fc * 4, n=4), start=False,
                    stop=(fc == 15)))
            gmm = mm.n
            W("ve", mm, gmm)
            I("ve", vv, 1, lambda e: e.reduce_max(
                V("ST", a=32, n=1), pbk[4][:, 0:4], axis=AX.X))
            I("ve", vv, 1, lambda e: e.tensor_scalar(
                V("ST", a=33, n=1), V("ST", a=32, n=1), -1.0, None, ALU.mult))
            W("sc", vv)
            I("sc", ex, 1, lambda e: e.activation(
                V("GE4", a=0, n=4), pbk[4][:, 0:4], AF.Exp,
                bias=V("ST", a=33, n=1), scale=1.0, accum_out=V("ST", a=34, n=1)))
            W("ve", ex)
            I("ve", vv, 1, lambda e: e.reciprocal(V("ST", a=35, n=1), V("ST", a=34, n=1)))
            I("ve", vv, 1, lambda e: e.tensor_scalar(
                V("GRW", a=0, n=4), V("GE4", a=0, n=4), V("ST", a=35, n=1),
                None, ALU.mult))
            I("ve", vv, 1, lambda e: e.reduce_max(
                V("ST", a=36, n=1), V("GRW", a=0, n=4), axis=AX.X))
            I("ve", vv, 1, lambda e: e.tensor_scalar(
                V("GRW", a=4, n=4), V("GRW", a=0, n=4), V("ST", a=36, n=1),
                None, ALU.is_lt))
            I("ve", vv, 1, lambda e: e.tensor_tensor(
                V("GRW", a=8, n=4), V("GRW", a=0, n=4), V("GRW", a=4, n=4), ALU.mult))
            I("ve", vv, 1, lambda e: e.reduce_max(
                V("ST", a=37, n=1), V("GRW", a=8, n=4), axis=AX.X))
            I("ve", vv, 1, lambda e: e.tensor_scalar(
                V("GRW", a=4, n=4), V("GRW", a=0, n=4), V("ST", a=37, n=1),
                None, ALU.is_ge))
            W("sc", vv)
            I("sc", ex, 1, lambda e: e.activation(
                V("ST", a=38, n=1), V("ST", a=36, n=1), AF.Exp, bias=0.0, scale=1.0))
            I("sc", ex, 1, lambda e: e.activation(
                V("ST", a=39, n=1), V("ST", a=37, n=1), AF.Exp, bias=0.0, scale=1.0))
            I("sc", ex, 1, lambda e: e.activation(
                V("GE4", a=4, n=4), V("GRW", a=0, n=4), AF.Exp, bias=0.0, scale=1.0))
            W("ve", ex)
            I("ve", vv, 1, lambda e: e.tensor_tensor(
                V("ST", a=40, n=1), V("ST", a=38, n=1), V("ST", a=39, n=1), ALU.add))
            I("ve", vv, 1, lambda e: e.reciprocal(V("ST", a=41, n=1), V("ST", a=40, n=1)))
            I("ve", vv, 1, lambda e: e.tensor_tensor(
                V("GRW", a=12, n=4), V("GE4", a=4, n=4), V("GRW", a=4, n=4), ALU.mult))
            I("ve", vv, 1, lambda e: e.tensor_scalar(
                V("CMB"), V("GRW", a=12, n=4), V("ST", a=41, n=1), None, ALU.mult))
            W("te", vv)
            for e4 in range(NE):
                I("te", mm, 1, lambda e, e4=e4: e.transpose(
                    pt_bf[0:1, e4 * 128:(e4 + 1) * 128], V("CMB", a=e4, n=1),
                    V("IDN")))
            W("ve", mm)
            for e4 in range(NE):
                W("ve", sems["dCr"], crs_store[e4])
                I("ve", vv, 1, lambda e, e4=e4: e.tensor_copy(
                    V("CROWS", a=e4 * 128, n=128, rows=(0, 1)),
                    pt_bf[0:1, e4 * 128:(e4 + 1) * 128]))
                W("sy", vv)
                I("sy", sems["dCr"], 16, lambda e, e4=e4, tt=tt: e.dma_start(
                    out=crow_d[e4:e4 + 1, tt * 128:(tt + 1) * 128],
                    in_=V("CROWS", a=e4 * 128, n=128, rows=(0, 1))))
                crs_store[e4] = sems["dCr"].n
            W("te", vv)
    barrier()

    gelu_ex = [None, None]
    mst_free = [None, None]
    mp_used = [None, None]
    m_prev_expert = [0, 0]
    vM = (sems["vM0"], sems["vM1"])
    n_exp = DBG_E if DBG_E else NE
    for e4 in range(n_exp):
        if e4 == 0:
            I("gp", sems["dWE2"], 16, lambda e: e.dma_start(
                out=V("CROW1", rows=(0, 1)), in_=crow_d[0:1, :]))
            we2_snap[0] = (sems["dWE2"].n, we2_snap[0][1])
        for sl in range(DBG_SL if DBG_SL else 4):
            pp = sl % 2
            if e4 > 0 or sl >= 2:
                W("sy", mm)
            for fc in range(16):
                I("sy", dK[pp] if fc % 2 == 0 else dLn[pp], 16,
                  lambda e, sl=sl, fc=fc, pp=pp: e.dma_start(
                    out=V("L2F", a=pp * 16 * 512 + fc * 512, n=512),
                    in_=ln2h_d[sl * 512:(sl + 1) * 512, fc * 128:(fc + 1) * 128],
                    transpose=True))
            kv2_loads[pp] += 8
            l2f_ready = dK[pp].n
            l2b_ready = dLn[pp].n
            if e4 > 0 and sl == 0:
                W("sy", vM[0], m_prev_expert[0])
                W("sy", vM[1], m_prev_expert[1])
            W("te", sems["dWE2"], we2_snap[e4][0])
            W("te", sems["dWE2b"], we2_snap[e4][1])
            if sl == 0:
                W("te", sems["dW1a"], w1e_snap[e4][0])
                W("te", sems["dW1b"], w1e_snap[e4][1])
            W("te", vv)
            I("te", mm, 1, lambda e, e4=e4, sl=sl: e.matmul(
                pbk[5][:, :], V("ONEB", rows=(0, 1), n=128),
                V("CROW1", a=sl * 512, n=512, rows=(0, 1)),
                start=True, stop=True))
            W("sc", mm)
            I("sc", ex, 1, lambda e: e.copy(V("CB"), pbk[5][:, :]))
            cb_ready = ex.n
            for ic in range(16):
                bi = ic % 2
                W("te", dK[pp], l2f_ready)
                W("te", dLn[pp], l2b_ready)
                W("te", ex, gelu_ex[bi])
                for fc in range(16):
                    I("te", mm, 1, lambda e, fc=fc, bi=bi, pp=pp,
                      e4=e4, sl=sl, ic=ic: e.matmul(
                        pbk[bi][:, :], V("W1E", a=ic * D + fc * 128, n=128),
                        V("L2F", a=pp * 16 * 512 + fc * 512, n=512),
                        start=(fc == 0), stop=(fc == 15)).annotate(
                            f"mm1 e{e4} sl{sl} ic{ic} fc{fc}"))
                W("sc", mm)
                W("sc", vv)
                I("sc", ex, 1, lambda e, bi=bi, ic=ic, e4=e4: e.activation(
                    V("GT"), pbk[bi][:, :], AF.Gelu,
                    bias=V("BE1", a=ic * 4 + e4, n=1), scale=1.0))
                gelu_ex[bi] = ex.n
                W("ve", ex)
                W("ve", ex, cb_ready)
                I("ve", vv, 1, lambda e, ic=ic: e.tensor_tensor(
                    V("HMW", a=ic * 512, n=512), V("GT"), V("CB"), ALU.mult))
            if sl == (DBG_SL if DBG_SL else 4) - 1 and e4 + 1 < n_exp:
                W("gp", mm, mm.n)
                emit_w1e_loads(e4 + 1)
            hm_ready = vv.n
            W("te", vv, hm_ready)
            for twi in range(4):
                tt = sl * 4 + twi
                for ic in range(16):
                    for dg in range(4):
                        I("te", mm, 1, lambda e, ic=ic, dg=dg, twi=twi: e.matmul(
                            pbk[dg][:, :], V("HMW", a=ic * 512 + twi * 128, n=128),
                            V("WE2", a=ic * D + dg * 512, n=512),
                            start=(ic == 0), stop=False))
                for dg in range(4):
                    I("te", mm, 1, lambda e, dg=dg, e4=e4, tt=tt: e.matmul(
                        pbk[dg][:, :],
                        V("CROW1", a=tt * 128, n=128, rows=(0, 1)),
                        V("BE2", a=dg * 512, n=512, rows=(0, 1)),
                        start=False, stop=True))
                m2done = mm.n
                for dg in range(4):
                    mb = (tt * 4 + dg) % 2
                    if e4 > 0:
                        W("sy", vv, mp_used[mb])
                        I("sy", dM[mb], 16, lambda e, tt=tt, dg=dg, mb=mb:
                            e.dma_start(
                                out=V("MP", a=mb * 512, n=512),
                                in_=m_d[tt * 128:(tt + 1) * 128,
                                        dg * 512:(dg + 1) * 512]))
                    W("ve", mm, m2done)
                    W("ve", vM[mb], mst_free[mb])
                    if e4 == 0:
                        I("ve", vv, 1, lambda e, dg=dg, mb=mb: e.tensor_copy(
                            V("MST", a=mb * 512, n=512), pbk[dg][:, :]))
                    else:
                        W("ve", dM[mb], dM[mb].n)
                        I("ve", vv, 1, lambda e, dg=dg, mb=mb: e.tensor_tensor(
                            V("MST", a=mb * 512, n=512), pbk[dg][:, :],
                            V("MP", a=mb * 512, n=512), ALU.add))
                        mp_used[mb] = vv.n
                    W("sc", vv)
                    I("sc", vM[mb], 16, lambda e, tt=tt, dg=dg, mb=mb: e.dma_start(
                        out=m_d[tt * 128:(tt + 1) * 128, dg * 512:(dg + 1) * 512],
                        in_=V("MST", a=mb * 512, n=512)),
                      chain=False, record=False)
                    mst_free[mb] = vM[mb].n
                W("te", vv)
        if e4 + 1 < n_exp:
            W("gp", mm, mm.n)
            emit_we2_loads(e4 + 1)
        m_prev_expert[0] = vM[0].n
        m_prev_expert[1] = vM[1].n
    barrier()

    fin_free = [None, None]
    out_store = [None, None]
    fin_loads = [0, 0]
    for ot in range(ONT):
        pp = ot % 2
        base = pp * 3 * D
        W("sy", vv, fin_free[pp])
        fsems = [dXq[(ot + j) % 4] for j in range(3)]
        I("sy", fsems[0], 16, lambda e, ot=ot, base=base: e.dma_start(
            out=V("FIN", a=base, n=D), in_=x1_d[ot * 128:(ot + 1) * 128, :]))
        I("sy", fsems[1], 16, lambda e, ot=ot, base=base: e.dma_start(
            out=V("FIN", a=base + D, n=D), in_=h_d[ot * 128:(ot + 1) * 128, :]))
        I("sy", fsems[2], 16, lambda e, ot=ot, base=base: e.dma_start(
            out=V("FIN", a=base + 2 * D, n=D), in_=m_d[ot * 128:(ot + 1) * 128, :]))
        for sm in fsems:
            W("ve", sm, sm.n)
        W("ve", dO[pp], out_store[pp])
        I("ve", vv, 1, lambda e, base=base: e.tensor_tensor(
            V("FIN", a=base, n=D), V("FIN", a=base, n=D),
            V("FIN", a=base + D, n=D), ALU.add))
        I("ve", vv, 1, lambda e, base=base, pp=pp: e.tensor_tensor(
            V("OUTT", a=pp * D, n=D), V("FIN", a=base, n=D),
            V("FIN", a=base + 2 * D, n=D), ALU.add))
        fin_free[pp] = vv.n
        W("sy", vv)
        I("sy", dO[pp], 16, lambda e, ot=ot, pp=pp: e.dma_start(
            out=out[ot * 128:(ot + 1) * 128, :], in_=V("OUTT", a=pp * D, n=D)))
        out_store[pp] = dO[pp].n
    if tap_outs:
        barrier()
        for t, (dst, src) in tap_outs.items():
            I("sy", dO[0], 16, lambda e, dst=dst, src=src: e.dma_start(
                out=dst[:, :], in_=src[:, :]))
    W("sy", dO[0], dO[0].n)
    W("sy", dO[1], dO[1].n)

    if max_phase < 99:
        cut_ph = min(k for k in cuts if k > max_phase) if any(
            k > max_phase for k in cuts) else None
        if cut_ph is not None:
            for ek in ENGS:
                plan[ek] = plan[ek][:cuts[cut_ph][ek]]
    with nc.Block() as block:
        @block.tensor
        def _(e):
            for f in plan["te"]:
                f(e)

        @block.scalar
        def _(e):
            for f in plan["sc"]:
                f(e)

        @block.vector
        def _(e):
            for f in plan["ve"]:
                f(e)

        @block.gpsimd
        def _(e):
            for f in plan["gp"]:
                f(e)

        @block.sync
        def _(e):
            for f in plan["sy"]:
                f(e)

    es.close()
    return nc



def _host_inputs(core, args):
    b, p = core // 2, core % 2
    g = np.arange(NG)
    tg = (g ^ 1) if p == 0 else g
    tp = (tg[:, None] * G + np.arange(G)[None, :]).reshape(-1)

    x0 = np.ascontiguousarray(np.asarray(args["L0_attn_x"], np.float32)[b][tp])

    ki = np.arange(128)
    qj = np.arange(512)
    dmask = np.empty((4, 128, 512), np.float32)
    for k in range(4):
        dmask[k] = np.where((k * 128 + ki)[:, None] <= qj[None, :], 0.0, NEG)
    pmask = np.empty((NG, 128, 512), np.float32)
    for s in range(NG):
        pmask[s] = 0.0 if tg[s ^ 1] < tg[s] else NEG

    inv = 1.0 / (10000.0 ** (np.arange(0, HD, 2) / HD))
    ang = tp[None, :].astype(np.float64) * inv[:, None]
    cost = np.concatenate([np.cos(ang), np.cos(ang)], 0).astype(np.float32)
    sint = np.concatenate([np.sin(ang), np.sin(ang)], 0).astype(np.float32)
    rot = np.zeros((128, 128), np.float32)
    rot[64:, :64] = -np.eye(64)
    rot[:64, 64:] = np.eye(64)

    def wt(w):
        return np.ascontiguousarray(
            np.asarray(w, np.float32).reshape(16, 128, 16, 128).transpose(2, 1, 0, 3))

    def bt(v):
        return np.ascontiguousarray(np.asarray(v, np.float32).reshape(16, 128).T)

    we1 = np.asarray(args["L0_ffn_we1"], np.float32)
    we1_t = np.ascontiguousarray(
        we1.reshape(NE, 16, 128, 16, 128).transpose(0, 3, 2, 1, 4))
    wg = np.asarray(args["L0_ffn_w_gate"], np.float32)
    wg_t = np.ascontiguousarray(
        wg.reshape(16, 128, 4).transpose(1, 0, 2).reshape(128, 64))
    be1 = np.asarray(args["L0_ffn_be1"], np.float32)
    be1_t = np.ascontiguousarray(
        be1.reshape(NE, 16, 128).transpose(2, 1, 0).reshape(128, 64))

    return {
        "x0p": x0,
        "dmask": dmask, "pmask": pmask, "cost": cost, "sint": sint,
        "rot_t": rot.astype(BF),
        "wq_t": wt(args["L0_ffn_wq"]), "wk_t": wt(args["L0_ffn_wk"]),
        "wv_t": wt(args["L0_ffn_wv"]),
        "wo_t": np.ascontiguousarray(np.asarray(args["L0_ffn_wo"], np.float32)),
        "bq_t": bt(args["L0_ffn_bq"]), "bk_t": bt(args["L0_ffn_bk"]),
        "bv_t": bt(args["L0_ffn_bv"]),
        "bo_b": np.asarray(args["L0_ffn_bo"], np.float32).reshape(1, D),
        "ln1w": np.asarray(args["L0_ffn_ln1_w"], np.float32).reshape(1, D),
        "ln1b": np.asarray(args["L0_ffn_ln1_b"], np.float32).reshape(1, D),
        "ln2w": np.asarray(args["L0_ffn_ln2_w"], np.float32).reshape(1, D),
        "ln2b": np.asarray(args["L0_ffn_ln2_b"], np.float32).reshape(1, D),
        "wg_t": wg_t, "be1_t": be1_t,
        "be2_t": np.ascontiguousarray(np.asarray(args["L0_ffn_be2"], np.float32)),
        "we1_t": we1_t,
        "we2_t": np.ascontiguousarray(np.asarray(args["L0_ffn_we2"], np.float32)),
        "ones_r": np.ones((1, 128), np.float32),
        "fconst": np.broadcast_to(np.array(
            [EPS, -M1 * SCALE, -M2 * SCALE, 0.0], np.float32), (128, 4)).copy(),
        "pmul_in": np.broadcast_to(
            np.array([1.0 if tg[s ^ 1] < tg[s] else 0.0 for s in range(NG)],
                     np.float32), (128, NG)).copy(),
        "ident_in": np.eye(128, dtype=BF),
    }, tp


def kernel(**inputs):
    nc = build_program()
    in_maps = []
    tps = []
    for core in range(8):
        m, tp = _host_inputs(core, inputs)
        in_maps.append(m)
        tps.append(tp)
    res = run_bass_kernel_spmd(nc, in_maps, core_ids=list(range(8)))
    full = np.empty((B, L, D), np.float32)
    own_storage = np.concatenate(
        [np.arange((2 * j + 1) * G, (2 * j + 2) * G) for j in range(4)])
    for core in range(8):
        b = core // 2
        full[b, tps[core][own_storage]] = res.results[core]["out"]
    return full
